# version 02
# baseline (speedup 1.0000x reference)
import numpy as np
import ml_dtypes
import concourse.bass as bass
import concourse.mybir as mybir
from concourse.bass_utils import run_bass_kernel_spmd

F32 = mybir.dt.float32
BF16 = mybir.dt.bfloat16
I16 = mybir.dt.int16
I32 = mybir.dt.int32
U8 = mybir.dt.uint8
AF = mybir.ActivationFunctionType
AOP = mybir.AluOpType
BF = ml_dtypes.bfloat16

NCORES = 8
Bb, Lq, D, HID = 16, 2048, 1024, 4096
T = Bb * Lq
TPC = T // NCORES
NTILE = TPC // 128
E, K, CAP = 64, 6, 384
EPC = E // NCORES
SPC = EPC * CAP
ECAP = E * CAP
NPAIR = TPC * E
NSEL = TPC * K
SF = 2048
ZPAD = 128
ZROWS = SPC + ZPAD
ZERO_ROW = SPC
TRASH_T = NCORES * ZROWS + 8
TAB_ROWS = NCORES * ZROWS + 128
FL_ROWS = NSEL + 128
FL_TRASH = NSEL + 8
GTRASH = TPC

RG = [list(range(NCORES))]



def _toeplitz_stack(ker):
    p = np.arange(128)[:, None]
    j = np.arange(64)[None, :]
    kk = j - p + 64
    mask = (kk >= 0) & (kk < 64)
    kkc = np.clip(kk, 0, 63)
    S = ker[:, kkc] * mask[None, :, :]
    T = S.reshape(16, 64, 128, 64).transpose(2, 0, 1, 3).reshape(128, 16 * 4096)
    return np.ascontiguousarray(T.astype(np.float32))


def _fm_bias(b):
    n = b.shape[0] // 128
    return np.ascontiguousarray(b.reshape(n, 128).T.astype(np.float32))


def _striped_i16(ids, n):
    a = np.zeros((128, n // 16), np.int16)
    seq = np.asarray(ids, np.int64)
    for g in range(8):
        a[g * 16:(g + 1) * 16, :] = seq.reshape(n // 16, 16).T
    return a


def _host_prepare(inputs):
    x = np.ascontiguousarray(np.asarray(inputs["L0_attn_x"], np.float32).reshape(T, D))
    ker = np.asarray(inputs["L0_attn_kernel"], np.float32)
    toep = _toeplitz_stack(ker)
    sw1T = np.ascontiguousarray(np.asarray(inputs["L0_ffn_sw1"], np.float32).T)
    sw2T = np.ascontiguousarray(np.asarray(inputs["L0_ffn_sw2"], np.float32).T)
    rT = np.ascontiguousarray(np.asarray(inputs["L0_ffn_router_weight"], np.float32).T)
    w1 = np.ascontiguousarray(np.asarray(inputs["L0_attn_mod_w1"], np.float32))
    w2 = np.ascontiguousarray(np.asarray(inputs["L0_attn_mod_w2"], np.float32))
    inw = np.ascontiguousarray(np.asarray(inputs["L0_attn_in_w"], np.float32))
    outw = np.ascontiguousarray(np.asarray(inputs["L0_attn_out_w"], np.float32))
    ew1 = np.asarray(inputs["L0_ffn_w1"], np.float32)
    ew2 = np.asarray(inputs["L0_ffn_w2"], np.float32)
    ebias = np.asarray(inputs["L0_ffn_expert_bias"], np.float32)

    inb = _fm_bias(np.asarray(inputs["L0_attn_in_b"], np.float32))
    b1 = _fm_bias(np.asarray(inputs["L0_attn_mod_b1"], np.float32))
    b2 = _fm_bias(np.asarray(inputs["L0_attn_mod_b2"], np.float32))
    outb_row = np.ascontiguousarray(np.asarray(inputs["L0_attn_out_b"], np.float32)[None, :])

    ident = np.eye(128, dtype=BF)
    rep16 = np.tile(np.eye(16, dtype=np.float32), (1, 8))
    identf = np.eye(128, dtype=np.float32)
    lt128 = np.triu(np.ones((128, 128), BF), 1)
    onesrow = np.ones((1, 128), BF)
    onesrowf = np.ones((1, 128), np.float32)
    lt128f = np.triu(np.ones((128, 128), np.float32), 1)
    onesf = np.ones((128, 128), np.float32)
    onescolb = np.ones((128, 1), BF)
    ones64 = np.ones((128, 64), np.float32)
    ee = np.arange(64)
    e384row = (384.0 * ee + 128.0 * (ee // 8)).astype(np.float32)[None, :]
    ebias_row = ebias[None, :].astype(np.float32)
    ebias_col = ebias[:, None].astype(np.float32)

    xg = x.reshape(256, 128, D)
    in_maps = []
    for c in range(NCORES):
        gids = (np.arange(NTILE) * 8 + c) * 128
        my_blocks = np.arange(NTILE) * 8 + c
        x_moe = np.ascontiguousarray(xg[my_blocks].reshape(TPC, D))
        gid = np.ascontiguousarray(
            (gids[None, :] + np.arange(128)[:, None]).astype(np.float32))
        lid = np.ascontiguousarray(
            (np.arange(NTILE)[None, :] * 128 + np.arange(128)[:, None]).astype(np.float32))
        moeg_idx = _striped_i16(
            (gids[:, None] + np.arange(128)[None, :]).reshape(-1), TPC)
        sel = np.zeros((256, NTILE), np.float32)
        sel[my_blocks, np.arange(NTILE)] = 1.0
        selmat = np.ascontiguousarray(sel.reshape(2, 128, NTILE))
        myslots = _striped_i16(np.arange(SPC) + c * ZROWS, SPC)
        m = {
            "x_mix": np.ascontiguousarray(x[c * TPC:(c + 1) * TPC]),
            "x_moe": x_moe,
            "inw": inw, "inb": inb, "w1": w1, "b1": b1, "w2": w2, "b2": b2,
            "outw": outw, "outb_row": outb_row,
            "toep": toep,
            "rT": rT, "ebias_row": ebias_row, "ebias_col": ebias_col,
            "ew1": np.ascontiguousarray(ew1[c * EPC:(c + 1) * EPC]),
            "ew2": np.ascontiguousarray(ew2[c * EPC:(c + 1) * EPC]),
            "sw1T": sw1T, "sw2T": sw2T,
            "ident": ident, "identf": identf, "lt128": lt128, "onesrow": onesrow,
            "onesrowf": onesrowf, "lt128f": lt128f, "onesf": onesf, "onescolb": onescolb,
            "ones64": ones64, "e384row": e384row,
            "gid": gid, "lid": lid, "moeg_idx": moeg_idx, "rep16": rep16,
            "selmat": selmat, "myslots": myslots,
        }
        in_maps.append(m)
    return in_maps


def _assemble(results):
    out = np.empty((256, 128, D), np.float32)
    for c in range(NCORES):
        blocks = np.arange(NTILE) * 8 + c
        out[blocks] = results[c]["out"].reshape(NTILE, 128, D)
    return out.reshape(Bb, Lq, D)



ENGS = ("tensor", "vector", "scalar", "sync", "gpsimd")


class Prog:
    def __init__(self, nc):
        self.nc = nc
        self.ops = {e: [] for e in ENGS}
        self.sems = {}
        self.cnt = {}

    def sem(self, name):
        h = self.nc.alloc_semaphore(name)
        self.sems[name] = h
        self.cnt[name] = 0
        return h

    def inc(self, name, by):
        self.cnt[name] += by
        return self.cnt[name]

    def on(self, eng, fn):
        self.ops[eng].append(fn)

    def wait(self, eng, sem_name, val):
        if val <= 0:
            return
        k = (eng, sem_name)
        lw = getattr(self, "last_wait", None)
        if lw is None:
            self.last_wait = lw = {}
        if lw.get(k, 0) >= val:
            return
        lw[k] = val
        s = self.sems[sem_name]
        self.on(eng, lambda e, s=s, v=val: e.wait_ge(s, v))

    def dma(self, eng, sem_name, out, in_, **kw):
        v = self.inc(sem_name, 16)
        s = self.sems[sem_name]
        self.on(eng, lambda e, o=out, i=in_, s=s, kw=kw: e.dma_start(out=o, in_=i, **kw).then_inc(s, 16))
        return v

    def dmaT(self, eng, sem_name, out, in_):
        v = self.inc(sem_name, 16)
        s = self.sems[sem_name]
        self.on(eng, lambda e, o=out, i=in_, s=s: e.dma_start(out=o, in_=i, transpose=True).then_inc(s, 16))
        return v

    def tick(self, eng, sem_name, fn):
        v = self.inc(sem_name, 1)
        s = self.sems[sem_name]
        def wrapped(e, fn=fn, s=s):
            ins = fn(e)
            ins.then_inc(s, 1)
        self.on(eng, wrapped)
        return v

    def swcnt(self, g, val):
        if getattr(self, "_swreg", None) is None:
            self._swreg = g.alloc_register("swcnt")
        g.reg_mov(self._swreg, val)
        return self._swreg

    def gatherT(self, out_handle, total_cols, src_ap, idx_ap, n, elem=1024, chunk=256):
        nch = elem // 128
        for k in range(n // chunk):
            self.inc("qG", 16)
            def call(g, k=k, out_handle=out_handle, total_cols=total_cols, src_ap=src_ap,
                     idx_ap=idx_ap, elem=elem, chunk=chunk, nch=nch):
                o = bass.AP(out_handle, k * nch * chunk,
                            [[total_cols, 128], [chunk, nch], [1, chunk]])
                g.dma_gather(out_ap=o, in_ap=src_ap,
                             idxs_ap=idx_ap[:, k * (chunk // 16):(k + 1) * (chunk // 16)],
                             num_idxs=chunk, num_idxs_reg=self.swcnt(g, chunk), elem_size=elem,
                             transpose=True).then_inc(self.sems["qG"], 16)
            self.on("gpsimd", call)
        return self.cnt["qG"]

    def mm(self, out, lhsT, rhs, start, stop, tick=False):
        if tick:
            return self.tick("tensor", "tE", lambda t, o=out, l=lhsT, r=rhs, a=start, z=stop:
                             t.matmul(o, l, r, start=a, stop=z, skip_group_check=True))
        self.on("tensor", lambda t, o=out, l=lhsT, r=rhs, a=start, z=stop:
                t.matmul(o, l, r, start=a, stop=z, skip_group_check=True))
        return None

    def run(self, eng, fn):
        self.on(eng, fn)

    def barrier(self, producers):
        for e in ENGS:
            for sn, v in producers:
                self.wait(e, sn, v)

    def emit(self, block):
        for ename in ENGS:
            getattr(block, ename)(self._mk_runner(ename))

    def _mk_runner(self, ename):
        ops = self.ops[ename]
        def runner(eng):
            for fn in ops:
                fn(eng)
        return runner


def _finalize_passes(nc):
    from concourse.library_config import all_libraries, standard
    m = {}
    for lib in all_libraries:
        for it in lib.instructions:
            m[it] = m.get(it, 0) | (1 << lib.index)
    import bass_rust as _br
    _br.insert_library_loads(nc, m, len(all_libraries), standard.index)
    _br.insert_act_table_loads(nc)
    mybir.codegen_inst_isa_subclasses(nc)



def build(debug=()):
    nc = bass.Bass("TRN2", target_bir_lowering=False, detect_race_conditions=False, enable_asserts=False)
    dbg = set(debug)

    di = {}
    def inp(name, shape, dtype=F32):
        di[name] = nc.dram_tensor(name, shape, dtype, kind="ExternalInput")
        return di[name]

    x_mix = inp("x_mix", [TPC, D]); x_moe = inp("x_moe", [TPC, D])
    inw = inp("inw", [D, D]); inb = inp("inb", [128, 8])
    w1 = inp("w1", [D, HID]); b1 = inp("b1", [128, 32])
    w2 = inp("w2", [HID, D]); b2 = inp("b2", [128, 8])
    outw = inp("outw", [D, D]); outb_row = inp("outb_row", [1, D])
    toep = inp("toep", [128, 16 * 4096])
    rT = inp("rT", [D, E]); ebias_row = inp("ebias_row", [1, E]); ebias_col = inp("ebias_col", [E, 1])
    ew1 = inp("ew1", [EPC, D, 512]); ew2 = inp("ew2", [EPC, 512, D])
    sw1T = inp("sw1T", [D, SF]); sw2T = inp("sw2T", [SF, D])
    ident = inp("ident", [128, 128], BF16); identf = inp("identf", [128, 128], F32)
    lt128 = inp("lt128", [128, 128], BF16); onesrow = inp("onesrow", [1, 128], BF16)
    onesrowf = inp("onesrowf", [1, 128], F32); lt128f = inp("lt128f", [128, 128], F32)
    onesf = inp("onesf", [128, 128], F32); onescolb = inp("onescolb", [128, 1], BF16)
    ones64 = inp("ones64", [128, 64], F32); e384row = inp("e384row", [1, E])
    gid = inp("gid", [128, NTILE]); lid = inp("lid", [128, NTILE])
    moeg_idx = inp("moeg_idx", [128, TPC // 16], I16)
    selmat = inp("selmat", [2, 128, NTILE]); myslots = inp("myslots", [128, SPC // 16], I16)
    rep16 = inp("rep16", [16, 128], F32)

    out_t = nc.dram_tensor("out", [TPC, D], F32, kind="ExternalOutput")
    dbg_t = {}
    def dbgout(name, shape, dtype=F32):
        if name in dbg:
            dbg_t[name] = nc.dram_tensor("dbg_" + name, shape, dtype, kind="ExternalOutput")
            return dbg_t[name]
        return None

    xn_shard = nc.dram_tensor("xn_shard", [TPC, D], BF16)
    xn_all = nc.dram_tensor("xn_all", [T, D], BF16)
    MTW = Lq + 64
    modT_hbm = nc.dram_tensor("modT_hbm", [D, 2 * MTW], BF16)
    y_hbm = nc.dram_tensor("y_hbm", [TPC, D], BF16)
    s_shard = nc.dram_tensor("s_shard", [TPC, D], BF16)
    s_all = nc.dram_tensor("s_all", [T, D], BF16)
    cnt_shard = nc.dram_tensor("cnt_shard", [NTILE * E], F32)
    cnt_all = nc.dram_tensor("cnt_all", [NCORES * NTILE * E], F32)
    tabTW = nc.dram_tensor("tabTW", [TAB_ROWS, 64], F32)
    list6h = nc.dram_tensor("list6h", [TAB_ROWS, 128], I16)
    boun_f = nc.dram_tensor("boun_f", [4096], F32)
    z_shard = nc.dram_tensor("z_shard", [ZROWS, D], BF16)
    z_all = nc.dram_tensor("z_all", [NCORES * ZROWS, D], BF16)

    p = Prog(nc)
    for s in ("qS", "qG", "qA", "tE", "vE", "sE", "gE", "cc"):
        p.sem(s)

    A = nc.alloc_sbuf_tensor
    sb = {}
    def sbuf(name, shape, dtype):
        sb[name] = A(name, shape, dtype)
        return sb[name]

    ident_sb = sbuf("ident_sb", [128, 128], BF16)
    identf_sb = sbuf("identf_sb", [128, 128], F32)
    lt128_sb = sbuf("lt128_sb", [128, 128], BF16)
    onesrow_sb = sbuf("onesrow_sb", [1, 128], BF16)
    onesrowf_sb = sbuf("onesrowf_sb", [1, 128], F32)
    lt128f_sb = sbuf("lt128f_sb", [128, 128], F32)
    onesf_sb = sbuf("onesf_sb", [128, 128], F32)
    onescolb_sb = sbuf("onescolb_sb", [128, 1], BF16)
    selmat_sb = sbuf("selmat_sb", [128, 2 * NTILE], F32)
    myslots_sb = sbuf("myslots_sb", [128, SPC // 16], I16)
    ones64_sb = sbuf("ones64_sb", [128, 64], F32)
    e384_sb = sbuf("e384_sb", [1, E], F32)
    ebrow_sb = sbuf("ebrow_sb", [1, E], F32)
    ebcol_sb = sbuf("ebcol_sb", [E, 1], F32)
    gid_sb = sbuf("gid_sb", [128, NTILE], F32)
    lid_sb = sbuf("lid_sb", [128, NTILE], F32)
    inb_sb = sbuf("inb_sb", [128, 8], F32)
    b1_sb = sbuf("b1_sb", [128, 32], F32)
    b2_sb = sbuf("b2_sb", [128, 8], F32)
    outbrow_sb = sbuf("outbrow_sb", [1, D], BF16)
    moegi_sb = sbuf("moegi_sb", [128, TPC // 16], I16)
    zero_sb = sbuf("zero_sb", [128, 2048], F32)
    stat_sb = sbuf("stat_sb", [128, 16], F32)
    rep16_sb = sbuf("rep16_sb", [16, 128], F32)
    cntj_sb = sbuf("cntj_sb", [1, 8], F32)
    cntj_i = sbuf("cntj_i", [1, 8], I32)

    for (t, s_) in ((ident, ident_sb), (identf, identf_sb), (lt128, lt128_sb),
                    (onesrow, onesrow_sb), (ones64, ones64_sb), (e384row, e384_sb),
                    (ebias_row, ebrow_sb), (ebias_col, ebcol_sb), (gid, gid_sb),
                    (lid, lid_sb), (inb, inb_sb), (b1, b1_sb), (b2, b2_sb),
                    (moeg_idx, moegi_sb), (onesrowf, onesrowf_sb), (lt128f, lt128f_sb),
                    (onesf, onesf_sb), (onescolb, onescolb_sb), (myslots, myslots_sb),
                    (rep16, rep16_sb)):
        p.dma("sync", "qS", s_.ap()[:, :], t.ap()[:, :])
    p.dma("sync", "qS", selmat_sb.ap()[:, 0:NTILE], selmat.ap()[0, :, :])
    p.dma("sync", "qS", selmat_sb.ap()[:, NTILE:2 * NTILE], selmat.ap()[1, :, :])
    p.dma("gpsimd", "qG", outbrow_sb.ap()[:, :], outb_row.ap()[:, :])
    CONST_Q = p.cnt["qS"]
    CONST_G = p.cnt["qG"]
    V_ZERO = p.tick("vector", "vE", lambda v: v.memset(zero_sb[:, :], 0.0))

    pb = [nc.alloc_psum_tensor(f"pb{i}", [128, 512], F32) for i in range(8)]

    AT_BASE = 47424
    AT_SIZE = 229376 - AT_BASE
    _at_names = {}
    def at(name, off_bytes, shape, dtype):
        assert off_bytes % 64 == 0
        sz = mybir.dt.size(dtype)
        for d in shape[1:]:
            pass
        return nc.alloc_sbuf_tensor_at(name, shape, dtype, offset=AT_BASE + off_bytes)

    xrow_t = at("xrow", 147456, [128, 1024], F32)
    xnrow_t = at("xnrow", 151552, [128, 1024], BF16)
    sq_t = at("sq", 157696, [128, 1024], F32)
    xrow = xrow_t.ap()[:, :]
    xnrow = xnrow_t.ap()[:, :]
    sq_sb = sq_t.ap()[:, :]
    for tt in range(NTILE):
        qv = p.dma("sync", "qS", xrow, x_mix.ap()[tt * 128:(tt + 1) * 128, :])
        p.wait("scalar", "qS", qv)
        sv = p.tick("scalar", "sE", lambda s: s.activation(
            out=sq_sb[:, :], in_=xrow, func=AF.Square, bias=0.0, scale=1.0))
        p.wait("vector", "qS", qv)
        p.tick("vector", "vE", lambda v: v.reduce_sum(stat_sb[:, 1:2], xrow, axis=mybir.AxisListType.X))
        p.wait("vector", "sE", sv)
        vv = p.tick("vector", "vE", lambda v: v.reduce_sum(stat_sb[:, 0:1], sq_sb[:, :], axis=mybir.AxisListType.X))
        p.wait("vector", "vE", vv)
        p.tick("vector", "vE", lambda v: v.tensor_scalar_mul(stat_sb[:, 2:3], stat_sb[:, 1:2], 1.0 / D))
        vv = p.tick("vector", "vE", lambda v: v.tensor_scalar_mul(stat_sb[:, 3:4], stat_sb[:, 0:1], 1.0 / D))
        p.wait("vector", "vE", vv)
        vv = p.tick("vector", "vE", lambda v: v.tensor_tensor(stat_sb[:, 4:5], stat_sb[:, 2:3], stat_sb[:, 2:3], AOP.mult))
        p.wait("vector", "vE", vv)
        vv = p.tick("vector", "vE", lambda v: v.tensor_tensor(stat_sb[:, 5:6], stat_sb[:, 3:4], stat_sb[:, 4:5], AOP.subtract))
        p.wait("vector", "vE", vv)
        vv = p.tick("vector", "vE", lambda v: v.tensor_scalar_add(stat_sb[:, 5:6], stat_sb[:, 5:6], 1e-5))
        p.wait("scalar", "vE", vv)
        sv = p.tick("scalar", "sE", lambda s: s.activation(
            out=stat_sb[:, 6:7], in_=stat_sb[:, 5:6], func=AF.Sqrt, bias=0.0, scale=1.0))
        p.wait("vector", "sE", sv)
        vv = p.tick("vector", "vE", lambda v: v.reciprocal(stat_sb[:, 7:8], stat_sb[:, 6:7]))
        p.wait("vector", "vE", vv)
        vv = p.tick("vector", "vE", lambda v: v.tensor_scalar(
            xnrow, xrow, stat_sb[:, 2:3], stat_sb[:, 7:8], AOP.subtract, AOP.mult))
        p.wait("sync", "vE", vv)
        p.dma("sync", "qS", xn_shard.ap()[tt * 128:(tt + 1) * 128, :], xnrow)
    P0_Q = p.cnt["qS"]

    p.wait("gpsimd", "qS", P0_Q)
    ccv = p.inc("cc", 1)
    p.run("gpsimd", lambda g: g.collective_compute(
        "AllGather", AOP.bypass, replica_groups=RG,
        ins=[xn_shard.ap().opt()], outs=[xn_all.ap().opt()]).then_inc(p.sems["cc"], 1))
    AG1 = ccv

    if "xn" in dbg:
        tap = dbgout("xn", [TPC, D], BF16)
        p.wait("scalar", "qS", P0_Q)
        for tt in range(NTILE):
            p.dma("scalar", "qA", tap.ap()[tt * 128:(tt + 1) * 128, :],
                  xn_shard.ap()[tt * 128:(tt + 1) * 128, :])
        p.wait("scalar", "qA", p.cnt["qA"])

    w1_sb = at("w1_sb", 0, [128, 8 * 4096], BF16).ap()[:, :]
    w2_sb = at("w2_sb", 65536, [128, 32 * 1024], BF16).ap()[:, :]
    inw_sb = at("inw_sb", 131072, [128, 8 * 1024], BF16).ap()[:, :]
    xnT = [at(f"xnT{i}", 147456 + i * 4096, [128, 8 * 256], BF16).ap()[:, :] for i in range(2)]
    hch = [at(f"hch{i}", 155648 + i * 512, [128, 256], BF16).ap()[:, :] for i in range(2)]
    xp_sb = at("xp_sb", 157696, [128, 8 * 256], F32).ap()[:, :]
    modu_sb = at("modu_sb", 165888, [128, 8 * 256], BF16).ap()[:, :]

    for dc in range(8):
        p.dma("gpsimd", "qG", w1_sb[:, dc * 4096:(dc + 1) * 4096], w1.ap()[dc * 128:(dc + 1) * 128, :])
    for hc in range(32):
        p.dma("gpsimd", "qG", w2_sb[:, hc * 1024:(hc + 1) * 1024], w2.ap()[hc * 128:(hc + 1) * 128, :])
    for dc in range(8):
        p.dma("gpsimd", "qG", inw_sb[:, dc * 1024:(dc + 1) * 1024], inw.ap()[dc * 128:(dc + 1) * 128, :])
    P1_W = p.cnt["qG"]

    NT1 = TPC // 256
    xnT_q = [0, 0]
    xnb_done = {}
    modu_q = 0
    gelu_tick = {}
    modmm_tick = {}
    xp_tick = {}

    def load_xnT(tile):
        buf = tile % 2
        p.wait("sync", "qS", P0_Q)
        if tile >= 2:
            p.wait("sync", "tE", xnb_done[tile - 2])
        for dc in range(8):
            p.dmaT("sync", "qS",
                   xnT[buf][:, dc * 256:(dc + 1) * 256],
                   xn_shard.ap()[tile * 256:(tile + 1) * 256, dc * 128:(dc + 1) * 128])
        xnT_q[buf] = p.cnt["qS"]

    load_xnT(0)
    load_xnT(1)
    p.wait("tensor", "qG", P1_W)

    modu_evac = 0
    for tile in range(NT1):
        buf = tile % 2
        p.wait("tensor", "qS", xnT_q[buf])
        xnb = xnT[buf]
        if tile:
            p.wait("scalar", "vE", modu_evac)
        for nc_ in range(8):
            bank = pb[4 + nc_ % 2].ap()[:, 0:256]
            prev = (tile, nc_ - 2) if nc_ >= 2 else (tile - 1, 6 + nc_)
            if prev in xp_tick:
                p.wait("tensor", "sE", xp_tick[prev][1])
            for dc in range(8):
                st_ = inw_sb[:, dc * 1024 + nc_ * 128: dc * 1024 + (nc_ + 1) * 128]
                tv = p.mm(bank, st_, xnb[:, dc * 256:(dc + 1) * 256],
                          start=(dc == 0), stop=(dc == 7), tick=(dc == 7))
            p.wait("scalar", "tE", tv)
            sv = p.tick("scalar", "sE", lambda s, o=xp_sb[:, nc_ * 256:(nc_ + 1) * 256],
                        i=bank, b_=inb_sb[:, nc_:nc_ + 1]: s.activation(
                            out=o, in_=i, func=AF.Identity, bias=b_, scale=1.0))
            xp_tick[(tile, nc_)] = (tv, sv)
        for hc in range(32):
            hb = pb[6 + hc % 2].ap()[:, 0:256]
            hbuf = hch[hc % 2]
            if (tile, hc - 2) in gelu_tick:
                p.wait("tensor", "sE", gelu_tick[(tile, hc - 2)])
            elif tile and hc < 2:
                p.wait("tensor", "sE", gelu_tick[(tile - 1, 30 + hc)])
            for dc in range(8):
                st_ = w1_sb[:, dc * 4096 + hc * 128: dc * 4096 + (hc + 1) * 128]
                tv = p.mm(hb, st_, xnb[:, dc * 256:(dc + 1) * 256],
                          start=(dc == 0), stop=(dc == 7), tick=(dc == 7))
            if hc == 31:
                xnb_done[tile] = tv
            p.wait("scalar", "tE", tv)
            if hc >= 2:
                p.wait("scalar", "tE", modmm_tick[(tile, hc - 2)])
            elif tile and hc < 2:
                p.wait("scalar", "tE", modmm_tick[(tile - 1, 30 + hc)])
            gv = p.tick("scalar", "sE", lambda s, o=hbuf, i=hb, b_=b1_sb[:, hc:hc + 1]:
                        s.activation(out=o, in_=i, func=AF.Gelu, bias=b_, scale=1.0))
            gelu_tick[(tile, hc)] = gv
            p.wait("tensor", "sE", gv)
            if hc == 0 and tile:
                p.wait("tensor", "vE", modu_evac)
            for dc in range(8):
                bank = pb[dc // 2].ap()[:, (dc % 2) * 256:(dc % 2) * 256 + 256]
                st_ = w2_sb[:, hc * 1024 + dc * 128: hc * 1024 + (dc + 1) * 128]
                tv = p.mm(bank, st_, hbuf, start=(hc == 0 and dc % 2 == 0),
                          stop=(hc == 31), tick=(dc == 7))
            modmm_tick[(tile, hc)] = tv
        p.wait("vector", "tE", modmm_tick[(tile, 31)])
        p.wait("vector", "sE", xp_tick[(tile, 7)][1])
        if tile:
            p.wait("vector", "qS", modu_q)
        for dc in range(8):
            bank = pb[dc // 2].ap()[:, (dc % 2) * 256:(dc % 2) * 256 + 256]
            vv = p.tick("vector", "vE", lambda v, o=modu_sb[:, dc * 256:(dc + 1) * 256],
                        i=bank, b_=b2_sb[:, dc:dc + 1], x2=xp_sb[:, dc * 256:(dc + 1) * 256]:
                        v.scalar_tensor_tensor(o, i, b_, x2, AOP.add, AOP.mult))
        modu_evac = vv
        p.wait("sync", "vE", modu_evac)
        b_of = tile // 8
        col0 = b_of * MTW + 64 + (tile % 8) * 256
        W2H = 2 * MTW
        dst = bass.AP(modT_hbm, col0, [[W2H, 128], [128 * W2H, 8], [1, 256]])
        modu_q = p.dma("sync", "qS", dst, modu_sb[:, :].rearrange("p (dc t) -> p dc t", dc=8))
        if tile % 8 == 7:
            hdst = bass.AP(modT_hbm, b_of * MTW, [[W2H, 128], [128 * W2H, 8], [1, 64]])
            modu_q = p.dma("sync", "qS", hdst,
                           modu_sb[:, :].rearrange("p (dc t) -> p dc t", dc=8)[:, :, 192:256])
        if tile + 2 < NT1:
            load_xnT(tile + 2)
    P1_MOD_Q = p.cnt["qS"]

    if "modT" in dbg:
        tap = dbgout("modT", [D, TPC], BF16)
        p.wait("scalar", "qS", P1_MOD_Q)
        for dc in range(8):
            for b_ in range(2):
                p.dma("scalar", "qA", tap.ap()[dc * 128:(dc + 1) * 128, b_ * Lq:(b_ + 1) * Lq],
                      modT_hbm.ap()[dc * 128:(dc + 1) * 128, b_ * MTW + 64: b_ * MTW + 64 + Lq])

    P1_T_END = p.cnt["tE"]

    MT_t = at("MT", 0, [128, 32 * 1024], BF16)
    ybuf_t = at("ybuf", 65536, [128, 32 * 512], BF16)
    toep_t = [at(f"toep{i}", 98304 + i * 8192, [128, 64 * 64], BF16) for i in range(2)]

    def load_toep(g):
        buf = g % 2
        return p.dma("gpsimd", "qG", toep_t[buf].ap()[:, :],
                     toep.ap()[:, (g % 16) * 4096:((g % 16) + 1) * 4096])

    CGRP = 16
    p.wait("gpsimd", "tE", P1_T_END)
    toep_q = {0: load_toep(0)}
    toep_q[1] = load_toep(1)

    y_store_q = 0
    mt_free_tick = 0
    ybuf_free = {}
    conv_mm_tick = {}
    for b_ in range(2):
        p.wait("sync", "qS", P1_MOD_Q)
        p.wait("sync", "tE", P1_T_END)
        if b_ == 1:
            p.wait("sync", "tE", mt_free_tick)
            p.wait("sync", "qS", y_store_q)
        for blk in range(32):
            p.dmaT("sync", "qS", MT_t.ap()[:, blk * 1024:(blk + 1) * 1024],
                   modT_hbm.ap()[:, b_ * MTW + blk * 64: b_ * MTW + blk * 64 + 128])
        mt_q = p.cnt["qS"]
        if b_ == 0 and "MT0" in dbg:
            tap = dbgout("MT0", [128, 32 * 1024], BF16)
            p.wait("scalar", "qS", mt_q)
            p.dma("scalar", "qA", tap.ap()[:, :], MT_t.ap()[:, :])
            p.wait("tensor", "qA", p.cnt["qA"])
        if b_ == 0 and "toep0" in dbg:
            tap2 = dbgout("toep0", [128, 4096], BF16)
            p.wait("scalar", "qG", toep_q[0])
            p.dma("scalar", "qA", tap2.ap()[:, :], bass.AP(toep_t[0], 0, [[4096, 128], [1, 4096]]))
            p.wait("tensor", "qA", p.cnt["qA"])
        p.wait("tensor", "qS", mt_q)
        for g in range(CGRP):
            gg = b_ * CGRP + g
            p.wait("tensor", "qG", toep_q[gg])
            for c_l in range(64):
                c = g * 64 + c_l
                bk = c_l // 16
                prevkey = (b_, g - 1, bk) if g else (b_ - 1, CGRP - 1, bk)
                if c_l % 16 == 0 and prevkey in ybuf_free:
                    p.wait("tensor", "vE", ybuf_free[prevkey])
                out = pb[bk].ap()[0:64, (c_l % 16) * 32:(c_l % 16) * 32 + 32]
                rhs = bass.AP(MT_t, c, [[32768, 128], [1024, 32]])
                tv = p.mm(out, toep_t[gg % 2].ap()[:, c_l * 64:(c_l + 1) * 64], rhs,
                          start=(c_l % 16 == 0), stop=True, tick=(c_l % 16 == 15))
                if c_l % 16 == 15:
                    conv_mm_tick[(b_, g, bk)] = tv
            if gg + 2 < 2 * CGRP:
                p.wait("gpsimd", "tE", conv_mm_tick[(b_, g, 3)])
                toep_q[gg + 2] = load_toep(gg + 2)
            for bk in range(4):
                cbase = g * 64 + bk * 16
                half = cbase // 512
                c2 = cbase % 512
                p.wait("vector", "tE", conv_mm_tick[(b_, g, bk)])
                src = bass.AP(pb[bk], 0, [[512, 64], [32, 16], [1, 32]])
                dstv = bass.AP(ybuf_t, half * 64 * 16384 + c2, [[16384, 64], [1, 16], [512, 32]])
                vv = p.tick("vector", "vE", lambda v, o=dstv, i=src: v.tensor_copy(o, i))
                ybuf_free[(b_, g, bk)] = vv
        mt_free_tick = conv_mm_tick[(b_, CGRP - 1, 3)]
        for g in range(CGRP):
            for bk in range(4):
                p.wait("sync", "vE", ybuf_free[(b_, g, bk)])
        for half in range(2):
            dst = bass.AP(y_hbm, b_ * Lq * D + half * 512, [[D, 64], [64 * D, 32], [1, 512]])
            src = bass.AP(ybuf_t, half * 64 * 16384, [[16384, 64], [512, 32], [1, 512]])
            y_store_q = p.dma("sync", "qS", dst, src)
    PCONV_Q = p.cnt["qS"]

    if "y" in dbg:
        tap = dbgout("y", [TPC, D], BF16)
        p.wait("scalar", "qS", PCONV_Q)
        p.dma("scalar", "qA", tap.ap()[:, :], y_hbm.ap()[:, :])

    yT_sb = at("yT_sb", 0, [128, 8 * TPC], BF16).ap()[:, :]
    outw_sb = at("outw_sb", 98304, [128, 8 * 1024], BF16).ap()[:, :]
    srow_sb = at("srow_sb", 81920, [128, 1024], BF16).ap()[:, :]
    p.wait("gpsimd", "qS", PCONV_Q)
    for cc in range(8):
        p.dma("gpsimd", "qG", outw_sb[:, cc * 1024:(cc + 1) * 1024], outw.ap()[cc * 128:(cc + 1) * 128, :])
    OUTW_Q = p.cnt["qG"]
    p.wait("sync", "tE", mt_free_tick)
    p.wait("sync", "qS", PCONV_Q)
    for cc in range(8):
        p.dmaT("sync", "qS", yT_sb[:, cc * TPC:(cc + 1) * TPC],
               y_hbm.ap()[:, cc * 128:(cc + 1) * 128])
    YT_Q = p.cnt["qS"]
    p.wait("tensor", "qS", YT_Q)
    p.wait("tensor", "qG", OUTW_Q)
    p.wait("tensor", "qG", CONST_G)
    s_evac_tick = {}
    s_q = {}
    for tt in range(NTILE):
        for hf in range(2):
            bank = pb[4 + hf]
            if (tt - 1, hf) in s_evac_tick:
                p.wait("tensor", "sE", s_evac_tick[(tt - 1, hf)])
            for cc in range(8):
                p.mm(bank.ap()[:, :],
                     yT_sb[:, cc * TPC + tt * 128: cc * TPC + (tt + 1) * 128],
                     outw_sb[:, cc * 1024 + hf * 512: cc * 1024 + (hf + 1) * 512],
                     start=(cc == 0), stop=False)
            tv = p.mm(bank.ap()[:, :], onesrow_sb.ap()[:, :],
                      outbrow_sb.ap()[:, hf * 512:(hf + 1) * 512],
                      start=False, stop=True, tick=True)
            p.wait("scalar", "tE", tv)
            if (tt - 1, hf) in s_q:
                p.wait("scalar", "qS", s_q[(tt - 1, hf)])
            sv = p.tick("scalar", "sE", lambda s, o=srow_sb[:, hf * 512:(hf + 1) * 512],
                        i=bank.ap()[:, :]: s.activation(out=o, in_=i, func=AF.Identity,
                                                        bias=0.0, scale=1.0))
            s_evac_tick[(tt, hf)] = sv
            p.wait("sync", "sE", sv)
            s_q[(tt, hf)] = p.dma("sync", "qS",
                                  s_shard.ap()[tt * 128:(tt + 1) * 128, hf * 512:(hf + 1) * 512],
                                  srow_sb[:, hf * 512:(hf + 1) * 512])
    POUT_Q = p.cnt["qS"]
    p.wait("gpsimd", "qS", POUT_Q)
    AG5 = p.inc("cc", 1)
    p.run("gpsimd", lambda g: g.collective_compute(
        "AllGather", AOP.bypass, replica_groups=RG,
        ins=[s_shard.ap().opt()], outs=[s_all.ap().opt()]).then_inc(p.sems["cc"], 1))

    if "s" in dbg:
        tap = dbgout("s", [TPC, D], BF16)
        p.wait("scalar", "qS", POUT_Q)
        p.dma("scalar", "qA", tap.ap()[:, :], s_shard.ap()[:, :])

    POUT_T = p.cnt["tE"]
    xnm = at("xnm", 0, [128, 8 * TPC], BF16)
    sT_sb = at("sT", 65536, [64, TPC], F32).ap()[:, :]
    sbT_sb = at("sbT", 81920, [64, TPC], F32).ap()[:, :]
    Itmf = at("Itmf", 161408, [128, NTILE * E], F32).ap()[:, :]
    Itmb = at("Itmb", 169600, [128, NTILE * E], BF16).ap()[:, :]
    Wtm = at("Wtm", 173696, [128, NTILE * E], F32).ap()[:, :]
    list6 = at("list6", 118784, [128, NTILE * K], I16)
    wq6 = at("wq6", 119808, [128, NTILE * K], I16)
    ls_idx = at("ls_idx", 124928, [128, NTILE * E], I16).ap()[:, :]
    ls_slot = at("ls_slot", 129024, [128, NTILE * E], I16).ap()[:, :]
    ls_wq = at("ls_wq", 133120, [128, NTILE * E], I16).ap()[:, :]
    fidx = at("fidx", 137216, [128, NTILE * K], I16).ap()[:, :]
    fval = at("fval", 138240, [128, NTILE * K * 2], I16).ap()[:, :]
    segpos_a = at("segpos", 140288, [128, NTILE * K], F32).ap()[:, :]
    sc2 = at("sc2", 143360, [128, 64 * 10], F32).ap()[:, :]
    o8_sb = at("o8", 145920, [128, 8], F32).ap()[:, :]
    ebb_sb = at("ebb", 146176, [128, 64], F32).ap()[:, :]
    e384b_sb = at("e384b", 146432, [128, 64], F32).ap()[:, :]
    cntv_sb = at("cntv", 146688, [64, 32], F32).ap()[:, :]
    goff_sb = at("goff", 146944, [128, 2 * 64], F32).ap()[:, :]
    myoffR = at("myoffR", 147456, [1, NTILE * E], F32).ap()[:, :]
    cumrow = at("cumrow", 155648, [1, 8], F32).ap()[:, :]
    rowtmp = at("rowtmp", 155712, [32, 64], F32).ap()[:, :]
    lidp1 = at("lidp1", 156160, [128, NTILE], I16).ap()[:, :]
    cntj_f = at("cntj_f", 156288, [1, 8], F32).ap()[:, :]
    cntj_ii = at("cntj_ii", 156352, [1, 8], I32).ap()[:, :]
    twc_sb = at("twc", 157696, [128, 2 * (TAB_ROWS // 128)], F32).ap()[:, :]
    myTW = at("myTW", 161408, [128, 24 * 64], F32).ap()[:, :]
    mytok_i = at("mytok_i", 159552, [128, 24], I16).ap()[:, :]
    xg_idx = at("xg_idx", 159616, [128, SPC // 16], I16).ap()[:, :]

    p.wait("gpsimd", "cc", AG1)
    p.wait("gpsimd", "tE", POUT_T)
    p.wait("gpsimd", "qS", POUT_Q)
    p.wait("vector", "tE", POUT_T)
    p.wait("vector", "qS", POUT_Q)
    p.wait("scalar", "qS", POUT_Q)
    XG1 = p.gatherT(xnm, 8 * TPC, xn_all.ap(), moegi_sb.ap(), TPC, elem=D, chunk=256)

    rT_sb = at("rT_sb", 159872, [128, 8 * 64], BF16).ap()[:, :]
    for dc in range(8):
        p.dma("gpsimd", "qG", rT_sb[:, dc * 64:(dc + 1) * 64], rT.ap()[dc * 128:(dc + 1) * 128, :])
    RT_Q = p.cnt["qG"]
    p.wait("tensor", "qG", RT_Q)
    p.wait("tensor", "qG", XG1)
    p.wait("tensor", "sE", p.cnt["sE"])
    for sl in range(16):
        for dc in range(8):
            tv = p.mm(pb[0].ap()[0:64, 0:256], rT_sb[:, dc * 64:(dc + 1) * 64],
                      bass.AP(xnm, sl * 2048 + dc * 256, [[8 * TPC, 128], [1, 256]]),
                      start=(dc == 0), stop=(dc == 7), tick=(dc == 7))
        p.wait("scalar", "tE", tv)
        sv = p.tick("scalar", "sE", lambda s, o=sT_sb[:, sl * 256:(sl + 1) * 256],
                    i=pb[0].ap()[0:64, 0:256]: s.activation(out=o, in_=i, func=AF.Sigmoid,
                                                            bias=0.0, scale=1.0))
        p.wait("tensor", "sE", sv)
    p.wait("vector", "sE", p.cnt["sE"])
    vv = p.tick("vector", "vE", lambda v: v.tensor_scalar_add(sbT_sb, sT_sb, ebcol_sb.ap()[:, :]))
    SBT_V = vv

    p.wait("tensor", "vE", SBT_V)
    tv = p.mm(pb[1].ap()[:, 0:64], onesrowf_sb.ap()[:, :], ebrow_sb.ap()[:, :],
              start=True, stop=True, tick=True)
    p.wait("vector", "tE", tv)
    p.tick("vector", "vE", lambda v: v.tensor_copy(ebb_sb, pb[1].ap()[:, 0:64]))
    tv = p.mm(pb[1].ap()[:, 64:128], onesrowf_sb.ap()[:, :], e384_sb.ap()[:, :],
              start=False, stop=True, tick=True)
    p.wait("vector", "tE", tv)
    vv = p.tick("vector", "vE", lambda v: v.tensor_copy(e384b_sb, pb[1].ap()[:, 64:128]))
    vv = p.tick("vector", "vE", lambda v: v.tensor_scalar(lidp1, lid_sb.ap()[:, :],
                                                          1.0, None, AOP.add))
    ONETIME_V = vv

    sc = lambda i: sc2[:, i * 64:(i + 1) * 64]
    cnt_mm = {}
    for tt in range(NTILE):
        p.wait("tensor", "vE", SBT_V)
        tv = p.tick("tensor", "tE", lambda t, i=sbT_sb[:, tt * 128:(tt + 1) * 128]:
                    t.transpose(pb[2].ap()[:, 0:64], i, identf_sb.ap()[0:64, 0:64]))
        p.wait("vector", "tE", tv)
        vv = p.tick("vector", "vE", lambda v: v.tensor_copy(sc(0), pb[2].ap()[:, 0:64]))
        p.wait("vector", "vE", vv)
        vv = p.tick("vector", "vE", lambda v: v.max(out=o8_sb, in_=sc(0)))
        p.wait("vector", "vE", vv)
        vv = p.tick("vector", "vE", lambda v: v.tensor_single_scalar(sc(1), sc(0), o8_sb[:, 5:6], AOP.is_ge))
        p.wait("vector", "vE", vv)
        p.tick("vector", "vE", lambda v, o=Itmf[:, tt * 64:(tt + 1) * 64]: v.tensor_copy(o, sc(1)))
        p.tick("vector", "vE", lambda v, o=Itmb[:, tt * 64:(tt + 1) * 64]: v.tensor_copy(o, sc(1)))
        p.wait("vector", "vE", ONETIME_V)
        p.tick("vector", "vE", lambda v: v.tensor_sub(sc(2), sc(0), ebb_sb))
        vv = p.tick("vector", "vE", lambda v: v.tensor_mul(sc(3), sc(1), sc(2)))
        p.wait("vector", "vE", vv)
        vv = p.tick("vector", "vE", lambda v: v.reduce_sum(sc(4)[:, 0:1], sc(3), axis=mybir.AxisListType.X))
        p.wait("vector", "vE", vv)
        vv = p.tick("vector", "vE", lambda v: v.tensor_scalar_add(sc(4)[:, 1:2], sc(4)[:, 0:1], 1e-20))
        p.wait("vector", "vE", vv)
        vv = p.tick("vector", "vE", lambda v: v.reciprocal(sc(4)[:, 2:3], sc(4)[:, 1:2]))
        p.wait("vector", "vE", vv)
        vv = p.tick("vector", "vE", lambda v, o=Wtm[:, tt * 64:(tt + 1) * 64]:
                    v.tensor_scalar_mul(o, sc(3), sc(4)[:, 2:3]))
        p.wait("tensor", "vE", p.cnt["vE"])
        tv = p.mm(pb[3].ap()[0:64, tt:tt + 1], Itmb[:, tt * 64:(tt + 1) * 64],
                  onescolb_sb.ap()[:, :], start=(tt == 0), stop=True, tick=True)
        cnt_mm[tt] = tv
    p.wait("vector", "tE", cnt_mm[NTILE - 1])
    vv = p.tick("vector", "vE", lambda v: v.tensor_copy(cntv_sb, pb[3].ap()[0:64, 0:32]))
    p.wait("sync", "vE", vv)
    CQ = p.dma("sync", "qS", bass.AP(cnt_shard, 0, [[1, 64], [64, 32]]), cntv_sb, allow_slow_non_contiguous=True)
    p.wait("gpsimd", "qS", CQ)
    AG2 = p.inc("cc", 1)
    p.run("gpsimd", lambda g: g.collective_compute(
        "AllGather", AOP.bypass, replica_groups=RG,
        ins=[cnt_shard.ap().opt()], outs=[cnt_all.ap().opt()]).then_inc(p.sems["cc"], 1))

    cntg = at("cntg", 160896, [128, 2 * 64], F32).ap()[:, :]
    p.wait("sync", "cc", AG2)
    for ch in range(2):
        p.dma("sync", "qS", cntg[:, ch * 64:(ch + 1) * 64],
              bass.AP(cnt_all, ch * 1024, [[64, 16], [2048, 8], [1, 64]]))
    GLD = p.cnt["qS"]
    if "cntg" in dbg:
        tapc = dbgout("cntg", [128, 128], F32)
        p.wait("scalar", "qS", GLD)
        p.dma("scalar", "qA", tapc.ap()[:, :], cntg)
    p.wait("tensor", "qS", GLD)
    tv = p.mm(pb[2].ap()[:, 0:64], lt128f_sb.ap()[:, :], cntg[:, 0:64],
              start=True, stop=True, tick=True)
    tv = p.mm(pb[2].ap()[:, 64:128], onesf_sb.ap()[:, :], cntg[:, 0:64],
              start=False, stop=True, tick=True)
    tv = p.mm(pb[2].ap()[:, 64:128], lt128f_sb.ap()[:, :], cntg[:, 64:128],
              start=False, stop=True, tick=True)
    p.wait("vector", "tE", tv)
    vv = p.tick("vector", "vE", lambda v: v.tensor_copy(goff_sb, pb[2].ap()[:, 0:128]))
    p.wait("tensor", "vE", vv)
    tv = p.mm(pb[3].ap()[0:64, 32:64], goff_sb[:, 0:64], selmat_sb.ap()[:, 0:NTILE],
              start=True, stop=False, tick=False)
    tv = p.mm(pb[3].ap()[0:64, 32:64], goff_sb[:, 64:128], selmat_sb.ap()[:, NTILE:2 * NTILE],
              start=False, stop=True, tick=True)
    p.wait("vector", "tE", tv)
    vv = p.tick("vector", "vE", lambda v: v.tensor_copy(sc(5)[0:64, 0:32], pb[3].ap()[0:64, 32:64]))
    p.wait("tensor", "vE", vv)
    tv = p.tick("tensor", "tE", lambda t: t.transpose(pb[2].ap()[0:32, 0:64],
                                                      sc(5)[0:64, 0:32], identf_sb.ap()[0:64, 0:64]))
    p.wait("vector", "tE", tv)
    vv = p.tick("vector", "vE", lambda v: v.tensor_copy(rowtmp, pb[2].ap()[0:32, 0:64]))
    p.wait("sync", "vE", vv)
    BQ = p.dma("sync", "qS", bass.AP(boun_f, 0, [[64, 32], [1, 64]]), rowtmp)
    p.wait("sync", "qS", BQ)
    BQ2 = p.dma("sync", "qS", myoffR, bass.AP(boun_f, 0, [[2048, 1], [1, 2048]]))

    lsv = 0
    rank_copy = 0
    for tt in range(NTILE):
        p.wait("tensor", "qS", BQ2)
        if rank_copy:
            p.wait("tensor", "vE", rank_copy)
        tv = p.mm(pb[2].ap()[:, 0:64], lt128f_sb.ap()[:, :],
                  Itmf[:, tt * 64:(tt + 1) * 64], start=True, stop=False)
        tv = p.mm(pb[2].ap()[:, 0:64], onesrowf_sb.ap()[:, :],
                  myoffR[:, tt * 64:(tt + 1) * 64], start=False, stop=True, tick=True)
        p.wait("vector", "tE", tv)
        vv = p.tick("vector", "vE", lambda v: v.tensor_copy(sc(6), pb[2].ap()[:, 0:64]))
        rank_copy = vv
        p.wait("vector", "vE", vv)
        vv = p.tick("vector", "vE", lambda v: v.tensor_single_scalar(sc(7), sc(6), float(CAP), AOP.is_lt))
        p.wait("vector", "vE", vv)
        vv = p.tick("vector", "vE", lambda v, i=Itmf[:, tt * 64:(tt + 1) * 64]:
                    v.tensor_mul(sc(7), sc(7), i))
        p.tick("vector", "vE", lambda v: v.tensor_add(sc(8), sc(6), e384b_sb))
        vv = p.tick("vector", "vE", lambda v, i=Itmf[:, tt * 64:(tt + 1) * 64]: v.tensor_tensor_scan(
            sc(9), i, zero_sb.ap()[:, 0:64], 0.0, AOP.add, AOP.add))
        p.wait("vector", "vE", vv)
        vv = p.tick("vector", "vE", lambda v, i=Itmf[:, tt * 64:(tt + 1) * 64]:
                    v.tensor_sub(sc(9), sc(9), i))
        p.wait("vector", "vE", vv)
        vv = p.tick("vector", "vE", lambda v, c_=float(6 * tt + 1): v.tensor_scalar(
            sc(9), sc(9), c_, None, AOP.add))
        p.wait("vector", "vE", vv)
        vv = p.tick("vector", "vE", lambda v: v.tensor_mul(sc(9), sc(9), sc(7)))
        p.wait("vector", "vE", vv)
        vv = p.tick("vector", "vE", lambda v, o=ls_idx[:, tt * 64:(tt + 1) * 64]:
                    v.tensor_scalar(o, sc(9), 1.0, None, AOP.subtract))
        p.tick("vector", "vE", lambda v, o=ls_slot[:, tt * 64:(tt + 1) * 64]:
               v.tensor_scalar(o, sc(8), 1.0, None, AOP.add))
        vv = p.tick("vector", "vE", lambda v, o=ls_wq[:, tt * 64:(tt + 1) * 64],
                    i=Wtm[:, tt * 64:(tt + 1) * 64]:
                    v.tensor_scalar(o, i, 32000.0, None, AOP.mult))
        lsv = vv
    p.wait("gpsimd", "vE", lsv)
    g1 = p.tick("gpsimd", "gE", lambda g: g.local_scatter(
        out_ap=list6.ap()[:, :], data_ap=ls_slot, idxs_ap=ls_idx,
        channels=128, num_elems=NTILE * K, num_idxs=NTILE * E))
    g2 = p.tick("gpsimd", "gE", lambda g: g.local_scatter(
        out_ap=wq6.ap()[:, :], data_ap=ls_wq, idxs_ap=ls_idx,
        channels=128, num_elems=NTILE * K, num_idxs=NTILE * E))

    if "route" in dbg:
        tapr = dbgout("route", [128, NTILE * K * 2], I16)
        p.wait("scalar", "gE", g2)
        p.dma("scalar", "qA", tapr.ap()[:, 0:NTILE * K], list6.ap()[:, :])
        p.dma("scalar", "qA", tapr.ap()[:, NTILE * K:], wq6.ap()[:, :])

    indf = at("indf", 65536, [128, NTILE * K], F32).ap()[:, :]
    l6f2 = at("l6f2", 66368, [128, NTILE * K], F32).ap()[:, :]
    wqf = at("wqf", 67200, [128, NTILE * K], F32).ap()[:, :]
    j4b = at("j4b", 68032, [128, 8], F32).ap()[:, :]
    tabval_t = at("tabval", 68096, [128, NTILE * K * 2], F32)
    tidx_str = at("tidx_str", 72832, [128, NSEL // 16], I16).ap()[:, :]
    zcomb_str = sbuf("zcomb_str", [128, NSEL // 16], I16).ap()[:, :]
    tabidx = at("tabidx", 75904, [128, NTILE * K], I16)
    zcf = at("zcf", 137216, [128, NTILE * K], F32).ap()[:, :]
    tbf = at("tbf", 138240, [128, NTILE * K], F32).ap()[:, :]
    mytf = at("mytf", 139264, [128, 24], F32).ap()[:, :]
    str16f_t = at("str16f", 147456, [16, NSEL // 16], F32)

    p.wait("vector", "gE", g2)
    vv = p.tick("vector", "vE", lambda v: v.tensor_copy(l6f2, list6.ap()[:, :]))
    p.wait("vector", "vE", vv)
    vv = p.tick("vector", "vE", lambda v: v.tensor_single_scalar(indf, l6f2, 0.5, AOP.is_gt))
    p.tick("vector", "vE", lambda v: v.tensor_copy(wqf, wq6.ap()[:, :]))
    vv = p.tick("vector", "vE", lambda v: v.tensor_scalar_mul(wqf, wqf, 1.0 / 32000.0))
    IND_V = vv

    p.wait("gpsimd", "vE", V_ZERO)
    off = 0
    while off < TAB_ROWS * 64:
        n = min(262144, TAB_ROWS * 64 - off)
        p.dma("gpsimd", "qG", bass.AP(tabTW, off, [[n // 128, 128], [1, n // 128]]), zero_sb.ap()[:, 0:n // 128])
        off += n
    p.dma("gpsimd", "qG", z_shard.ap()[SPC:ZROWS, :], zero_sb.ap()[:, 0:1024])
    PREZ_Q = p.cnt["qG"]

    p.wait("vector", "vE", IND_V)
    p.tick("vector", "vE", lambda v: v.tensor_scalar(sc2[:, 0:192], l6f2, -1.0 - float(TRASH_T), None, AOP.add))
    p.wait("vector", "vE", p.cnt["vE"])
    p.tick("vector", "vE", lambda v: v.tensor_mul(sc2[:, 0:192], sc2[:, 0:192], indf))
    p.wait("vector", "vE", p.cnt["vE"])
    p.tick("vector", "vE", lambda v: v.tensor_scalar(tbf, sc2[:, 0:192], float(TRASH_T), None, AOP.add))
    for tt in range(NTILE):
        p.tick("vector", "vE", lambda v, o=bass.AP(tabval_t, tt * 12, [[NSEL * 2 // 128, 128], [2, 6]]),
               i=gid_sb.ap()[:, tt:tt + 1].to_broadcast([128, 6]): v.tensor_copy(o, i))
    tv2 = p.tick("vector", "vE", lambda v: v.tensor_copy(
        bass.AP(tabval_t, 1, [[NSEL * 2 // 128, 128], [2, NSEL // 128]]), wqf))
    p.wait("vector", "vE", p.cnt["vE"])
    p.tick("vector", "vE", lambda v: v.tensor_scalar(
        sc2[:, 0:192], l6f2, float(ZERO_ROW + 1), None, AOP.subtract))
    p.wait("vector", "vE", p.cnt["vE"])
    p.tick("vector", "vE", lambda v: v.tensor_mul(sc2[:, 0:192], sc2[:, 0:192], indf))
    p.wait("vector", "vE", p.cnt["vE"])
    p.tick("vector", "vE", lambda v: v.tensor_scalar(
        zcf, sc2[:, 0:192], float(ZERO_ROW), None, AOP.add))
    P3VEC = p.cnt["vE"]

    def stripe(natf_ap, ncols, dst_i16_ap, wait_v):
        W8 = ncols * 8
        last = wait_v
        for g16 in range(8):
            p.wait("tensor", "vE", last)
            tv = p.mm(pb[2].ap()[0:16, 0:ncols],
                      identf_sb.ap()[:, g16 * 16:(g16 + 1) * 16], natf_ap,
                      start=True, stop=True, tick=True)
            p.wait("vector", "tE", tv)
            last = p.tick("vector", "vE", lambda v, g16=g16, ncols=ncols: v.tensor_copy(
                bass.AP(str16f_t, g16, [[NSEL // 16, 16], [8, ncols]]), pb[2].ap()[0:16, 0:ncols]))
        out_v = last
        for k0 in range(0, W8, 512):
            kw = min(512, W8 - k0)
            p.wait("tensor", "vE", out_v)
            tv = p.mm(pb[3].ap()[:, 0:kw], rep16_sb.ap()[:, :],
                      bass.AP(str16f_t, k0, [[NSEL // 16, 16], [1, kw]]),
                      start=True, stop=True, tick=True)
            p.wait("vector", "tE", tv)
            out_v = p.tick("vector", "vE", lambda v, k0=k0, kw=kw, d=dst_i16_ap: v.tensor_copy(
                d[:, k0:k0 + kw], pb[3].ap()[:, 0:kw]))
        return out_v

    STP_TB = stripe(tbf, NTILE * K, tidx_str, P3VEC)
    STP_ZC = stripe(zcf, NTILE * K, zcomb_str, P3VEC)

    p.wait("gpsimd", "vE", STP_TB)
    p.wait("gpsimd", "qG", PREZ_Q)
    NCH = NSEL // 1536
    for k in range(NCH):
        p.inc("qG", 16)
        def c2(g, k=k):
            g.dma_scatter_add(
                out_ap=tabTW.ap()[:, 0:2],
                in_ap=bass.AP(tabval_t, k * 24, [[NSEL * 2 // 128, 128], [2, 12], [1, 2]]),
                idxs_ap=tidx_str[:, k * 96:(k + 1) * 96],
                num_idxs=1536, num_idxs_reg=p.swcnt(g, 1536), elem_size=2, elem_step=64,
            ).then_inc(p.sems["qG"], 16)
        p.on("gpsimd", c2)
    SCAT_Q = p.cnt["qG"]

    p.wait("gpsimd", "qG", SCAT_Q)
    tab2 = nc.dram_tensor("tab2", [TAB_ROWS, 64], F32)
    AR3 = p.inc("cc", 1)
    p.run("gpsimd", lambda g: g.collective_compute(
        "AllReduce", AOP.add, replica_groups=RG,
        ins=[tabTW.ap().opt()], outs=[tab2.ap().opt()]).then_inc(p.sems["cc"], 1))
    p.wait("gpsimd", "cc", AR3)
    for k in range(3):
        p.inc("qG", 16)
        def c3(g, k=k):
            g.dma_gather(
                out_ap=bass.AP(myTW.tensor, k * 512, [[1536, 128], [64, 8], [1, 64]]),
                in_ap=tab2.ap(),
                idxs_ap=myslots_sb.ap()[:, k * 64:(k + 1) * 64],
                num_idxs=1024, num_idxs_reg=p.swcnt(g, 1024), elem_size=64,
            ).then_inc(p.sems["qG"], 16)
        p.on("gpsimd", c3)
    MYTW_Q = p.cnt["qG"]
    p.wait("vector", "qG", MYTW_Q)
    vv = p.tick("vector", "vE", lambda v: v.tensor_copy(
        mytf, bass.AP(myTW.tensor, 0, [[1536, 128], [64, 24]])))
    STP_MT = stripe(mytf, 24, xg_idx, vv)

    xsb_t = at("xsb", 65536, [128, 8 * SPC], BF16)
    p.wait("gpsimd", "vE", STP_MT)
    p.wait("gpsimd", "vE", P3VEC)
    p.wait("gpsimd", "qG", SCAT_Q)
    XSQ = p.gatherT(xsb_t, 8 * SPC, xn_all.ap(), xg_idx, SPC, elem=D, chunk=384)

    ew1_sb = [at(f"ew1_{i}", 114688 + i * 8192, [128, 8 * 512], BF16).ap()[:, :] for i in range(2)]
    ew2_sb = [at(f"ew2_{i}", 131072 + i * 8192, [128, 4 * 1024], BF16).ap()[:, :] for i in range(2)]
    hmid = at("hmid", 147456, [128, 4 * 384], BF16).ap()[:, :]
    zstg = [at(f"zstg{i}", 150592 + i * 2048, [128, 1024], BF16).ap()[:, :] for i in range(2)]

    def load_ew(e):
        buf = e % 2
        for dc in range(8):
            p.dma("gpsimd", "qG", ew1_sb[buf][:, dc * 512:(dc + 1) * 512],
                  ew1.ap()[e, dc * 128:(dc + 1) * 128, :])
        for fc in range(4):
            p.dma("gpsimd", "qG", ew2_sb[buf][:, fc * 1024:(fc + 1) * 1024],
                  ew2.ap()[e, fc * 128:(fc + 1) * 128, :])
        return p.cnt["qG"]

    ew_q = {0: load_ew(0)}
    ew_q[1] = load_ew(1)
    p.wait("tensor", "qG", XSQ)
    zq = {}
    ew_done = {}
    for e in range(EPC):
        buf = e % 2
        p.wait("tensor", "qG", ew_q[e])
        for fc in range(4):
            bank = pb[fc].ap()[:, 0:384]
            if e:
                p.wait("tensor", "sE", ew_done[(e - 1, fc)])
            for dc in range(8):
                tv = p.mm(bank, ew1_sb[buf][:, dc * 512 + fc * 128: dc * 512 + (fc + 1) * 128],
                          bass.AP(xsb_t, e * 3072 + dc * 384, [[8 * SPC, 128], [1, 384]]),
                          start=(dc == 0), stop=(dc == 7), tick=(dc == 7))
            p.wait("scalar", "tE", tv)
            sv = p.tick("scalar", "sE", lambda s, o=hmid[:, fc * 384:(fc + 1) * 384], i=bank:
                        s.activation(out=o, in_=i, func=AF.Gelu_apprx_tanh, bias=0.0, scale=1.0))
            ew_done[(e, fc)] = sv
        for sub in range(3):
            bank2 = [pb[4 + 2 * (sub % 2)], pb[5 + 2 * (sub % 2)]]
            p.wait("tensor", "sE", ew_done[(e, 3)])
            if (e, sub - 2) in zq:
                p.wait("tensor", "qS", zq[(e, sub - 2)])
            elif e and sub < 2:
                p.wait("tensor", "qS", zq.get((e - 1, sub + 1), 0))
            for hf in range(2):
                for fc in range(4):
                    tv = p.mm(bank2[hf].ap()[:, :],
                              hmid[:, fc * 384 + sub * 128: fc * 384 + sub * 128 + 128],
                              ew2_sb[buf][:, fc * 1024 + hf * 512: fc * 1024 + (hf + 1) * 512],
                              start=(fc == 0), stop=(fc == 3), tick=(fc == 3))
                p.wait("scalar", "tE", tv)
                wcol = bass.AP(myTW.tensor, 64 * (e * 3 + sub) + 1, [[1536, 128], [1, 1]])
                sv = p.tick("scalar", "sE", lambda s, o=zstg[sub % 2][:, hf * 512:(hf + 1) * 512],
                            i=bank2[hf].ap()[:, :], w=wcol:
                            s.activation(out=o, in_=i, func=AF.Identity, bias=0.0, scale=w))
            p.wait("sync", "sE", sv)
            zq[(e, sub)] = p.dma("sync", "qS",
                                 z_shard.ap()[e * 384 + sub * 128: e * 384 + (sub + 1) * 128, :],
                                 zstg[sub % 2][:, :])
        if e + 2 < EPC:
            p.wait("gpsimd", "tE", p.cnt["tE"])
            ew_q[e + 2] = load_ew(e + 2)
    PEXP_Q = p.cnt["qS"]
    p.wait("gpsimd", "qS", PEXP_Q)
    p.wait("gpsimd", "qG", PREZ_Q)
    AG4 = p.inc("cc", 1)
    p.run("gpsimd", lambda g: g.collective_compute(
        "AllGather", AOP.bypass, replica_groups=RG,
        ins=[z_shard.ap().opt()], outs=[z_all.ap().opt()]).then_inc(p.sems["cc"], 1))

    if "ztab" in dbg:
        tapz = dbgout("ztab", [ZROWS, D], BF16)
        p.wait("scalar", "qS", PEXP_Q)
        p.dma("scalar", "qA", tapz.ap()[:, :], z_shard.ap()[:, :])
        tapt = dbgout("tab", [TAB_ROWS, 2], F32)
        p.wait("scalar", "cc", AR3)
        p.dma("scalar", "qA", tapt.ap()[:, :], tab2.ap()[:, 0:2])

    sw1_sb = at("sw1_sb", 114688, [128, 8 * SF], BF16).ap()[:, :]
    sw2_sb = at("sw2_sb", 147456, [128, 16 * 1024], BF16).ap()[:, :]
    mid_sb = at("mid_sb", 65536, [128, 16 * 256], BF16).ap()[:, :]
    zrows = [at(f"zrows{i}", 73728 + i * 12288, [128, 6 * 1024], BF16) for i in range(2)]
    srows4_t = at("srows4", 98304, [128, 4 * 1024], BF16)
    xrow2 = at("xrow2", 106496, [128, 1024], F32).ap()[:, :]
    orow = at("orow", 110592, [128, 1024], F32).ap()[:, :]

    p.wait("gpsimd", "tE", p.cnt["tE"])
    for dc in range(8):
        p.dma("gpsimd", "qG", sw1_sb[:, dc * SF:(dc + 1) * SF], sw1T.ap()[dc * 128:(dc + 1) * 128, :])
    for hc in range(16):
        p.dma("gpsimd", "qG", sw2_sb[:, hc * 1024:(hc + 1) * 1024], sw2T.ap()[hc * 128:(hc + 1) * 128, :])
    SWQ = p.cnt["qG"]

    CC_ALL = p.cnt["cc"]
    p.wait("gpsimd", "cc", CC_ALL)
    p.wait("gpsimd", "vE", STP_ZC)
    p.wait("gpsimd", "qG", SWQ)

    ZG = {}
    def em_zg(tt):
        if tt >= 2:
            p.wait("gpsimd", "qG", ZG[tt - 1])
        ZG[tt] = p.inc("qG", 16)
        def call(g, tt=tt):
            g.dma_gather(
                out_ap=bass.AP(zrows[tt % 2], 0, [[6 * 1024, 128], [1024, 6], [1, 1024]]),
                in_ap=z_all.ap(),
                idxs_ap=zcomb_str[:, tt * 48:(tt + 1) * 48],
                num_idxs=768, num_idxs_reg=p.swcnt(g, 768), elem_size=D,
            ).then_inc(p.sems["qG"], 16)
        p.on("gpsimd", call)

    SGq = {}
    def em_sg(gg):
        if gg >= 1:
            p.wait("gpsimd", "qG", SGq[gg - 1])
        SGq[gg] = p.inc("qG", 16)
        def call(g, gg=gg):
            g.dma_gather(
                out_ap=bass.AP(srows4_t, 0, [[4 * 1024, 128], [1024, 4], [1, 1024]]),
                in_ap=s_all.ap(),
                idxs_ap=moegi_sb.ap()[:, gg * 32:(gg + 1) * 32],
                num_idxs=512, num_idxs_reg=p.swcnt(g, 512), elem_size=D,
            ).then_inc(p.sems["qG"], 16)
        p.on("gpsimd", call)

    em_zg(0)
    em_zg(1)
    em_sg(0)
    XR = {0: p.dma("sync", "qS", xrow2, x_moe.ap()[0:128, :])}

    p.wait("tensor", "qG", SWQ)
    gq = {}
    ztick = {}
    vops = {}
    oq = {}
    for tl in range(16):
        for hc in range(16):
            bank = pb[hc % 2].ap()[:, 0:256]
            for dc in range(8):
                tv = p.mm(bank, sw1_sb[:, dc * SF + hc * 128: dc * SF + (hc + 1) * 128],
                          bass.AP(xnm, tl * 2048 + dc * 256, [[8 * TPC, 128], [1, 256]]),
                          start=(dc == 0), stop=(dc == 7), tick=(dc == 7))
            p.wait("scalar", "tE", tv)
            sv = p.tick("scalar", "sE", lambda s, o=mid_sb[:, hc * 256:(hc + 1) * 256], i=bank:
                        s.activation(out=o, in_=i, func=AF.Gelu_apprx_tanh, bias=0.0, scale=1.0))
            gq[("mid", tl, hc)] = sv
            p.wait("tensor", "sE", sv)
        p.wait("tensor", "sE", gq[("mid", tl, 15)])
        for sub in range(2):
            tt = tl * 2 + sub
            b2 = [pb[4 + 2 * sub], pb[5 + 2 * sub]]
            if tt >= 2:
                p.wait("tensor", "sE", gq[("sil", tt - 2)])
            for hf in range(2):
                for hc in range(16):
                    p.mm(b2[hf].ap()[:, :],
                         mid_sb[:, hc * 256 + sub * 128: hc * 256 + sub * 128 + 128],
                         sw2_sb[:, hc * 1024 + hf * 512: hc * 1024 + (hf + 1) * 512],
                         start=(hc == 0), stop=False)
            p.wait("tensor", "qG", ZG[tt])
            for hf in range(2):
                for j in range(6):
                    tv = p.mm(b2[hf].ap()[:, :], ident_sb.ap()[:, :],
                              bass.AP(zrows[tt % 2], j * 1024 + hf * 512, [[6 * 1024, 128], [1, 512]]),
                              start=False, stop=(j == 5), tick=(j == 5))
            ztick[tt] = tv
            if tt + 2 < NTILE:
                p.wait("gpsimd", "tE", ztick[tt])
                em_zg(tt + 2)
            if "gdbg" in dbg:
                tapg = dbgout("gdbg", [TPC, D], F32) if tt == 0 else dbg_t["gdbg"]
                p.wait("vector", "tE", ztick[tt])
                if tt >= 1:
                    p.wait("vector", "qA", p.cnt["qA"])
                    p.wait("vector", "qS", oq[tt - 1])
                for hf_ in range(2):
                    gv = p.tick("vector", "vE", lambda v, o=orow[:, hf_ * 512:(hf_ + 1) * 512],
                                i=b2[hf_].ap()[:, :]: v.tensor_copy(o, i))
                p.wait("scalar", "vE", gv)
                p.dma("scalar", "qA", tapg.ap()[tt * 128:(tt + 1) * 128, :], orow)
            p.wait("scalar", "tE", ztick[tt])
            if tt >= 1:
                p.wait("scalar", "vE", vops[tt - 1])
            for hf in range(2):
                sv = p.tick("scalar", "sE", lambda s, i=b2[hf].ap()[:, :], o=pb[2 + hf].ap()[:, :]:
                            s.activation(out=o, in_=i, func=AF.Silu, bias=0.0, scale=1.0))
            gq[("sil", tt)] = sv
            gg, r = tt // 4, tt % 4
            if "gdbg" in dbg:
                p.wait("vector", "qA", p.cnt["qA"])
            p.wait("vector", "sE", sv)
            p.wait("vector", "qG", SGq[gg])
            p.wait("vector", "qS", XR[tt])
            if tt >= 1:
                p.wait("vector", "qS", oq[tt - 1])
            for hf in range(2):
                vv = p.tick("vector", "vE", lambda v, hf=hf, r=r:
                            v.tensor_mul(orow[:, hf * 512:(hf + 1) * 512],
                                         pb[2 + hf].ap()[:, :],
                                         bass.AP(srows4_t, r * 1024 + hf * 512, [[4 * 1024, 128], [1, 512]])))
            vops[tt] = vv
            p.wait("vector", "vE", vv)
            vv = p.tick("vector", "vE", lambda v: v.tensor_add(orow, orow, xrow2))
            p.wait("sync", "vE", vv)
            oq[tt] = p.dma("sync", "qS", out_t.ap()[tt * 128:(tt + 1) * 128, :], orow)
            if tt + 1 < NTILE:
                XR[tt + 1] = p.dma("sync", "qS", xrow2, x_moe.ap()[(tt + 1) * 128:(tt + 2) * 128, :])
            if r == 3 and gg + 1 < 8:
                p.wait("gpsimd", "vE", vops[tt])
                em_sg(gg + 1)
    OUT_Q = p.cnt["qS"]

    if "zc" in dbg:
        tapzc = dbgout("zc", [128, NSEL // 16], I16)
        p.wait("scalar", "qS", OUT_Q)
        p.dma("scalar", "qA", tapzc.ap()[:, :], zcomb_str)
    if "zall" in dbg:
        tapza = dbgout("zall", [NCORES * ZROWS, D], BF16)
        p.wait("scalar", "qS", OUT_Q)
        p.dma("scalar", "qA", tapza.ap()[:, :], z_all.ap()[:, :])
    if "sall" in dbg:
        tapsa = dbgout("sall", [T, D], BF16)
        p.wait("scalar", "qS", OUT_Q)
        p.dma("scalar", "qA", tapsa.ap()[:, :], s_all.ap()[:, :])
    if dbg_t:
        for e_ in ("scalar",):
            p.wait(e_, "qA", p.cnt["qA"])
    p.wait("sync", "qS", OUT_Q)
    p.wait("sync", "qG", p.cnt["qG"])

    with nc.Block() as block:
        p.emit(block)
    _finalize_passes(nc)
    return nc


_NC_CACHE = {}


def kernel(**inputs):
    in_maps = _host_prepare(inputs)
    key = ()
    if key not in _NC_CACHE:
        _NC_CACHE[key] = build()
    nc = _NC_CACHE[key]
    res = run_bass_kernel_spmd(nc, in_maps, core_ids=list(range(NCORES)))
    return _assemble(res.results)


# version 03
# speedup vs baseline: 2.4772x; 2.4772x over previous
import numpy as np
import ml_dtypes
import concourse.bass as bass
import concourse.mybir as mybir
from concourse.bass_utils import run_bass_kernel_spmd

F32 = mybir.dt.float32
BF16 = mybir.dt.bfloat16
I16 = mybir.dt.int16
I32 = mybir.dt.int32
U8 = mybir.dt.uint8
AF = mybir.ActivationFunctionType
AOP = mybir.AluOpType
BF = ml_dtypes.bfloat16

NCORES = 8
Bb, Lq, D, HID = 16, 2048, 1024, 4096
T = Bb * Lq
TPC = T // NCORES
NTILE = TPC // 128
E, K, CAP = 64, 6, 384
EPC = E // NCORES
SPC = EPC * CAP
ECAP = E * CAP
NPAIR = TPC * E
NSEL = TPC * K
SF = 2048
ZPAD = 128
ZROWS = SPC + ZPAD
ZERO_ROW = SPC
TRASH_T = NCORES * ZROWS + 8
TAB_ROWS = NCORES * ZROWS + 128
FL_ROWS = NSEL + 128
FL_TRASH = NSEL + 8
GTRASH = TPC

RG = [list(range(NCORES))]



def _toeplitz_stack(ker):
    p = np.arange(128)[:, None]
    j = np.arange(64)[None, :]
    kk = j - p + 64
    mask = (kk >= 0) & (kk < 64)
    kkc = np.clip(kk, 0, 63)
    S = ker[:, kkc] * mask[None, :, :]
    T = S.reshape(16, 64, 128, 64).transpose(2, 0, 1, 3).reshape(128, 16 * 4096)
    return np.ascontiguousarray(T.astype(np.float32))


def _fm_bias(b):
    n = b.shape[0] // 128
    return np.ascontiguousarray(b.reshape(n, 128).T.astype(np.float32))


def _striped_i16(ids, n):
    a = np.zeros((128, n // 16), np.int16)
    seq = np.asarray(ids, np.int64)
    for g in range(8):
        a[g * 16:(g + 1) * 16, :] = seq.reshape(n // 16, 16).T
    return a


def _host_prepare(inputs):
    x = np.ascontiguousarray(np.asarray(inputs["L0_attn_x"], np.float32).reshape(T, D))
    ker = np.asarray(inputs["L0_attn_kernel"], np.float32)
    toep = _toeplitz_stack(ker)
    sw1T = np.ascontiguousarray(np.asarray(inputs["L0_ffn_sw1"], np.float32).T)
    sw2T = np.ascontiguousarray(np.asarray(inputs["L0_ffn_sw2"], np.float32).T)
    rT = np.ascontiguousarray(np.asarray(inputs["L0_ffn_router_weight"], np.float32).T)
    w1 = np.ascontiguousarray(np.asarray(inputs["L0_attn_mod_w1"], np.float32))
    w2 = np.ascontiguousarray(np.asarray(inputs["L0_attn_mod_w2"], np.float32))
    inw = np.ascontiguousarray(np.asarray(inputs["L0_attn_in_w"], np.float32))
    outw = np.ascontiguousarray(np.asarray(inputs["L0_attn_out_w"], np.float32))
    ew1 = np.asarray(inputs["L0_ffn_w1"], np.float32)
    ew2 = np.asarray(inputs["L0_ffn_w2"], np.float32)
    ebias = np.asarray(inputs["L0_ffn_expert_bias"], np.float32)

    inb = _fm_bias(np.asarray(inputs["L0_attn_in_b"], np.float32))
    b1 = _fm_bias(np.asarray(inputs["L0_attn_mod_b1"], np.float32))
    b2 = _fm_bias(np.asarray(inputs["L0_attn_mod_b2"], np.float32))
    outb_row = np.ascontiguousarray(np.asarray(inputs["L0_attn_out_b"], np.float32)[None, :])

    ident = np.eye(128, dtype=BF)
    rep16 = np.tile(np.eye(16, dtype=np.float32), (1, 8))
    trashpat = (NCORES * ZROWS + (np.arange(128)[:, None] * NTILE * K
                + np.arange(NTILE * K)[None, :]) % 128).astype(np.float32)
    identf = np.eye(128, dtype=np.float32)
    lt128 = np.triu(np.ones((128, 128), BF), 1)
    onesrow = np.ones((1, 128), BF)
    onesrowf = np.ones((1, 128), np.float32)
    lt128f = np.triu(np.ones((128, 128), np.float32), 1)
    onesf = np.ones((128, 128), np.float32)
    onescolb = np.ones((128, 1), BF)
    ones64 = np.ones((128, 64), np.float32)
    ee = np.arange(64)
    e384row = (384.0 * ee + 128.0 * (ee // 8)).astype(np.float32)[None, :]
    ebias_row = ebias[None, :].astype(np.float32)
    ebias_col = ebias[:, None].astype(np.float32)

    xg = x.reshape(256, 128, D)
    in_maps = []
    for c in range(NCORES):
        gids = (np.arange(NTILE) * 8 + c) * 128
        my_blocks = np.arange(NTILE) * 8 + c
        x_moe = np.ascontiguousarray(xg[my_blocks].reshape(TPC, D))
        gid = np.ascontiguousarray(
            (gids[None, :] + np.arange(128)[:, None]).astype(np.float32))
        lid = np.ascontiguousarray(
            (np.arange(NTILE)[None, :] * 128 + np.arange(128)[:, None]).astype(np.float32))
        moeg_idx = _striped_i16(
            (gids[:, None] + np.arange(128)[None, :]).reshape(-1), TPC)
        sel = np.zeros((256, NTILE), np.float32)
        sel[my_blocks, np.arange(NTILE)] = 1.0
        selmat = np.ascontiguousarray(sel.reshape(2, 128, NTILE))
        myslots = _striped_i16(np.arange(SPC) + c * ZROWS, SPC)
        m = {
            "x_mix": np.ascontiguousarray(x[c * TPC:(c + 1) * TPC]),
            "x_moe": x_moe,
            "inw": inw, "inb": inb, "w1": w1, "b1": b1, "w2": w2, "b2": b2,
            "outw": outw, "outb_row": outb_row,
            "toep": toep,
            "rT": rT, "ebias_row": ebias_row, "ebias_col": ebias_col,
            "ew1": np.ascontiguousarray(ew1[c * EPC:(c + 1) * EPC]),
            "ew2": np.ascontiguousarray(ew2[c * EPC:(c + 1) * EPC]),
            "sw1T": sw1T, "sw2T": sw2T,
            "ident": ident, "identf": identf, "lt128": lt128, "onesrow": onesrow,
            "onesrowf": onesrowf, "lt128f": lt128f, "onesf": onesf, "onescolb": onescolb,
            "ones64": ones64, "e384row": e384row,
            "gid": gid, "lid": lid, "moeg_idx": moeg_idx, "rep16": rep16, "trashpat": trashpat,
            "selmat": selmat, "myslots": myslots,
        }
        in_maps.append(m)
    return in_maps


def _assemble(results):
    out = np.empty((256, 128, D), np.float32)
    for c in range(NCORES):
        blocks = np.arange(NTILE) * 8 + c
        out[blocks] = results[c]["out"].reshape(NTILE, 128, D)
    return out.reshape(Bb, Lq, D)



ENGS = ("tensor", "vector", "scalar", "sync", "gpsimd")


class Prog:
    def __init__(self, nc):
        self.nc = nc
        self.ops = {e: [] for e in ENGS}
        self.sems = {}
        self.cnt = {}

    def sem(self, name):
        h = self.nc.alloc_semaphore(name)
        self.sems[name] = h
        self.cnt[name] = 0
        return h

    def inc(self, name, by):
        self.cnt[name] += by
        return self.cnt[name]

    def on(self, eng, fn):
        self.ops[eng].append(fn)

    def wait(self, eng, sem_name, val):
        if val <= 0:
            return
        k = (eng, sem_name)
        lw = getattr(self, "last_wait", None)
        if lw is None:
            self.last_wait = lw = {}
        if lw.get(k, 0) >= val:
            return
        lw[k] = val
        s = self.sems[sem_name]
        self.on(eng, lambda e, s=s, v=val: e.wait_ge(s, v))

    def dma(self, eng, sem_name, out, in_, **kw):
        v = self.inc(sem_name, 16)
        s = self.sems[sem_name]
        self.on(eng, lambda e, o=out, i=in_, s=s, kw=kw: e.dma_start(out=o, in_=i, **kw).then_inc(s, 16))
        return v

    def dmaT(self, eng, sem_name, out, in_):
        v = self.inc(sem_name, 16)
        s = self.sems[sem_name]
        self.on(eng, lambda e, o=out, i=in_, s=s: e.dma_start(out=o, in_=i, transpose=True).then_inc(s, 16))
        return v

    def tick(self, eng, sem_name, fn):
        v = self.inc(sem_name, 1)
        s = self.sems[sem_name]
        def wrapped(e, fn=fn, s=s):
            ins = fn(e)
            ins.then_inc(s, 1)
        self.on(eng, wrapped)
        return v

    def swcnt(self, g, val):
        if getattr(self, "_swreg", None) is None:
            self._swreg = g.alloc_register("swcnt")
        g.reg_mov(self._swreg, val)
        return self._swreg

    def gatherT(self, out_handle, total_cols, src_ap, idx_ap, n, elem=1024, chunk=256):
        nch = elem // 128
        for k in range(n // chunk):
            self.inc("qG", 16)
            def call(g, k=k, out_handle=out_handle, total_cols=total_cols, src_ap=src_ap,
                     idx_ap=idx_ap, elem=elem, chunk=chunk, nch=nch):
                o = bass.AP(out_handle, k * nch * chunk,
                            [[total_cols, 128], [chunk, nch], [1, chunk]])
                g.dma_gather(out_ap=o, in_ap=src_ap,
                             idxs_ap=idx_ap[:, k * (chunk // 16):(k + 1) * (chunk // 16)],
                             num_idxs=chunk, num_idxs_reg=self.swcnt(g, chunk), elem_size=elem,
                             transpose=True).then_inc(self.sems["qG"], 16)
            self.on("gpsimd", call)
        return self.cnt["qG"]

    def mm(self, out, lhsT, rhs, start, stop, tick=False):
        if tick:
            return self.tick("tensor", "tE", lambda t, o=out, l=lhsT, r=rhs, a=start, z=stop:
                             t.matmul(o, l, r, start=a, stop=z, skip_group_check=True))
        self.on("tensor", lambda t, o=out, l=lhsT, r=rhs, a=start, z=stop:
                t.matmul(o, l, r, start=a, stop=z, skip_group_check=True))
        return None

    def run(self, eng, fn):
        self.on(eng, fn)

    def barrier(self, producers):
        for e in ENGS:
            for sn, v in producers:
                self.wait(e, sn, v)

    def emit(self, block):
        for ename in ENGS:
            getattr(block, ename)(self._mk_runner(ename))

    def _mk_runner(self, ename):
        ops = self.ops[ename]
        def runner(eng):
            for fn in ops:
                fn(eng)
        return runner


def _finalize_passes(nc):
    from concourse.library_config import all_libraries, standard
    m = {}
    for lib in all_libraries:
        for it in lib.instructions:
            m[it] = m.get(it, 0) | (1 << lib.index)
    import bass_rust as _br
    _br.insert_library_loads(nc, m, len(all_libraries), standard.index)
    _br.insert_act_table_loads(nc)
    mybir.codegen_inst_isa_subclasses(nc)



def build(debug=()):
    nc = bass.Bass("TRN2", target_bir_lowering=False, detect_race_conditions=False, enable_asserts=False)
    dbg = set(debug)

    di = {}
    def inp(name, shape, dtype=F32):
        di[name] = nc.dram_tensor(name, shape, dtype, kind="ExternalInput")
        return di[name]

    x_mix = inp("x_mix", [TPC, D]); x_moe = inp("x_moe", [TPC, D])
    inw = inp("inw", [D, D]); inb = inp("inb", [128, 8])
    w1 = inp("w1", [D, HID]); b1 = inp("b1", [128, 32])
    w2 = inp("w2", [HID, D]); b2 = inp("b2", [128, 8])
    outw = inp("outw", [D, D]); outb_row = inp("outb_row", [1, D])
    toep = inp("toep", [128, 16 * 4096])
    rT = inp("rT", [D, E]); ebias_row = inp("ebias_row", [1, E]); ebias_col = inp("ebias_col", [E, 1])
    ew1 = inp("ew1", [EPC, D, 512]); ew2 = inp("ew2", [EPC, 512, D])
    sw1T = inp("sw1T", [D, SF]); sw2T = inp("sw2T", [SF, D])
    ident = inp("ident", [128, 128], BF16); identf = inp("identf", [128, 128], F32)
    lt128 = inp("lt128", [128, 128], BF16); onesrow = inp("onesrow", [1, 128], BF16)
    onesrowf = inp("onesrowf", [1, 128], F32); lt128f = inp("lt128f", [128, 128], F32)
    onesf = inp("onesf", [128, 128], F32); onescolb = inp("onescolb", [128, 1], BF16)
    ones64 = inp("ones64", [128, 64], F32); e384row = inp("e384row", [1, E])
    gid = inp("gid", [128, NTILE]); lid = inp("lid", [128, NTILE])
    moeg_idx = inp("moeg_idx", [128, TPC // 16], I16)
    selmat = inp("selmat", [2, 128, NTILE]); myslots = inp("myslots", [128, SPC // 16], I16)
    rep16 = inp("rep16", [16, 128], F32); trashpat = inp("trashpat", [128, NTILE * K], F32)

    out_t = nc.dram_tensor("out", [TPC, D], F32, kind="ExternalOutput")
    dbg_t = {}
    def dbgout(name, shape, dtype=F32):
        if name in dbg:
            dbg_t[name] = nc.dram_tensor("dbg_" + name, shape, dtype, kind="ExternalOutput")
            return dbg_t[name]
        return None

    xn_shard = nc.dram_tensor("xn_shard", [TPC, D], BF16)
    xn_all = nc.dram_tensor("xn_all", [T, D], BF16)
    MTW = Lq + 64
    modT_hbm = nc.dram_tensor("modT_hbm", [D, 2 * MTW], BF16)
    y_hbm = nc.dram_tensor("y_hbm", [TPC, D], BF16)
    s_shard = nc.dram_tensor("s_shard", [TPC, D], BF16)
    s_all = nc.dram_tensor("s_all", [T, D], BF16)
    cnt_shard = nc.dram_tensor("cnt_shard", [NTILE * E], F32)
    cnt_all = nc.dram_tensor("cnt_all", [NCORES * NTILE * E], F32)
    tabTW = nc.dram_tensor("tabTW", [TAB_ROWS, 64], F32)
    list6h = nc.dram_tensor("list6h", [TAB_ROWS, 128], I16)
    boun_f = nc.dram_tensor("boun_f", [4096], F32)
    z_shard = nc.dram_tensor("z_shard", [ZROWS, D], BF16)
    z_all = nc.dram_tensor("z_all", [NCORES * ZROWS, D], BF16)

    p = Prog(nc)
    for s in ("qS", "qG", "qA", "tE", "vE", "sE", "gE", "cc"):
        p.sem(s)

    A = nc.alloc_sbuf_tensor
    sb = {}
    def sbuf(name, shape, dtype):
        sb[name] = A(name, shape, dtype)
        return sb[name]

    ident_sb = sbuf("ident_sb", [128, 128], BF16)
    identf_sb = sbuf("identf_sb", [128, 128], F32)
    lt128_sb = sbuf("lt128_sb", [128, 128], BF16)
    onesrow_sb = sbuf("onesrow_sb", [1, 128], BF16)
    onesrowf_sb = sbuf("onesrowf_sb", [1, 128], F32)
    lt128f_sb = sbuf("lt128f_sb", [128, 128], F32)
    onesf_sb = sbuf("onesf_sb", [128, 128], F32)
    onescolb_sb = sbuf("onescolb_sb", [128, 1], BF16)
    selmat_sb = sbuf("selmat_sb", [128, 2 * NTILE], F32)
    myslots_sb = sbuf("myslots_sb", [128, SPC // 16], I16)
    ones64_sb = sbuf("ones64_sb", [128, 64], F32)
    e384_sb = sbuf("e384_sb", [1, E], F32)
    ebrow_sb = sbuf("ebrow_sb", [1, E], F32)
    ebcol_sb = sbuf("ebcol_sb", [E, 1], F32)
    gid_sb = sbuf("gid_sb", [128, NTILE], F32)
    lid_sb = sbuf("lid_sb", [128, NTILE], F32)
    inb_sb = sbuf("inb_sb", [128, 8], F32)
    b1_sb = sbuf("b1_sb", [128, 32], F32)
    b2_sb = sbuf("b2_sb", [128, 8], F32)
    outbrow_sb = sbuf("outbrow_sb", [1, D], BF16)
    moegi_sb = sbuf("moegi_sb", [128, TPC // 16], I16)
    zero_sb = sbuf("zero_sb", [128, 2048], F32)
    stat_sb = sbuf("stat_sb", [128, 16], F32)
    rep16_sb = sbuf("rep16_sb", [16, 128], F32)
    trashp_sb = sbuf("trashp_sb", [128, NTILE * K], F32)
    cntj_sb = sbuf("cntj_sb", [1, 8], F32)
    cntj_i = sbuf("cntj_i", [1, 8], I32)

    for (t, s_) in ((ident, ident_sb), (identf, identf_sb), (lt128, lt128_sb),
                    (onesrow, onesrow_sb), (ones64, ones64_sb), (e384row, e384_sb),
                    (ebias_row, ebrow_sb), (ebias_col, ebcol_sb), (gid, gid_sb),
                    (lid, lid_sb), (inb, inb_sb), (b1, b1_sb), (b2, b2_sb),
                    (moeg_idx, moegi_sb), (onesrowf, onesrowf_sb), (lt128f, lt128f_sb),
                    (onesf, onesf_sb), (onescolb, onescolb_sb), (myslots, myslots_sb),
                    (rep16, rep16_sb), (trashpat, trashp_sb)):
        p.dma("sync", "qS", s_.ap()[:, :], t.ap()[:, :])
    p.dma("sync", "qS", selmat_sb.ap()[:, 0:NTILE], selmat.ap()[0, :, :])
    p.dma("sync", "qS", selmat_sb.ap()[:, NTILE:2 * NTILE], selmat.ap()[1, :, :])
    p.dma("gpsimd", "qG", outbrow_sb.ap()[:, :], outb_row.ap()[:, :])
    CONST_Q = p.cnt["qS"]
    CONST_G = p.cnt["qG"]
    V_ZERO = p.tick("vector", "vE", lambda v: v.memset(zero_sb[:, :], 0.0))

    pb = [nc.alloc_psum_tensor(f"pb{i}", [128, 512], F32) for i in range(8)]

    AT_BASE = 47424
    AT_SIZE = 229376 - AT_BASE
    _at_names = {}
    def at(name, off_bytes, shape, dtype):
        assert off_bytes % 64 == 0
        sz = mybir.dt.size(dtype)
        for d in shape[1:]:
            pass
        return nc.alloc_sbuf_tensor_at(name, shape, dtype, offset=AT_BASE + off_bytes)

    xrow_t = at("xrow", 147456, [128, 1024], F32)
    xnrow_t = at("xnrow", 151552, [128, 1024], BF16)
    sq_t = at("sq", 157696, [128, 1024], F32)
    xrow = xrow_t.ap()[:, :]
    xnrow = xnrow_t.ap()[:, :]
    sq_sb = sq_t.ap()[:, :]
    for tt in range(NTILE):
        qv = p.dma("sync", "qS", xrow, x_mix.ap()[tt * 128:(tt + 1) * 128, :])
        p.wait("scalar", "qS", qv)
        sv = p.tick("scalar", "sE", lambda s: s.activation(
            out=sq_sb[:, :], in_=xrow, func=AF.Square, bias=0.0, scale=1.0))
        p.wait("vector", "qS", qv)
        p.tick("vector", "vE", lambda v: v.reduce_sum(stat_sb[:, 1:2], xrow, axis=mybir.AxisListType.X))
        p.wait("vector", "sE", sv)
        vv = p.tick("vector", "vE", lambda v: v.reduce_sum(stat_sb[:, 0:1], sq_sb[:, :], axis=mybir.AxisListType.X))
        p.wait("vector", "vE", vv)
        p.tick("vector", "vE", lambda v: v.tensor_scalar_mul(stat_sb[:, 2:3], stat_sb[:, 1:2], 1.0 / D))
        vv = p.tick("vector", "vE", lambda v: v.tensor_scalar_mul(stat_sb[:, 3:4], stat_sb[:, 0:1], 1.0 / D))
        p.wait("vector", "vE", vv)
        vv = p.tick("vector", "vE", lambda v: v.tensor_tensor(stat_sb[:, 4:5], stat_sb[:, 2:3], stat_sb[:, 2:3], AOP.mult))
        p.wait("vector", "vE", vv)
        vv = p.tick("vector", "vE", lambda v: v.tensor_tensor(stat_sb[:, 5:6], stat_sb[:, 3:4], stat_sb[:, 4:5], AOP.subtract))
        p.wait("vector", "vE", vv)
        vv = p.tick("vector", "vE", lambda v: v.tensor_scalar_add(stat_sb[:, 5:6], stat_sb[:, 5:6], 1e-5))
        p.wait("scalar", "vE", vv)
        sv = p.tick("scalar", "sE", lambda s: s.activation(
            out=stat_sb[:, 6:7], in_=stat_sb[:, 5:6], func=AF.Sqrt, bias=0.0, scale=1.0))
        p.wait("vector", "sE", sv)
        vv = p.tick("vector", "vE", lambda v: v.reciprocal(stat_sb[:, 7:8], stat_sb[:, 6:7]))
        p.wait("vector", "vE", vv)
        vv = p.tick("vector", "vE", lambda v: v.tensor_scalar(
            xnrow, xrow, stat_sb[:, 2:3], stat_sb[:, 7:8], AOP.subtract, AOP.mult))
        p.wait("sync", "vE", vv)
        p.dma("sync", "qS", xn_shard.ap()[tt * 128:(tt + 1) * 128, :], xnrow)
    P0_Q = p.cnt["qS"]

    p.wait("gpsimd", "qS", P0_Q)
    ccv = p.inc("cc", 1)
    p.run("gpsimd", lambda g: g.collective_compute(
        "AllGather", AOP.bypass, replica_groups=RG,
        ins=[xn_shard.ap().opt()], outs=[xn_all.ap().opt()]).then_inc(p.sems["cc"], 1))
    AG1 = ccv

    if "xn" in dbg:
        tap = dbgout("xn", [TPC, D], BF16)
        p.wait("scalar", "qS", P0_Q)
        for tt in range(NTILE):
            p.dma("scalar", "qA", tap.ap()[tt * 128:(tt + 1) * 128, :],
                  xn_shard.ap()[tt * 128:(tt + 1) * 128, :])
        p.wait("scalar", "qA", p.cnt["qA"])

    w1_sb = at("w1_sb", 0, [128, 8 * 4096], BF16).ap()[:, :]
    w2_sb = at("w2_sb", 65536, [128, 32 * 1024], BF16).ap()[:, :]
    inw_sb = at("inw_sb", 131072, [128, 8 * 1024], BF16).ap()[:, :]
    xnT = [at(f"xnT{i}", 147456 + i * 4096, [128, 8 * 256], BF16).ap()[:, :] for i in range(2)]
    hch = [at(f"hch{i}", 155648 + i * 512, [128, 256], BF16).ap()[:, :] for i in range(2)]
    xp_sb = at("xp_sb", 157696, [128, 8 * 256], F32).ap()[:, :]
    modu_sb = at("modu_sb", 165888, [128, 8 * 256], BF16).ap()[:, :]

    for dc in range(8):
        p.dma("gpsimd", "qG", w1_sb[:, dc * 4096:(dc + 1) * 4096], w1.ap()[dc * 128:(dc + 1) * 128, :])
    for hc in range(32):
        p.dma("gpsimd", "qG", w2_sb[:, hc * 1024:(hc + 1) * 1024], w2.ap()[hc * 128:(hc + 1) * 128, :])
    for dc in range(8):
        p.dma("gpsimd", "qG", inw_sb[:, dc * 1024:(dc + 1) * 1024], inw.ap()[dc * 128:(dc + 1) * 128, :])
    P1_W = p.cnt["qG"]

    NT1 = TPC // 256
    xnT_q = [0, 0]
    xnb_done = {}
    modu_q = 0
    gelu_tick = {}
    modmm_tick = {}
    xp_tick = {}

    def load_xnT(tile):
        buf = tile % 2
        p.wait("sync", "qS", P0_Q)
        if tile >= 2:
            p.wait("sync", "tE", xnb_done[tile - 2])
        for dc in range(8):
            p.dmaT("sync", "qS",
                   xnT[buf][:, dc * 256:(dc + 1) * 256],
                   xn_shard.ap()[tile * 256:(tile + 1) * 256, dc * 128:(dc + 1) * 128])
        xnT_q[buf] = p.cnt["qS"]

    load_xnT(0)
    load_xnT(1)
    p.wait("tensor", "qG", P1_W)

    modu_evac = 0
    for tile in range(NT1):
        buf = tile % 2
        p.wait("tensor", "qS", xnT_q[buf])
        xnb = xnT[buf]
        if tile:
            p.wait("scalar", "vE", modu_evac)
        for nc_ in range(8):
            bank = pb[4 + nc_ % 2].ap()[:, 0:256]
            prev = (tile, nc_ - 2) if nc_ >= 2 else (tile - 1, 6 + nc_)
            if prev in xp_tick:
                p.wait("tensor", "sE", xp_tick[prev][1])
            for dc in range(8):
                st_ = inw_sb[:, dc * 1024 + nc_ * 128: dc * 1024 + (nc_ + 1) * 128]
                tv = p.mm(bank, st_, xnb[:, dc * 256:(dc + 1) * 256],
                          start=(dc == 0), stop=(dc == 7), tick=(dc == 7))
            p.wait("scalar", "tE", tv)
            sv = p.tick("scalar", "sE", lambda s, o=xp_sb[:, nc_ * 256:(nc_ + 1) * 256],
                        i=bank, b_=inb_sb[:, nc_:nc_ + 1]: s.activation(
                            out=o, in_=i, func=AF.Identity, bias=b_, scale=1.0))
            xp_tick[(tile, nc_)] = (tv, sv)
        for hc in range(32):
            hb = pb[6 + hc % 2].ap()[:, 0:256]
            hbuf = hch[hc % 2]
            if (tile, hc - 2) in gelu_tick:
                p.wait("tensor", "sE", gelu_tick[(tile, hc - 2)])
            elif tile and hc < 2:
                p.wait("tensor", "sE", gelu_tick[(tile - 1, 30 + hc)])
            for dc in range(8):
                st_ = w1_sb[:, dc * 4096 + hc * 128: dc * 4096 + (hc + 1) * 128]
                tv = p.mm(hb, st_, xnb[:, dc * 256:(dc + 1) * 256],
                          start=(dc == 0), stop=(dc == 7), tick=(dc == 7))
            if hc == 31:
                xnb_done[tile] = tv
            p.wait("scalar", "tE", tv)
            if hc >= 2:
                p.wait("scalar", "tE", modmm_tick[(tile, hc - 2)])
            elif tile and hc < 2:
                p.wait("scalar", "tE", modmm_tick[(tile - 1, 30 + hc)])
            gv = p.tick("scalar", "sE", lambda s, o=hbuf, i=hb, b_=b1_sb[:, hc:hc + 1]:
                        s.activation(out=o, in_=i, func=AF.Gelu, bias=b_, scale=1.0))
            gelu_tick[(tile, hc)] = gv
            p.wait("tensor", "sE", gv)
            if hc == 0 and tile:
                p.wait("tensor", "vE", modu_evac)
            for dc in range(8):
                bank = pb[dc // 2].ap()[:, (dc % 2) * 256:(dc % 2) * 256 + 256]
                st_ = w2_sb[:, hc * 1024 + dc * 128: hc * 1024 + (dc + 1) * 128]
                tv = p.mm(bank, st_, hbuf, start=(hc == 0 and dc % 2 == 0),
                          stop=(hc == 31), tick=(dc == 7))
            modmm_tick[(tile, hc)] = tv
        p.wait("vector", "tE", modmm_tick[(tile, 31)])
        p.wait("vector", "sE", xp_tick[(tile, 7)][1])
        if tile:
            p.wait("vector", "qS", modu_q)
        for dc in range(8):
            bank = pb[dc // 2].ap()[:, (dc % 2) * 256:(dc % 2) * 256 + 256]
            vv = p.tick("vector", "vE", lambda v, o=modu_sb[:, dc * 256:(dc + 1) * 256],
                        i=bank, b_=b2_sb[:, dc:dc + 1], x2=xp_sb[:, dc * 256:(dc + 1) * 256]:
                        v.scalar_tensor_tensor(o, i, b_, x2, AOP.add, AOP.mult))
        modu_evac = vv
        p.wait("sync", "vE", modu_evac)
        b_of = tile // 8
        col0 = b_of * MTW + 64 + (tile % 8) * 256
        W2H = 2 * MTW
        dst = bass.AP(modT_hbm, col0, [[W2H, 128], [128 * W2H, 8], [1, 256]])
        modu_q = p.dma("sync", "qS", dst, modu_sb[:, :].rearrange("p (dc t) -> p dc t", dc=8))
        if tile % 8 == 7:
            hdst = bass.AP(modT_hbm, b_of * MTW, [[W2H, 128], [128 * W2H, 8], [1, 64]])
            modu_q = p.dma("sync", "qS", hdst,
                           modu_sb[:, :].rearrange("p (dc t) -> p dc t", dc=8)[:, :, 192:256])
        if tile + 2 < NT1:
            load_xnT(tile + 2)
    P1_MOD_Q = p.cnt["qS"]

    if "modT" in dbg:
        tap = dbgout("modT", [D, TPC], BF16)
        p.wait("scalar", "qS", P1_MOD_Q)
        for dc in range(8):
            for b_ in range(2):
                p.dma("scalar", "qA", tap.ap()[dc * 128:(dc + 1) * 128, b_ * Lq:(b_ + 1) * Lq],
                      modT_hbm.ap()[dc * 128:(dc + 1) * 128, b_ * MTW + 64: b_ * MTW + 64 + Lq])

    P1_T_END = p.cnt["tE"]

    MT_t = at("MT", 0, [128, 32 * 1024], BF16)
    ybuf_t = at("ybuf", 65536, [128, 32 * 512], BF16)
    toep_t = [at(f"toep{i}", 98304 + i * 8192, [128, 64 * 64], BF16) for i in range(2)]

    def load_toep(g):
        buf = g % 2
        return p.dma("gpsimd", "qG", toep_t[buf].ap()[:, :],
                     toep.ap()[:, (g % 16) * 4096:((g % 16) + 1) * 4096])

    CGRP = 16
    p.wait("gpsimd", "tE", P1_T_END)
    toep_q = {0: load_toep(0)}
    toep_q[1] = load_toep(1)

    y_store_q = 0
    mt_free_tick = 0
    ybuf_free = {}
    conv_mm_tick = {}
    for b_ in range(2):
        p.wait("sync", "qS", P1_MOD_Q)
        p.wait("sync", "tE", P1_T_END)
        if b_ == 1:
            p.wait("sync", "tE", mt_free_tick)
            p.wait("sync", "qS", y_store_q)
        for blk in range(32):
            p.dmaT("sync", "qS", MT_t.ap()[:, blk * 1024:(blk + 1) * 1024],
                   modT_hbm.ap()[:, b_ * MTW + blk * 64: b_ * MTW + blk * 64 + 128])
        mt_q = p.cnt["qS"]
        if b_ == 0 and "MT0" in dbg:
            tap = dbgout("MT0", [128, 32 * 1024], BF16)
            p.wait("scalar", "qS", mt_q)
            p.dma("scalar", "qA", tap.ap()[:, :], MT_t.ap()[:, :])
            p.wait("tensor", "qA", p.cnt["qA"])
        if b_ == 0 and "toep0" in dbg:
            tap2 = dbgout("toep0", [128, 4096], BF16)
            p.wait("scalar", "qG", toep_q[0])
            p.dma("scalar", "qA", tap2.ap()[:, :], bass.AP(toep_t[0], 0, [[4096, 128], [1, 4096]]))
            p.wait("tensor", "qA", p.cnt["qA"])
        p.wait("tensor", "qS", mt_q)
        for g in range(CGRP):
            gg = b_ * CGRP + g
            p.wait("tensor", "qG", toep_q[gg])
            for c_l in range(64):
                c = g * 64 + c_l
                bk = c_l // 16
                prevkey = (b_, g - 1, bk) if g else (b_ - 1, CGRP - 1, bk)
                if c_l % 16 == 0 and prevkey in ybuf_free:
                    p.wait("tensor", "vE", ybuf_free[prevkey])
                out = pb[bk].ap()[0:64, (c_l % 16) * 32:(c_l % 16) * 32 + 32]
                rhs = bass.AP(MT_t, c, [[32768, 128], [1024, 32]])
                tv = p.mm(out, toep_t[gg % 2].ap()[:, c_l * 64:(c_l + 1) * 64], rhs,
                          start=(c_l % 16 == 0), stop=True, tick=(c_l % 16 == 15))
                if c_l % 16 == 15:
                    conv_mm_tick[(b_, g, bk)] = tv
            if gg + 2 < 2 * CGRP:
                p.wait("gpsimd", "tE", conv_mm_tick[(b_, g, 3)])
                toep_q[gg + 2] = load_toep(gg + 2)
            for bk in range(4):
                cbase = g * 64 + bk * 16
                half = cbase // 512
                c2 = cbase % 512
                p.wait("vector", "tE", conv_mm_tick[(b_, g, bk)])
                src = bass.AP(pb[bk], 0, [[512, 64], [32, 16], [1, 32]])
                dstv = bass.AP(ybuf_t, half * 64 * 16384 + c2, [[16384, 64], [1, 16], [512, 32]])
                vv = p.tick("vector", "vE", lambda v, o=dstv, i=src: v.tensor_copy(o, i))
                ybuf_free[(b_, g, bk)] = vv
        mt_free_tick = conv_mm_tick[(b_, CGRP - 1, 3)]
        for g in range(CGRP):
            for bk in range(4):
                p.wait("sync", "vE", ybuf_free[(b_, g, bk)])
        for half in range(2):
            dst = bass.AP(y_hbm, b_ * Lq * D + half * 512, [[D, 64], [64 * D, 32], [1, 512]])
            src = bass.AP(ybuf_t, half * 64 * 16384, [[16384, 64], [512, 32], [1, 512]])
            y_store_q = p.dma("sync", "qS", dst, src)
    PCONV_Q = p.cnt["qS"]

    if "y" in dbg:
        tap = dbgout("y", [TPC, D], BF16)
        p.wait("scalar", "qS", PCONV_Q)
        p.dma("scalar", "qA", tap.ap()[:, :], y_hbm.ap()[:, :])

    yT_sb = at("yT_sb", 0, [128, 8 * TPC], BF16).ap()[:, :]
    outw_sb = at("outw_sb", 98304, [128, 8 * 1024], BF16).ap()[:, :]
    srow_sb = at("srow_sb", 81920, [128, 1024], BF16).ap()[:, :]
    p.wait("gpsimd", "qS", PCONV_Q)
    for cc in range(8):
        p.dma("gpsimd", "qG", outw_sb[:, cc * 1024:(cc + 1) * 1024], outw.ap()[cc * 128:(cc + 1) * 128, :])
    OUTW_Q = p.cnt["qG"]
    p.wait("sync", "tE", mt_free_tick)
    p.wait("sync", "qS", PCONV_Q)
    for cc in range(8):
        p.dmaT("sync", "qS", yT_sb[:, cc * TPC:(cc + 1) * TPC],
               y_hbm.ap()[:, cc * 128:(cc + 1) * 128])
    YT_Q = p.cnt["qS"]
    p.wait("tensor", "qS", YT_Q)
    p.wait("tensor", "qG", OUTW_Q)
    p.wait("tensor", "qG", CONST_G)
    s_evac_tick = {}
    s_q = {}
    for tt in range(NTILE):
        for hf in range(2):
            bank = pb[4 + hf]
            if (tt - 1, hf) in s_evac_tick:
                p.wait("tensor", "sE", s_evac_tick[(tt - 1, hf)])
            for cc in range(8):
                p.mm(bank.ap()[:, :],
                     yT_sb[:, cc * TPC + tt * 128: cc * TPC + (tt + 1) * 128],
                     outw_sb[:, cc * 1024 + hf * 512: cc * 1024 + (hf + 1) * 512],
                     start=(cc == 0), stop=False)
            tv = p.mm(bank.ap()[:, :], onesrow_sb.ap()[:, :],
                      outbrow_sb.ap()[:, hf * 512:(hf + 1) * 512],
                      start=False, stop=True, tick=True)
            p.wait("scalar", "tE", tv)
            if (tt - 1, hf) in s_q:
                p.wait("scalar", "qS", s_q[(tt - 1, hf)])
            sv = p.tick("scalar", "sE", lambda s, o=srow_sb[:, hf * 512:(hf + 1) * 512],
                        i=bank.ap()[:, :]: s.activation(out=o, in_=i, func=AF.Identity,
                                                        bias=0.0, scale=1.0))
            s_evac_tick[(tt, hf)] = sv
            p.wait("sync", "sE", sv)
            s_q[(tt, hf)] = p.dma("sync", "qS",
                                  s_shard.ap()[tt * 128:(tt + 1) * 128, hf * 512:(hf + 1) * 512],
                                  srow_sb[:, hf * 512:(hf + 1) * 512])
    POUT_Q = p.cnt["qS"]
    p.wait("gpsimd", "qS", POUT_Q)
    AG5 = p.inc("cc", 1)
    p.run("gpsimd", lambda g: g.collective_compute(
        "AllGather", AOP.bypass, replica_groups=RG,
        ins=[s_shard.ap().opt()], outs=[s_all.ap().opt()]).then_inc(p.sems["cc"], 1))

    if "s" in dbg:
        tap = dbgout("s", [TPC, D], BF16)
        p.wait("scalar", "qS", POUT_Q)
        p.dma("scalar", "qA", tap.ap()[:, :], s_shard.ap()[:, :])

    POUT_T = p.cnt["tE"]
    xnm = at("xnm", 0, [128, 8 * TPC], BF16)
    sT_sb = at("sT", 65536, [64, TPC], F32).ap()[:, :]
    sbT_sb = at("sbT", 81920, [64, TPC], F32).ap()[:, :]
    Itmf = at("Itmf", 161408, [128, NTILE * E], F32).ap()[:, :]
    Itmb = at("Itmb", 169600, [128, NTILE * E], BF16).ap()[:, :]
    Wtm = at("Wtm", 173696, [128, NTILE * E], F32).ap()[:, :]
    list6 = at("list6", 118784, [128, NTILE * K], I16)
    wq6 = at("wq6", 119808, [128, NTILE * K], I16)
    ls_idx = at("ls_idx", 124928, [128, NTILE * E], I16).ap()[:, :]
    ls_slot = at("ls_slot", 129024, [128, NTILE * E], I16).ap()[:, :]
    ls_wq = at("ls_wq", 133120, [128, NTILE * E], I16).ap()[:, :]
    fidx = at("fidx", 137216, [128, NTILE * K], I16).ap()[:, :]
    fval = at("fval", 138240, [128, NTILE * K * 2], I16).ap()[:, :]
    segpos_a = at("segpos", 140288, [128, NTILE * K], F32).ap()[:, :]
    sc2 = at("sc2", 143360, [128, 64 * 10], F32).ap()[:, :]
    o8_sb = at("o8", 145920, [128, 8], F32).ap()[:, :]
    ebb_sb = at("ebb", 146176, [128, 64], F32).ap()[:, :]
    e384b_sb = at("e384b", 146432, [128, 64], F32).ap()[:, :]
    cntv_sb = at("cntv", 146688, [64, 32], F32).ap()[:, :]
    goff_sb = at("goff", 146944, [128, 2 * 64], F32).ap()[:, :]
    myoffR = at("myoffR", 147456, [1, NTILE * E], F32).ap()[:, :]
    cumrow = at("cumrow", 155648, [1, 8], F32).ap()[:, :]
    rowtmp = at("rowtmp", 155712, [32, 64], F32).ap()[:, :]
    lidp1 = at("lidp1", 156160, [128, NTILE], I16).ap()[:, :]
    cntj_f = at("cntj_f", 156288, [1, 8], F32).ap()[:, :]
    cntj_ii = at("cntj_ii", 156352, [1, 8], I32).ap()[:, :]
    twc_sb = at("twc", 157696, [128, 2 * (TAB_ROWS // 128)], F32).ap()[:, :]
    myTW = at("myTW", 161408, [128, 24 * 64], F32).ap()[:, :]
    mytok_i = at("mytok_i", 159552, [128, 24], I16).ap()[:, :]
    xg_idx = at("xg_idx", 159616, [128, SPC // 16], I16).ap()[:, :]

    p.wait("gpsimd", "cc", AG1)
    p.wait("gpsimd", "tE", POUT_T)
    p.wait("gpsimd", "qS", POUT_Q)
    p.wait("vector", "tE", POUT_T)
    p.wait("vector", "qS", POUT_Q)
    p.wait("scalar", "qS", POUT_Q)
    XG1 = p.gatherT(xnm, 8 * TPC, xn_all.ap(), moegi_sb.ap(), TPC, elem=D, chunk=256)

    rT_sb = at("rT_sb", 159872, [128, 8 * 64], BF16).ap()[:, :]
    for dc in range(8):
        p.dma("gpsimd", "qG", rT_sb[:, dc * 64:(dc + 1) * 64], rT.ap()[dc * 128:(dc + 1) * 128, :])
    RT_Q = p.cnt["qG"]
    p.wait("tensor", "qG", RT_Q)
    p.wait("tensor", "qG", XG1)
    p.wait("tensor", "sE", p.cnt["sE"])
    for sl in range(16):
        for dc in range(8):
            tv = p.mm(pb[0].ap()[0:64, 0:256], rT_sb[:, dc * 64:(dc + 1) * 64],
                      bass.AP(xnm, sl * 2048 + dc * 256, [[8 * TPC, 128], [1, 256]]),
                      start=(dc == 0), stop=(dc == 7), tick=(dc == 7))
        p.wait("scalar", "tE", tv)
        sv = p.tick("scalar", "sE", lambda s, o=sT_sb[:, sl * 256:(sl + 1) * 256],
                    i=pb[0].ap()[0:64, 0:256]: s.activation(out=o, in_=i, func=AF.Sigmoid,
                                                            bias=0.0, scale=1.0))
        p.wait("tensor", "sE", sv)
    p.wait("vector", "sE", p.cnt["sE"])
    vv = p.tick("vector", "vE", lambda v: v.tensor_scalar_add(sbT_sb, sT_sb, ebcol_sb.ap()[:, :]))
    SBT_V = vv

    p.wait("tensor", "vE", SBT_V)
    tv = p.mm(pb[1].ap()[:, 0:64], onesrowf_sb.ap()[:, :], ebrow_sb.ap()[:, :],
              start=True, stop=True, tick=True)
    p.wait("vector", "tE", tv)
    p.tick("vector", "vE", lambda v: v.tensor_copy(ebb_sb, pb[1].ap()[:, 0:64]))
    tv = p.mm(pb[1].ap()[:, 64:128], onesrowf_sb.ap()[:, :], e384_sb.ap()[:, :],
              start=False, stop=True, tick=True)
    p.wait("vector", "tE", tv)
    vv = p.tick("vector", "vE", lambda v: v.tensor_copy(e384b_sb, pb[1].ap()[:, 64:128]))
    vv = p.tick("vector", "vE", lambda v: v.tensor_scalar(lidp1, lid_sb.ap()[:, :],
                                                          1.0, None, AOP.add))
    ONETIME_V = vv

    sc = lambda i: sc2[:, i * 64:(i + 1) * 64]
    cnt_mm = {}
    for tt in range(NTILE):
        p.wait("tensor", "vE", SBT_V)
        tv = p.tick("tensor", "tE", lambda t, i=sbT_sb[:, tt * 128:(tt + 1) * 128]:
                    t.transpose(pb[2].ap()[:, 0:64], i, identf_sb.ap()[0:64, 0:64]))
        p.wait("vector", "tE", tv)
        vv = p.tick("vector", "vE", lambda v: v.tensor_copy(sc(0), pb[2].ap()[:, 0:64]))
        p.wait("vector", "vE", vv)
        vv = p.tick("vector", "vE", lambda v: v.max(out=o8_sb, in_=sc(0)))
        p.wait("vector", "vE", vv)
        vv = p.tick("vector", "vE", lambda v: v.tensor_single_scalar(sc(1), sc(0), o8_sb[:, 5:6], AOP.is_ge))
        p.wait("vector", "vE", vv)
        p.tick("vector", "vE", lambda v, o=Itmf[:, tt * 64:(tt + 1) * 64]: v.tensor_copy(o, sc(1)))
        p.tick("vector", "vE", lambda v, o=Itmb[:, tt * 64:(tt + 1) * 64]: v.tensor_copy(o, sc(1)))
        p.wait("vector", "vE", ONETIME_V)
        p.tick("vector", "vE", lambda v: v.tensor_sub(sc(2), sc(0), ebb_sb))
        vv = p.tick("vector", "vE", lambda v: v.tensor_mul(sc(3), sc(1), sc(2)))
        p.wait("vector", "vE", vv)
        vv = p.tick("vector", "vE", lambda v: v.reduce_sum(sc(4)[:, 0:1], sc(3), axis=mybir.AxisListType.X))
        p.wait("vector", "vE", vv)
        vv = p.tick("vector", "vE", lambda v: v.tensor_scalar_add(sc(4)[:, 1:2], sc(4)[:, 0:1], 1e-20))
        p.wait("vector", "vE", vv)
        vv = p.tick("vector", "vE", lambda v: v.reciprocal(sc(4)[:, 2:3], sc(4)[:, 1:2]))
        p.wait("vector", "vE", vv)
        vv = p.tick("vector", "vE", lambda v, o=Wtm[:, tt * 64:(tt + 1) * 64]:
                    v.tensor_scalar_mul(o, sc(3), sc(4)[:, 2:3]))
        p.wait("tensor", "vE", p.cnt["vE"])
        tv = p.mm(pb[3].ap()[0:64, tt:tt + 1], Itmb[:, tt * 64:(tt + 1) * 64],
                  onescolb_sb.ap()[:, :], start=(tt == 0), stop=True, tick=True)
        cnt_mm[tt] = tv
    p.wait("vector", "tE", cnt_mm[NTILE - 1])
    vv = p.tick("vector", "vE", lambda v: v.tensor_copy(cntv_sb, pb[3].ap()[0:64, 0:32]))
    p.wait("sync", "vE", vv)
    CQ = p.dma("sync", "qS", bass.AP(cnt_shard, 0, [[1, 64], [64, 32]]), cntv_sb, allow_slow_non_contiguous=True)
    p.wait("gpsimd", "qS", CQ)
    AG2 = p.inc("cc", 1)
    p.run("gpsimd", lambda g: g.collective_compute(
        "AllGather", AOP.bypass, replica_groups=RG,
        ins=[cnt_shard.ap().opt()], outs=[cnt_all.ap().opt()]).then_inc(p.sems["cc"], 1))

    cntg = at("cntg", 160896, [128, 2 * 64], F32).ap()[:, :]
    p.wait("sync", "cc", AG2)
    for ch in range(2):
        p.dma("sync", "qS", cntg[:, ch * 64:(ch + 1) * 64],
              bass.AP(cnt_all, ch * 1024, [[64, 16], [2048, 8], [1, 64]]))
    GLD = p.cnt["qS"]
    if "cntg" in dbg:
        tapc = dbgout("cntg", [128, 128], F32)
        p.wait("scalar", "qS", GLD)
        p.dma("scalar", "qA", tapc.ap()[:, :], cntg)
    p.wait("tensor", "qS", GLD)
    tv = p.mm(pb[2].ap()[:, 0:64], lt128f_sb.ap()[:, :], cntg[:, 0:64],
              start=True, stop=True, tick=True)
    tv = p.mm(pb[2].ap()[:, 64:128], onesf_sb.ap()[:, :], cntg[:, 0:64],
              start=False, stop=True, tick=True)
    tv = p.mm(pb[2].ap()[:, 64:128], lt128f_sb.ap()[:, :], cntg[:, 64:128],
              start=False, stop=True, tick=True)
    p.wait("vector", "tE", tv)
    vv = p.tick("vector", "vE", lambda v: v.tensor_copy(goff_sb, pb[2].ap()[:, 0:128]))
    p.wait("tensor", "vE", vv)
    tv = p.mm(pb[3].ap()[0:64, 32:64], goff_sb[:, 0:64], selmat_sb.ap()[:, 0:NTILE],
              start=True, stop=False, tick=False)
    tv = p.mm(pb[3].ap()[0:64, 32:64], goff_sb[:, 64:128], selmat_sb.ap()[:, NTILE:2 * NTILE],
              start=False, stop=True, tick=True)
    p.wait("vector", "tE", tv)
    vv = p.tick("vector", "vE", lambda v: v.tensor_copy(sc(5)[0:64, 0:32], pb[3].ap()[0:64, 32:64]))
    p.wait("tensor", "vE", vv)
    tv = p.tick("tensor", "tE", lambda t: t.transpose(pb[2].ap()[0:32, 0:64],
                                                      sc(5)[0:64, 0:32], identf_sb.ap()[0:64, 0:64]))
    p.wait("vector", "tE", tv)
    vv = p.tick("vector", "vE", lambda v: v.tensor_copy(rowtmp, pb[2].ap()[0:32, 0:64]))
    p.wait("sync", "vE", vv)
    BQ = p.dma("sync", "qS", bass.AP(boun_f, 0, [[64, 32], [1, 64]]), rowtmp)
    p.wait("sync", "qS", BQ)
    BQ2 = p.dma("sync", "qS", myoffR, bass.AP(boun_f, 0, [[2048, 1], [1, 2048]]))

    lsv = 0
    rank_copy = 0
    for tt in range(NTILE):
        p.wait("tensor", "qS", BQ2)
        if rank_copy:
            p.wait("tensor", "vE", rank_copy)
        tv = p.mm(pb[2].ap()[:, 0:64], lt128f_sb.ap()[:, :],
                  Itmf[:, tt * 64:(tt + 1) * 64], start=True, stop=False)
        tv = p.mm(pb[2].ap()[:, 0:64], onesrowf_sb.ap()[:, :],
                  myoffR[:, tt * 64:(tt + 1) * 64], start=False, stop=True, tick=True)
        p.wait("vector", "tE", tv)
        vv = p.tick("vector", "vE", lambda v: v.tensor_copy(sc(6), pb[2].ap()[:, 0:64]))
        rank_copy = vv
        p.wait("vector", "vE", vv)
        vv = p.tick("vector", "vE", lambda v: v.tensor_single_scalar(sc(7), sc(6), float(CAP), AOP.is_lt))
        p.wait("vector", "vE", vv)
        vv = p.tick("vector", "vE", lambda v, i=Itmf[:, tt * 64:(tt + 1) * 64]:
                    v.tensor_mul(sc(7), sc(7), i))
        p.tick("vector", "vE", lambda v: v.tensor_add(sc(8), sc(6), e384b_sb))
        vv = p.tick("vector", "vE", lambda v, i=Itmf[:, tt * 64:(tt + 1) * 64]: v.tensor_tensor_scan(
            sc(9), i, zero_sb.ap()[:, 0:64], 0.0, AOP.add, AOP.add))
        p.wait("vector", "vE", vv)
        vv = p.tick("vector", "vE", lambda v, i=Itmf[:, tt * 64:(tt + 1) * 64]:
                    v.tensor_sub(sc(9), sc(9), i))
        p.wait("vector", "vE", vv)
        vv = p.tick("vector", "vE", lambda v, c_=float(6 * tt + 1): v.tensor_scalar(
            sc(9), sc(9), c_, None, AOP.add))
        p.wait("vector", "vE", vv)
        vv = p.tick("vector", "vE", lambda v: v.tensor_mul(sc(9), sc(9), sc(7)))
        p.wait("vector", "vE", vv)
        vv = p.tick("vector", "vE", lambda v, o=ls_idx[:, tt * 64:(tt + 1) * 64]:
                    v.tensor_scalar(o, sc(9), 1.0, None, AOP.subtract))
        p.tick("vector", "vE", lambda v, o=ls_slot[:, tt * 64:(tt + 1) * 64]:
               v.tensor_scalar(o, sc(8), 1.0, None, AOP.add))
        vv = p.tick("vector", "vE", lambda v, o=ls_wq[:, tt * 64:(tt + 1) * 64],
                    i=Wtm[:, tt * 64:(tt + 1) * 64]:
                    v.tensor_scalar(o, i, 32000.0, None, AOP.mult))
        lsv = vv
    p.wait("gpsimd", "vE", lsv)
    g1 = p.tick("gpsimd", "gE", lambda g: g.local_scatter(
        out_ap=list6.ap()[:, :], data_ap=ls_slot, idxs_ap=ls_idx,
        channels=128, num_elems=NTILE * K, num_idxs=NTILE * E))
    g2 = p.tick("gpsimd", "gE", lambda g: g.local_scatter(
        out_ap=wq6.ap()[:, :], data_ap=ls_wq, idxs_ap=ls_idx,
        channels=128, num_elems=NTILE * K, num_idxs=NTILE * E))

    if "route" in dbg:
        tapr = dbgout("route", [128, NTILE * K * 2], I16)
        p.wait("scalar", "gE", g2)
        p.dma("scalar", "qA", tapr.ap()[:, 0:NTILE * K], list6.ap()[:, :])
        p.dma("scalar", "qA", tapr.ap()[:, NTILE * K:], wq6.ap()[:, :])

    indf = at("indf", 65536, [128, NTILE * K], F32).ap()[:, :]
    l6f2 = at("l6f2", 66368, [128, NTILE * K], F32).ap()[:, :]
    wqf = at("wqf", 67200, [128, NTILE * K], F32).ap()[:, :]
    j4b = at("j4b", 68032, [128, 8], F32).ap()[:, :]
    tabval_t = at("tabval", 68096, [128, NTILE * K * 2], F32)
    tidx_str = at("tidx_str", 72832, [128, NSEL // 16], I16).ap()[:, :]
    zcomb_str = sbuf("zcomb_str", [128, NSEL // 16], I16).ap()[:, :]
    tabidx = at("tabidx", 75904, [128, NTILE * K], I16)
    zcf = at("zcf", 137216, [128, NTILE * K], F32).ap()[:, :]
    tbf = at("tbf", 138240, [128, NTILE * K], F32).ap()[:, :]
    mytf = at("mytf", 139264, [128, 24], F32).ap()[:, :]
    str16f_t = at("str16f", 147456, [16, NSEL // 16], F32)

    p.wait("vector", "gE", g2)
    vv = p.tick("vector", "vE", lambda v: v.tensor_copy(l6f2, list6.ap()[:, :]))
    p.wait("vector", "vE", vv)
    vv = p.tick("vector", "vE", lambda v: v.tensor_single_scalar(indf, l6f2, 0.5, AOP.is_gt))
    p.tick("vector", "vE", lambda v: v.tensor_copy(wqf, wq6.ap()[:, :]))
    vv = p.tick("vector", "vE", lambda v: v.tensor_scalar_mul(wqf, wqf, 1.0 / 32000.0))
    IND_V = vv

    p.wait("gpsimd", "vE", V_ZERO)
    off = 0
    while off < TAB_ROWS * 64:
        n = min(262144, TAB_ROWS * 64 - off)
        p.dma("gpsimd", "qG", bass.AP(tabTW, off, [[n // 128, 128], [1, n // 128]]), zero_sb.ap()[:, 0:n // 128])
        off += n
    p.dma("gpsimd", "qG", z_shard.ap()[SPC:ZROWS, :], zero_sb.ap()[:, 0:1024])
    PREZ_Q = p.cnt["qG"]

    p.wait("vector", "vE", IND_V)
    p.tick("vector", "vE", lambda v: v.tensor_scalar(sc2[:, 0:192], l6f2, 1.0, None, AOP.subtract))
    p.wait("vector", "vE", p.cnt["vE"])
    p.tick("vector", "vE", lambda v: v.tensor_sub(sc2[:, 0:192], sc2[:, 0:192], trashp_sb.ap()[:, :]))
    p.wait("vector", "vE", p.cnt["vE"])
    p.tick("vector", "vE", lambda v: v.tensor_mul(sc2[:, 0:192], sc2[:, 0:192], indf))
    p.wait("vector", "vE", p.cnt["vE"])
    p.tick("vector", "vE", lambda v: v.tensor_add(tbf, sc2[:, 0:192], trashp_sb.ap()[:, :]))
    for tt in range(NTILE):
        p.tick("vector", "vE", lambda v, o=bass.AP(tabval_t, tt * 12, [[NSEL * 2 // 128, 128], [2, 6]]),
               i=gid_sb.ap()[:, tt:tt + 1].to_broadcast([128, 6]): v.tensor_copy(o, i))
    tv2 = p.tick("vector", "vE", lambda v: v.tensor_copy(
        bass.AP(tabval_t, 1, [[NSEL * 2 // 128, 128], [2, NSEL // 128]]), wqf))
    p.wait("vector", "vE", p.cnt["vE"])
    p.tick("vector", "vE", lambda v: v.tensor_scalar(
        sc2[:, 0:192], l6f2, float(ZERO_ROW + 1), None, AOP.subtract))
    p.wait("vector", "vE", p.cnt["vE"])
    p.tick("vector", "vE", lambda v: v.tensor_mul(sc2[:, 0:192], sc2[:, 0:192], indf))
    p.wait("vector", "vE", p.cnt["vE"])
    p.tick("vector", "vE", lambda v: v.tensor_scalar(
        zcf, sc2[:, 0:192], float(ZERO_ROW), None, AOP.add))
    P3VEC = p.cnt["vE"]

    def stripe(natf_ap, ncols, dst_i16_ap, wait_v):
        W8 = ncols * 8
        last = wait_v
        for g16 in range(8):
            p.wait("tensor", "vE", last)
            tv = p.mm(pb[2].ap()[0:16, 0:ncols],
                      identf_sb.ap()[:, g16 * 16:(g16 + 1) * 16], natf_ap,
                      start=True, stop=True, tick=True)
            p.wait("vector", "tE", tv)
            last = p.tick("vector", "vE", lambda v, g16=g16, ncols=ncols: v.tensor_copy(
                bass.AP(str16f_t, g16, [[NSEL // 16, 16], [8, ncols]]), pb[2].ap()[0:16, 0:ncols]))
        out_v = last
        for k0 in range(0, W8, 512):
            kw = min(512, W8 - k0)
            p.wait("tensor", "vE", out_v)
            tv = p.mm(pb[3].ap()[:, 0:kw], rep16_sb.ap()[:, :],
                      bass.AP(str16f_t, k0, [[NSEL // 16, 16], [1, kw]]),
                      start=True, stop=True, tick=True)
            p.wait("vector", "tE", tv)
            out_v = p.tick("vector", "vE", lambda v, k0=k0, kw=kw, d=dst_i16_ap: v.tensor_copy(
                d[:, k0:k0 + kw], pb[3].ap()[:, 0:kw]))
        return out_v

    STP_TB = stripe(tbf, NTILE * K, tidx_str, P3VEC)
    STP_ZC = stripe(zcf, NTILE * K, zcomb_str, P3VEC)

    p.wait("gpsimd", "vE", STP_TB)
    p.wait("gpsimd", "qG", PREZ_Q)
    NCH = NSEL // 1536
    for k in range(NCH):
        p.inc("qG", 16)
        def c2(g, k=k):
            g.dma_scatter_add(
                out_ap=tabTW.ap()[:, 0:2],
                in_ap=bass.AP(tabval_t, k * 24, [[NSEL * 2 // 128, 128], [2, 12], [1, 2]]),
                idxs_ap=tidx_str[:, k * 96:(k + 1) * 96],
                num_idxs=1536, num_idxs_reg=p.swcnt(g, 1536), elem_size=2, elem_step=64,
            ).then_inc(p.sems["qG"], 16)
        p.on("gpsimd", c2)
    SCAT_Q = p.cnt["qG"]

    p.wait("gpsimd", "qG", SCAT_Q)
    tab2 = nc.dram_tensor("tab2", [TAB_ROWS, 64], F32)
    AR3 = p.inc("cc", 1)
    p.run("gpsimd", lambda g: g.collective_compute(
        "AllReduce", AOP.add, replica_groups=RG,
        ins=[tabTW.ap().opt()], outs=[tab2.ap().opt()]).then_inc(p.sems["cc"], 1))
    p.wait("gpsimd", "cc", AR3)
    for k in range(3):
        p.inc("qG", 16)
        def c3(g, k=k):
            g.dma_gather(
                out_ap=bass.AP(myTW.tensor, k * 512, [[1536, 128], [64, 8], [1, 64]]),
                in_ap=tab2.ap(),
                idxs_ap=myslots_sb.ap()[:, k * 64:(k + 1) * 64],
                num_idxs=1024, num_idxs_reg=p.swcnt(g, 1024), elem_size=64,
            ).then_inc(p.sems["qG"], 16)
        p.on("gpsimd", c3)
    MYTW_Q = p.cnt["qG"]
    p.wait("vector", "qG", MYTW_Q)
    vv = p.tick("vector", "vE", lambda v: v.tensor_copy(
        mytf, bass.AP(myTW.tensor, 0, [[1536, 128], [64, 24]])))
    STP_MT = stripe(mytf, 24, xg_idx, vv)

    xsb_t = at("xsb", 65536, [128, 8 * SPC], BF16)
    p.wait("gpsimd", "vE", STP_MT)
    p.wait("gpsimd", "vE", P3VEC)
    p.wait("gpsimd", "qG", SCAT_Q)
    XSQ = p.gatherT(xsb_t, 8 * SPC, xn_all.ap(), xg_idx, SPC, elem=D, chunk=384)

    ew1_sb = [at(f"ew1_{i}", 114688 + i * 8192, [128, 8 * 512], BF16).ap()[:, :] for i in range(2)]
    ew2_sb = [at(f"ew2_{i}", 131072 + i * 8192, [128, 4 * 1024], BF16).ap()[:, :] for i in range(2)]
    hmid = at("hmid", 147456, [128, 4 * 384], BF16).ap()[:, :]
    zstg = [at(f"zstg{i}", 150592 + i * 2048, [128, 1024], BF16).ap()[:, :] for i in range(2)]

    def load_ew(e):
        buf = e % 2
        for dc in range(8):
            p.dma("gpsimd", "qG", ew1_sb[buf][:, dc * 512:(dc + 1) * 512],
                  ew1.ap()[e, dc * 128:(dc + 1) * 128, :])
        for fc in range(4):
            p.dma("gpsimd", "qG", ew2_sb[buf][:, fc * 1024:(fc + 1) * 1024],
                  ew2.ap()[e, fc * 128:(fc + 1) * 128, :])
        return p.cnt["qG"]

    ew_q = {0: load_ew(0)}
    ew_q[1] = load_ew(1)
    p.wait("tensor", "qG", XSQ)
    zq = {}
    ew_done = {}
    for e in range(EPC):
        buf = e % 2
        p.wait("tensor", "qG", ew_q[e])
        for fc in range(4):
            bank = pb[fc].ap()[:, 0:384]
            if e:
                p.wait("tensor", "sE", ew_done[(e - 1, fc)])
            for dc in range(8):
                tv = p.mm(bank, ew1_sb[buf][:, dc * 512 + fc * 128: dc * 512 + (fc + 1) * 128],
                          bass.AP(xsb_t, e * 3072 + dc * 384, [[8 * SPC, 128], [1, 384]]),
                          start=(dc == 0), stop=(dc == 7), tick=(dc == 7))
            p.wait("scalar", "tE", tv)
            sv = p.tick("scalar", "sE", lambda s, o=hmid[:, fc * 384:(fc + 1) * 384], i=bank:
                        s.activation(out=o, in_=i, func=AF.Gelu_apprx_tanh, bias=0.0, scale=1.0))
            ew_done[(e, fc)] = sv
        for sub in range(3):
            bank2 = [pb[4 + 2 * (sub % 2)], pb[5 + 2 * (sub % 2)]]
            p.wait("tensor", "sE", ew_done[(e, 3)])
            if (e, sub - 2) in zq:
                p.wait("tensor", "qS", zq[(e, sub - 2)])
            elif e and sub < 2:
                p.wait("tensor", "qS", zq.get((e - 1, sub + 1), 0))
            for hf in range(2):
                for fc in range(4):
                    tv = p.mm(bank2[hf].ap()[:, :],
                              hmid[:, fc * 384 + sub * 128: fc * 384 + sub * 128 + 128],
                              ew2_sb[buf][:, fc * 1024 + hf * 512: fc * 1024 + (hf + 1) * 512],
                              start=(fc == 0), stop=(fc == 3), tick=(fc == 3))
                p.wait("scalar", "tE", tv)
                wcol = bass.AP(myTW.tensor, 64 * (e * 3 + sub) + 1, [[1536, 128], [1, 1]])
                sv = p.tick("scalar", "sE", lambda s, o=zstg[sub % 2][:, hf * 512:(hf + 1) * 512],
                            i=bank2[hf].ap()[:, :], w=wcol:
                            s.activation(out=o, in_=i, func=AF.Identity, bias=0.0, scale=w))
            p.wait("sync", "sE", sv)
            zq[(e, sub)] = p.dma("sync", "qS",
                                 z_shard.ap()[e * 384 + sub * 128: e * 384 + (sub + 1) * 128, :],
                                 zstg[sub % 2][:, :])
        if e + 2 < EPC:
            p.wait("gpsimd", "tE", p.cnt["tE"])
            ew_q[e + 2] = load_ew(e + 2)
    PEXP_Q = p.cnt["qS"]
    p.wait("gpsimd", "qS", PEXP_Q)
    p.wait("gpsimd", "qG", PREZ_Q)
    AG4 = p.inc("cc", 1)
    p.run("gpsimd", lambda g: g.collective_compute(
        "AllGather", AOP.bypass, replica_groups=RG,
        ins=[z_shard.ap().opt()], outs=[z_all.ap().opt()]).then_inc(p.sems["cc"], 1))

    if "ztab" in dbg:
        tapz = dbgout("ztab", [ZROWS, D], BF16)
        p.wait("scalar", "qS", PEXP_Q)
        p.dma("scalar", "qA", tapz.ap()[:, :], z_shard.ap()[:, :])
        tapt = dbgout("tab", [TAB_ROWS, 2], F32)
        p.wait("scalar", "cc", AR3)
        p.dma("scalar", "qA", tapt.ap()[:, :], tab2.ap()[:, 0:2])

    sw1_sb = at("sw1_sb", 114688, [128, 8 * SF], BF16).ap()[:, :]
    sw2_sb = at("sw2_sb", 147456, [128, 16 * 1024], BF16).ap()[:, :]
    mid_sb = at("mid_sb", 65536, [128, 16 * 256], BF16).ap()[:, :]
    zrows = [at(f"zrows{i}", 73728 + i * 12288, [128, 6 * 1024], BF16) for i in range(2)]
    srows4_t = at("srows4", 98304, [128, 4 * 1024], BF16)
    xrow2 = at("xrow2", 106496, [128, 1024], F32).ap()[:, :]
    orow = at("orow", 110592, [128, 1024], F32).ap()[:, :]

    p.wait("gpsimd", "tE", p.cnt["tE"])
    for dc in range(8):
        p.dma("gpsimd", "qG", sw1_sb[:, dc * SF:(dc + 1) * SF], sw1T.ap()[dc * 128:(dc + 1) * 128, :])
    for hc in range(16):
        p.dma("gpsimd", "qG", sw2_sb[:, hc * 1024:(hc + 1) * 1024], sw2T.ap()[hc * 128:(hc + 1) * 128, :])
    SWQ = p.cnt["qG"]

    CC_ALL = p.cnt["cc"]
    p.wait("gpsimd", "cc", CC_ALL)
    p.wait("gpsimd", "vE", STP_ZC)
    p.wait("gpsimd", "qG", SWQ)

    ZG = {}
    def em_zg(tt):
        if tt >= 2:
            p.wait("gpsimd", "qG", ZG[tt - 1])
        ZG[tt] = p.inc("qG", 16)
        def call(g, tt=tt):
            g.dma_gather(
                out_ap=bass.AP(zrows[tt % 2], 0, [[6 * 1024, 128], [1024, 6], [1, 1024]]),
                in_ap=z_all.ap(),
                idxs_ap=zcomb_str[:, tt * 48:(tt + 1) * 48],
                num_idxs=768, num_idxs_reg=p.swcnt(g, 768), elem_size=D,
            ).then_inc(p.sems["qG"], 16)
        p.on("gpsimd", call)

    SGq = {}
    def em_sg(gg):
        if gg >= 1:
            p.wait("gpsimd", "qG", SGq[gg - 1])
        SGq[gg] = p.inc("qG", 16)
        def call(g, gg=gg):
            g.dma_gather(
                out_ap=bass.AP(srows4_t, 0, [[4 * 1024, 128], [1024, 4], [1, 1024]]),
                in_ap=s_all.ap(),
                idxs_ap=moegi_sb.ap()[:, gg * 32:(gg + 1) * 32],
                num_idxs=512, num_idxs_reg=p.swcnt(g, 512), elem_size=D,
            ).then_inc(p.sems["qG"], 16)
        p.on("gpsimd", call)

    em_zg(0)
    em_zg(1)
    em_sg(0)
    XR = {0: p.dma("sync", "qS", xrow2, x_moe.ap()[0:128, :])}

    p.wait("tensor", "qG", SWQ)
    gq = {}
    ztick = {}
    vops = {}
    oq = {}
    for tl in range(16):
        for hc in range(16):
            bank = pb[hc % 2].ap()[:, 0:256]
            for dc in range(8):
                tv = p.mm(bank, sw1_sb[:, dc * SF + hc * 128: dc * SF + (hc + 1) * 128],
                          bass.AP(xnm, tl * 2048 + dc * 256, [[8 * TPC, 128], [1, 256]]),
                          start=(dc == 0), stop=(dc == 7), tick=(dc == 7))
            p.wait("scalar", "tE", tv)
            sv = p.tick("scalar", "sE", lambda s, o=mid_sb[:, hc * 256:(hc + 1) * 256], i=bank:
                        s.activation(out=o, in_=i, func=AF.Gelu_apprx_tanh, bias=0.0, scale=1.0))
            gq[("mid", tl, hc)] = sv
            p.wait("tensor", "sE", sv)
        p.wait("tensor", "sE", gq[("mid", tl, 15)])
        for sub in range(2):
            tt = tl * 2 + sub
            b2 = [pb[4 + 2 * sub], pb[5 + 2 * sub]]
            if tt >= 2:
                p.wait("tensor", "sE", gq[("sil", tt - 2)])
            for hf in range(2):
                for hc in range(16):
                    p.mm(b2[hf].ap()[:, :],
                         mid_sb[:, hc * 256 + sub * 128: hc * 256 + sub * 128 + 128],
                         sw2_sb[:, hc * 1024 + hf * 512: hc * 1024 + (hf + 1) * 512],
                         start=(hc == 0), stop=False)
            p.wait("tensor", "qG", ZG[tt])
            for hf in range(2):
                for j in range(6):
                    tv = p.mm(b2[hf].ap()[:, :], ident_sb.ap()[:, :],
                              bass.AP(zrows[tt % 2], j * 1024 + hf * 512, [[6 * 1024, 128], [1, 512]]),
                              start=False, stop=(j == 5), tick=(j == 5))
            ztick[tt] = tv
            if tt + 2 < NTILE:
                p.wait("gpsimd", "tE", ztick[tt])
                em_zg(tt + 2)
            if "gdbg" in dbg:
                tapg = dbgout("gdbg", [TPC, D], F32) if tt == 0 else dbg_t["gdbg"]
                p.wait("vector", "tE", ztick[tt])
                if tt >= 1:
                    p.wait("vector", "qA", p.cnt["qA"])
                    p.wait("vector", "qS", oq[tt - 1])
                for hf_ in range(2):
                    gv = p.tick("vector", "vE", lambda v, o=orow[:, hf_ * 512:(hf_ + 1) * 512],
                                i=b2[hf_].ap()[:, :]: v.tensor_copy(o, i))
                p.wait("scalar", "vE", gv)
                p.dma("scalar", "qA", tapg.ap()[tt * 128:(tt + 1) * 128, :], orow)
            p.wait("scalar", "tE", ztick[tt])
            if tt >= 1:
                p.wait("scalar", "vE", vops[tt - 1])
            for hf in range(2):
                sv = p.tick("scalar", "sE", lambda s, i=b2[hf].ap()[:, :], o=pb[2 + hf].ap()[:, :]:
                            s.activation(out=o, in_=i, func=AF.Silu, bias=0.0, scale=1.0))
            gq[("sil", tt)] = sv
            gg, r = tt // 4, tt % 4
            if "gdbg" in dbg:
                p.wait("vector", "qA", p.cnt["qA"])
            p.wait("vector", "sE", sv)
            p.wait("vector", "qG", SGq[gg])
            p.wait("vector", "qS", XR[tt])
            if tt >= 1:
                p.wait("vector", "qS", oq[tt - 1])
            for hf in range(2):
                vv = p.tick("vector", "vE", lambda v, hf=hf, r=r:
                            v.tensor_mul(orow[:, hf * 512:(hf + 1) * 512],
                                         pb[2 + hf].ap()[:, :],
                                         bass.AP(srows4_t, r * 1024 + hf * 512, [[4 * 1024, 128], [1, 512]])))
            vops[tt] = vv
            p.wait("vector", "vE", vv)
            vv = p.tick("vector", "vE", lambda v: v.tensor_add(orow, orow, xrow2))
            p.wait("sync", "vE", vv)
            oq[tt] = p.dma("sync", "qS", out_t.ap()[tt * 128:(tt + 1) * 128, :], orow)
            if tt + 1 < NTILE:
                XR[tt + 1] = p.dma("sync", "qS", xrow2, x_moe.ap()[(tt + 1) * 128:(tt + 2) * 128, :])
            if r == 3 and gg + 1 < 8:
                p.wait("gpsimd", "vE", vops[tt])
                em_sg(gg + 1)
    OUT_Q = p.cnt["qS"]

    if "zc" in dbg:
        tapzc = dbgout("zc", [128, NSEL // 16], I16)
        p.wait("scalar", "qS", OUT_Q)
        p.dma("scalar", "qA", tapzc.ap()[:, :], zcomb_str)
    if "zall" in dbg:
        tapza = dbgout("zall", [NCORES * ZROWS, D], BF16)
        p.wait("scalar", "qS", OUT_Q)
        p.dma("scalar", "qA", tapza.ap()[:, :], z_all.ap()[:, :])
    if "sall" in dbg:
        tapsa = dbgout("sall", [T, D], BF16)
        p.wait("scalar", "qS", OUT_Q)
        p.dma("scalar", "qA", tapsa.ap()[:, :], s_all.ap()[:, :])
    if dbg_t:
        for e_ in ("scalar",):
            p.wait(e_, "qA", p.cnt["qA"])
    p.wait("sync", "qS", OUT_Q)
    p.wait("sync", "qG", p.cnt["qG"])

    with nc.Block() as block:
        p.emit(block)
    _finalize_passes(nc)
    return nc


_NC_CACHE = {}


def kernel(**inputs):
    in_maps = _host_prepare(inputs)
    key = ()
    if key not in _NC_CACHE:
        _NC_CACHE[key] = build()
    nc = _NC_CACHE[key]
    res = run_bass_kernel_spmd(nc, in_maps, core_ids=list(range(NCORES)))
    return _assemble(res.results)


# version 04 (final)
# speedup vs baseline: 2.6799x; 1.0818x over previous
import numpy as np
import ml_dtypes
import concourse.bass as bass
import concourse.mybir as mybir
from concourse.bass_utils import run_bass_kernel_spmd

F32 = mybir.dt.float32
BF16 = mybir.dt.bfloat16
I16 = mybir.dt.int16
I32 = mybir.dt.int32
U8 = mybir.dt.uint8
AF = mybir.ActivationFunctionType
AOP = mybir.AluOpType
BF = ml_dtypes.bfloat16

NCORES = 8
Bb, Lq, D, HID = 16, 2048, 1024, 4096
T = Bb * Lq
TPC = T // NCORES
NTILE = TPC // 128
E, K, CAP = 64, 6, 384
EPC = E // NCORES
SPC = EPC * CAP
ECAP = E * CAP
NPAIR = TPC * E
NSEL = TPC * K
SF = 2048
ZPAD = 128
ZROWS = SPC + ZPAD
ZERO_ROW = SPC
TRASH_T = NCORES * ZROWS + 8
TAB_ROWS = NCORES * ZROWS + 128
FL_ROWS = NSEL + 128
FL_TRASH = NSEL + 8
GTRASH = TPC

RG = [list(range(NCORES))]



def _toeplitz_stack(ker):
    p = np.arange(128)[:, None]
    j = np.arange(64)[None, :]
    kk = j - p + 64
    mask = (kk >= 0) & (kk < 64)
    kkc = np.clip(kk, 0, 63)
    S = ker[:, kkc] * mask[None, :, :]
    T = S.reshape(16, 64, 128, 64).transpose(2, 0, 1, 3).reshape(128, 16 * 4096)
    return np.ascontiguousarray(T.astype(np.float32))


def _fm_bias(b):
    n = b.shape[0] // 128
    return np.ascontiguousarray(b.reshape(n, 128).T.astype(np.float32))


def _striped_i16(ids, n):
    a = np.zeros((128, n // 16), np.int16)
    seq = np.asarray(ids, np.int64)
    for g in range(8):
        a[g * 16:(g + 1) * 16, :] = seq.reshape(n // 16, 16).T
    return a


def _host_prepare(inputs):
    x = np.ascontiguousarray(np.asarray(inputs["L0_attn_x"], np.float32).reshape(T, D))
    ker = np.asarray(inputs["L0_attn_kernel"], np.float32)
    toep = _toeplitz_stack(ker)
    sw1T = np.ascontiguousarray(np.asarray(inputs["L0_ffn_sw1"], np.float32).T)
    sw2T = np.ascontiguousarray(np.asarray(inputs["L0_ffn_sw2"], np.float32).T)
    rT = np.ascontiguousarray(np.asarray(inputs["L0_ffn_router_weight"], np.float32).T)
    w1 = np.ascontiguousarray(np.asarray(inputs["L0_attn_mod_w1"], np.float32))
    w2 = np.ascontiguousarray(np.asarray(inputs["L0_attn_mod_w2"], np.float32))
    inw = np.ascontiguousarray(np.asarray(inputs["L0_attn_in_w"], np.float32))
    outw = np.ascontiguousarray(np.asarray(inputs["L0_attn_out_w"], np.float32))
    ew1 = np.asarray(inputs["L0_ffn_w1"], np.float32)
    ew2 = np.asarray(inputs["L0_ffn_w2"], np.float32)
    ebias = np.asarray(inputs["L0_ffn_expert_bias"], np.float32)

    inb = _fm_bias(np.asarray(inputs["L0_attn_in_b"], np.float32))
    b1 = _fm_bias(np.asarray(inputs["L0_attn_mod_b1"], np.float32))
    b2 = _fm_bias(np.asarray(inputs["L0_attn_mod_b2"], np.float32))
    outb_row = np.ascontiguousarray(np.asarray(inputs["L0_attn_out_b"], np.float32)[None, :])

    ident = np.eye(128, dtype=BF)
    rep16 = np.tile(np.eye(16, dtype=np.float32), (1, 8))
    trashpat = (NCORES * ZROWS + (np.arange(128)[:, None] * NTILE * K
                + np.arange(NTILE * K)[None, :]) % 128).astype(np.float32)
    identf = np.eye(128, dtype=np.float32)
    lt128 = np.triu(np.ones((128, 128), BF), 1)
    onesrow = np.ones((1, 128), BF)
    onesrowf = np.ones((1, 128), np.float32)
    lt128f = np.triu(np.ones((128, 128), np.float32), 1)
    onesf = np.ones((128, 128), np.float32)
    onescolb = np.ones((128, 1), BF)
    ones64 = np.ones((128, 64), np.float32)
    ee = np.arange(64)
    e384row = (384.0 * ee + 128.0 * (ee // 8)).astype(np.float32)[None, :]
    ebias_row = ebias[None, :].astype(np.float32)
    ebias_col = ebias[:, None].astype(np.float32)

    xg = x.reshape(256, 128, D)
    in_maps = []
    for c in range(NCORES):
        gids = (np.arange(NTILE) * 8 + c) * 128
        my_blocks = np.arange(NTILE) * 8 + c
        x_moe = np.ascontiguousarray(xg[my_blocks].reshape(TPC, D))
        gid = np.ascontiguousarray(
            (gids[None, :] + np.arange(128)[:, None]).astype(np.float32))
        lid = np.ascontiguousarray(
            (np.arange(NTILE)[None, :] * 128 + np.arange(128)[:, None]).astype(np.float32))
        moeg_idx = _striped_i16(
            (gids[:, None] + np.arange(128)[None, :]).reshape(-1), TPC)
        sel = np.zeros((256, NTILE), np.float32)
        sel[my_blocks, np.arange(NTILE)] = 1.0
        selmat = np.ascontiguousarray(sel.reshape(2, 128, NTILE))
        myslots = _striped_i16(np.arange(SPC) + c * ZROWS, SPC)
        m = {
            "x_mix": np.ascontiguousarray(x[c * TPC:(c + 1) * TPC]),
            "x_moe": x_moe,
            "inw": inw, "inb": inb, "w1": w1, "b1": b1, "w2": w2, "b2": b2,
            "outw": outw, "outb_row": outb_row,
            "toep": toep,
            "rT": rT, "ebias_row": ebias_row, "ebias_col": ebias_col,
            "ew1": np.ascontiguousarray(ew1[c * EPC:(c + 1) * EPC]),
            "ew2": np.ascontiguousarray(ew2[c * EPC:(c + 1) * EPC]),
            "sw1T": sw1T, "sw2T": sw2T,
            "ident": ident, "identf": identf, "lt128": lt128, "onesrow": onesrow,
            "onesrowf": onesrowf, "lt128f": lt128f, "onesf": onesf, "onescolb": onescolb,
            "ones64": ones64, "e384row": e384row,
            "gid": gid, "lid": lid, "moeg_idx": moeg_idx, "rep16": rep16, "trashpat": trashpat,
            "selmat": selmat, "myslots": myslots,
        }
        in_maps.append(m)
    return in_maps


def _assemble(results):
    out = np.empty((256, 128, D), np.float32)
    for c in range(NCORES):
        blocks = np.arange(NTILE) * 8 + c
        out[blocks] = results[c]["out"].reshape(NTILE, 128, D)
    return out.reshape(Bb, Lq, D)



ENGS = ("tensor", "vector", "scalar", "sync", "gpsimd")


class Prog:
    def __init__(self, nc):
        self.nc = nc
        self.ops = {e: [] for e in ENGS}
        self.sems = {}
        self.cnt = {}

    def sem(self, name):
        h = self.nc.alloc_semaphore(name)
        self.sems[name] = h
        self.cnt[name] = 0
        return h

    def inc(self, name, by):
        self.cnt[name] += by
        return self.cnt[name]

    def on(self, eng, fn):
        self.ops[eng].append(fn)

    def wait(self, eng, sem_name, val):
        if val <= 0:
            return
        k = (eng, sem_name)
        lw = getattr(self, "last_wait", None)
        if lw is None:
            self.last_wait = lw = {}
        if lw.get(k, 0) >= val:
            return
        lw[k] = val
        s = self.sems[sem_name]
        self.on(eng, lambda e, s=s, v=val: e.wait_ge(s, v))

    def dma(self, eng, sem_name, out, in_, **kw):
        v = self.inc(sem_name, 16)
        s = self.sems[sem_name]
        self.on(eng, lambda e, o=out, i=in_, s=s, kw=kw: e.dma_start(out=o, in_=i, **kw).then_inc(s, 16))
        return v

    def dmaT(self, eng, sem_name, out, in_):
        v = self.inc(sem_name, 16)
        s = self.sems[sem_name]
        self.on(eng, lambda e, o=out, i=in_, s=s: e.dma_start(out=o, in_=i, transpose=True).then_inc(s, 16))
        return v

    def tick(self, eng, sem_name, fn):
        v = self.inc(sem_name, 1)
        s = self.sems[sem_name]
        def wrapped(e, fn=fn, s=s):
            ins = fn(e)
            ins.then_inc(s, 1)
        self.on(eng, wrapped)
        return v

    def swcnt(self, g, val):
        if getattr(self, "_swreg", None) is None:
            self._swreg = g.alloc_register("swcnt")
        g.reg_mov(self._swreg, val)
        return self._swreg

    def gatherT(self, out_handle, total_cols, src_ap, idx_ap, n, elem=1024, chunk=256):
        nch = elem // 128
        for k in range(n // chunk):
            self.inc("qG", 16)
            def call(g, k=k, out_handle=out_handle, total_cols=total_cols, src_ap=src_ap,
                     idx_ap=idx_ap, elem=elem, chunk=chunk, nch=nch):
                o = bass.AP(out_handle, k * nch * chunk,
                            [[total_cols, 128], [chunk, nch], [1, chunk]])
                g.dma_gather(out_ap=o, in_ap=src_ap,
                             idxs_ap=idx_ap[:, k * (chunk // 16):(k + 1) * (chunk // 16)],
                             num_idxs=chunk, num_idxs_reg=self.swcnt(g, chunk), elem_size=elem,
                             transpose=True).then_inc(self.sems["qG"], 16)
            self.on("gpsimd", call)
        return self.cnt["qG"]

    def mm(self, out, lhsT, rhs, start, stop, tick=False):
        if tick:
            return self.tick("tensor", "tE", lambda t, o=out, l=lhsT, r=rhs, a=start, z=stop:
                             t.matmul(o, l, r, start=a, stop=z, skip_group_check=True))
        self.on("tensor", lambda t, o=out, l=lhsT, r=rhs, a=start, z=stop:
                t.matmul(o, l, r, start=a, stop=z, skip_group_check=True))
        return None

    def run(self, eng, fn):
        self.on(eng, fn)

    def barrier(self, producers):
        for e in ENGS:
            for sn, v in producers:
                self.wait(e, sn, v)

    def emit(self, block):
        for ename in ENGS:
            getattr(block, ename)(self._mk_runner(ename))

    def _mk_runner(self, ename):
        ops = self.ops[ename]
        def runner(eng):
            for fn in ops:
                fn(eng)
        return runner


def _finalize_passes(nc):
    from concourse.library_config import all_libraries, standard
    m = {}
    for lib in all_libraries:
        for it in lib.instructions:
            m[it] = m.get(it, 0) | (1 << lib.index)
    import bass_rust as _br
    _br.insert_library_loads(nc, m, len(all_libraries), standard.index)
    _br.insert_act_table_loads(nc)
    mybir.codegen_inst_isa_subclasses(nc)



def build(debug=()):
    nc = bass.Bass("TRN2", target_bir_lowering=False, detect_race_conditions=False, enable_asserts=False)
    dbg = set(debug)

    di = {}
    def inp(name, shape, dtype=F32):
        di[name] = nc.dram_tensor(name, shape, dtype, kind="ExternalInput")
        return di[name]

    x_mix = inp("x_mix", [TPC, D]); x_moe = inp("x_moe", [TPC, D])
    inw = inp("inw", [D, D]); inb = inp("inb", [128, 8])
    w1 = inp("w1", [D, HID]); b1 = inp("b1", [128, 32])
    w2 = inp("w2", [HID, D]); b2 = inp("b2", [128, 8])
    outw = inp("outw", [D, D]); outb_row = inp("outb_row", [1, D])
    toep = inp("toep", [128, 16 * 4096])
    rT = inp("rT", [D, E]); ebias_row = inp("ebias_row", [1, E]); ebias_col = inp("ebias_col", [E, 1])
    ew1 = inp("ew1", [EPC, D, 512]); ew2 = inp("ew2", [EPC, 512, D])
    sw1T = inp("sw1T", [D, SF]); sw2T = inp("sw2T", [SF, D])
    ident = inp("ident", [128, 128], BF16); identf = inp("identf", [128, 128], F32)
    lt128 = inp("lt128", [128, 128], BF16); onesrow = inp("onesrow", [1, 128], BF16)
    onesrowf = inp("onesrowf", [1, 128], F32); lt128f = inp("lt128f", [128, 128], F32)
    onesf = inp("onesf", [128, 128], F32); onescolb = inp("onescolb", [128, 1], BF16)
    ones64 = inp("ones64", [128, 64], F32); e384row = inp("e384row", [1, E])
    gid = inp("gid", [128, NTILE]); lid = inp("lid", [128, NTILE])
    moeg_idx = inp("moeg_idx", [128, TPC // 16], I16)
    selmat = inp("selmat", [2, 128, NTILE]); myslots = inp("myslots", [128, SPC // 16], I16)
    rep16 = inp("rep16", [16, 128], F32); trashpat = inp("trashpat", [128, NTILE * K], F32)

    out_t = nc.dram_tensor("out", [TPC, D], F32, kind="ExternalOutput")
    dbg_t = {}
    def dbgout(name, shape, dtype=F32):
        if name in dbg:
            dbg_t[name] = nc.dram_tensor("dbg_" + name, shape, dtype, kind="ExternalOutput")
            return dbg_t[name]
        return None

    xn_shard = nc.dram_tensor("xn_shard", [TPC, D], BF16)
    xn_all = nc.dram_tensor("xn_all", [T, D], BF16, addr_space="Shared")
    MTW = Lq + 64
    modT_hbm = nc.dram_tensor("modT_hbm", [D, 2 * MTW], BF16)
    y_hbm = nc.dram_tensor("y_hbm", [TPC, D], BF16)
    s_shard = nc.dram_tensor("s_shard", [TPC, D], BF16)
    s_all = nc.dram_tensor("s_all", [T, D], BF16, addr_space="Shared")
    cnt_shard = nc.dram_tensor("cnt_shard", [NTILE * E], F32)
    cnt_all = nc.dram_tensor("cnt_all", [NCORES * NTILE * E], F32)
    tabTW = nc.dram_tensor("tabTW", [TAB_ROWS, 64], F32)
    list6h = nc.dram_tensor("list6h", [TAB_ROWS, 128], I16)
    boun_f = nc.dram_tensor("boun_f", [4096], F32)
    z_shard = nc.dram_tensor("z_shard", [ZROWS, D], BF16)
    z_all = nc.dram_tensor("z_all", [NCORES * ZROWS, D], BF16, addr_space="Shared")

    p = Prog(nc)
    for s in ("qS", "qG", "qA", "tE", "vE", "sE", "gE", "cc"):
        p.sem(s)

    A = nc.alloc_sbuf_tensor
    sb = {}
    def sbuf(name, shape, dtype):
        sb[name] = A(name, shape, dtype)
        return sb[name]

    ident_sb = sbuf("ident_sb", [128, 128], BF16)
    identf_sb = sbuf("identf_sb", [128, 128], F32)
    lt128_sb = sbuf("lt128_sb", [128, 128], BF16)
    onesrow_sb = sbuf("onesrow_sb", [1, 128], BF16)
    onesrowf_sb = sbuf("onesrowf_sb", [1, 128], F32)
    lt128f_sb = sbuf("lt128f_sb", [128, 128], F32)
    onesf_sb = sbuf("onesf_sb", [128, 128], F32)
    onescolb_sb = sbuf("onescolb_sb", [128, 1], BF16)
    selmat_sb = sbuf("selmat_sb", [128, 2 * NTILE], F32)
    myslots_sb = sbuf("myslots_sb", [128, SPC // 16], I16)
    ones64_sb = sbuf("ones64_sb", [128, 64], F32)
    e384_sb = sbuf("e384_sb", [1, E], F32)
    ebrow_sb = sbuf("ebrow_sb", [1, E], F32)
    ebcol_sb = sbuf("ebcol_sb", [E, 1], F32)
    gid_sb = sbuf("gid_sb", [128, NTILE], F32)
    lid_sb = sbuf("lid_sb", [128, NTILE], F32)
    inb_sb = sbuf("inb_sb", [128, 8], F32)
    b1_sb = sbuf("b1_sb", [128, 32], F32)
    b2_sb = sbuf("b2_sb", [128, 8], F32)
    outbrow_sb = sbuf("outbrow_sb", [1, D], BF16)
    moegi_sb = sbuf("moegi_sb", [128, TPC // 16], I16)
    zero_sb = sbuf("zero_sb", [128, 2048], F32)
    stat_sb = sbuf("stat_sb", [128, 16], F32)
    rep16_sb = sbuf("rep16_sb", [16, 128], F32)
    trashp_sb = sbuf("trashp_sb", [128, NTILE * K], F32)
    cntj_sb = sbuf("cntj_sb", [1, 8], F32)
    cntj_i = sbuf("cntj_i", [1, 8], I32)

    for (t, s_) in ((ident, ident_sb), (identf, identf_sb), (lt128, lt128_sb),
                    (onesrow, onesrow_sb), (ones64, ones64_sb), (e384row, e384_sb),
                    (ebias_row, ebrow_sb), (ebias_col, ebcol_sb), (gid, gid_sb),
                    (lid, lid_sb), (inb, inb_sb), (b1, b1_sb), (b2, b2_sb),
                    (moeg_idx, moegi_sb), (onesrowf, onesrowf_sb), (lt128f, lt128f_sb),
                    (onesf, onesf_sb), (onescolb, onescolb_sb), (myslots, myslots_sb),
                    (rep16, rep16_sb), (trashpat, trashp_sb)):
        p.dma("sync", "qS", s_.ap()[:, :], t.ap()[:, :])
    p.dma("sync", "qS", selmat_sb.ap()[:, 0:NTILE], selmat.ap()[0, :, :])
    p.dma("sync", "qS", selmat_sb.ap()[:, NTILE:2 * NTILE], selmat.ap()[1, :, :])
    p.dma("gpsimd", "qG", outbrow_sb.ap()[:, :], outb_row.ap()[:, :])
    CONST_Q = p.cnt["qS"]
    CONST_G = p.cnt["qG"]
    V_ZERO = p.tick("vector", "vE", lambda v: v.memset(zero_sb[:, :], 0.0))

    pb = [nc.alloc_psum_tensor(f"pb{i}", [128, 512], F32) for i in range(8)]

    AT_BASE = 47424
    AT_SIZE = 229376 - AT_BASE
    _at_names = {}
    def at(name, off_bytes, shape, dtype):
        assert off_bytes % 64 == 0
        sz = mybir.dt.size(dtype)
        for d in shape[1:]:
            pass
        return nc.alloc_sbuf_tensor_at(name, shape, dtype, offset=AT_BASE + off_bytes)

    xrow_t = at("xrow", 147456, [128, 1024], F32)
    xnrow_t = at("xnrow", 151552, [128, 1024], BF16)
    sq_t = at("sq", 157696, [128, 1024], F32)
    xrow = xrow_t.ap()[:, :]
    xnrow = xnrow_t.ap()[:, :]
    sq_sb = sq_t.ap()[:, :]
    for tt in range(NTILE):
        qv = p.dma("sync", "qS", xrow, x_mix.ap()[tt * 128:(tt + 1) * 128, :])
        p.wait("scalar", "qS", qv)
        sv = p.tick("scalar", "sE", lambda s: s.activation(
            out=sq_sb[:, :], in_=xrow, func=AF.Square, bias=0.0, scale=1.0))
        p.wait("vector", "qS", qv)
        p.tick("vector", "vE", lambda v: v.reduce_sum(stat_sb[:, 1:2], xrow, axis=mybir.AxisListType.X))
        p.wait("vector", "sE", sv)
        vv = p.tick("vector", "vE", lambda v: v.reduce_sum(stat_sb[:, 0:1], sq_sb[:, :], axis=mybir.AxisListType.X))
        p.wait("vector", "vE", vv)
        p.tick("vector", "vE", lambda v: v.tensor_scalar_mul(stat_sb[:, 2:3], stat_sb[:, 1:2], 1.0 / D))
        vv = p.tick("vector", "vE", lambda v: v.tensor_scalar_mul(stat_sb[:, 3:4], stat_sb[:, 0:1], 1.0 / D))
        p.wait("vector", "vE", vv)
        vv = p.tick("vector", "vE", lambda v: v.tensor_tensor(stat_sb[:, 4:5], stat_sb[:, 2:3], stat_sb[:, 2:3], AOP.mult))
        p.wait("vector", "vE", vv)
        vv = p.tick("vector", "vE", lambda v: v.tensor_tensor(stat_sb[:, 5:6], stat_sb[:, 3:4], stat_sb[:, 4:5], AOP.subtract))
        p.wait("vector", "vE", vv)
        vv = p.tick("vector", "vE", lambda v: v.tensor_scalar_add(stat_sb[:, 5:6], stat_sb[:, 5:6], 1e-5))
        p.wait("scalar", "vE", vv)
        sv = p.tick("scalar", "sE", lambda s: s.activation(
            out=stat_sb[:, 6:7], in_=stat_sb[:, 5:6], func=AF.Sqrt, bias=0.0, scale=1.0))
        p.wait("vector", "sE", sv)
        vv = p.tick("vector", "vE", lambda v: v.reciprocal(stat_sb[:, 7:8], stat_sb[:, 6:7]))
        p.wait("vector", "vE", vv)
        vv = p.tick("vector", "vE", lambda v: v.tensor_scalar(
            xnrow, xrow, stat_sb[:, 2:3], stat_sb[:, 7:8], AOP.subtract, AOP.mult))
        p.wait("sync", "vE", vv)
        p.dma("sync", "qS", xn_shard.ap()[tt * 128:(tt + 1) * 128, :], xnrow)
    P0_Q = p.cnt["qS"]

    p.wait("gpsimd", "qS", P0_Q)
    ccv = p.inc("cc", 1)
    p.run("gpsimd", lambda g: g.collective_compute(
        "AllGather", AOP.bypass, replica_groups=RG,
        ins=[xn_shard.ap().opt()], outs=[xn_all.ap().opt()]).then_inc(p.sems["cc"], 1))
    AG1 = ccv

    if "xn" in dbg:
        tap = dbgout("xn", [TPC, D], BF16)
        p.wait("scalar", "qS", P0_Q)
        for tt in range(NTILE):
            p.dma("scalar", "qA", tap.ap()[tt * 128:(tt + 1) * 128, :],
                  xn_shard.ap()[tt * 128:(tt + 1) * 128, :])
        p.wait("scalar", "qA", p.cnt["qA"])

    w1_sb = at("w1_sb", 0, [128, 8 * 4096], BF16).ap()[:, :]
    w2_sb = at("w2_sb", 65536, [128, 32 * 1024], BF16).ap()[:, :]
    inw_sb = at("inw_sb", 131072, [128, 8 * 1024], BF16).ap()[:, :]
    xnT = [at(f"xnT{i}", 147456 + i * 4096, [128, 8 * 256], BF16).ap()[:, :] for i in range(2)]
    hch = [at(f"hch{i}", 155648 + i * 512, [128, 256], BF16).ap()[:, :] for i in range(2)]
    xp_sb = at("xp_sb", 157696, [128, 8 * 256], F32).ap()[:, :]
    modu_sb = at("modu_sb", 165888, [128, 8 * 256], BF16).ap()[:, :]

    for dc in range(8):
        p.dma("gpsimd", "qG", w1_sb[:, dc * 4096:(dc + 1) * 4096], w1.ap()[dc * 128:(dc + 1) * 128, :])
    for hc in range(32):
        p.dma("gpsimd", "qG", w2_sb[:, hc * 1024:(hc + 1) * 1024], w2.ap()[hc * 128:(hc + 1) * 128, :])
    for dc in range(8):
        p.dma("gpsimd", "qG", inw_sb[:, dc * 1024:(dc + 1) * 1024], inw.ap()[dc * 128:(dc + 1) * 128, :])
    P1_W = p.cnt["qG"]

    NT1 = TPC // 256
    xnT_q = [0, 0]
    xnb_done = {}
    modu_q = 0
    gelu_tick = {}
    modmm_tick = {}
    xp_tick = {}

    def load_xnT(tile):
        buf = tile % 2
        p.wait("sync", "qS", P0_Q)
        if tile >= 2:
            p.wait("sync", "tE", xnb_done[tile - 2])
        for dc in range(8):
            p.dmaT("sync", "qS",
                   xnT[buf][:, dc * 256:(dc + 1) * 256],
                   xn_shard.ap()[tile * 256:(tile + 1) * 256, dc * 128:(dc + 1) * 128])
        xnT_q[buf] = p.cnt["qS"]

    load_xnT(0)
    load_xnT(1)
    p.wait("tensor", "qG", P1_W)

    modu_evac = 0
    for tile in range(NT1):
        buf = tile % 2
        p.wait("tensor", "qS", xnT_q[buf])
        xnb = xnT[buf]
        if tile:
            p.wait("scalar", "vE", modu_evac)
        for nc_ in range(8):
            bank = pb[4 + nc_ % 2].ap()[:, 0:256]
            prev = (tile, nc_ - 2) if nc_ >= 2 else (tile - 1, 6 + nc_)
            if prev in xp_tick:
                p.wait("tensor", "sE", xp_tick[prev][1])
            for dc in range(8):
                st_ = inw_sb[:, dc * 1024 + nc_ * 128: dc * 1024 + (nc_ + 1) * 128]
                tv = p.mm(bank, st_, xnb[:, dc * 256:(dc + 1) * 256],
                          start=(dc == 0), stop=(dc == 7), tick=(dc == 7))
            p.wait("scalar", "tE", tv)
            sv = p.tick("scalar", "sE", lambda s, o=xp_sb[:, nc_ * 256:(nc_ + 1) * 256],
                        i=bank, b_=inb_sb[:, nc_:nc_ + 1]: s.activation(
                            out=o, in_=i, func=AF.Identity, bias=b_, scale=1.0))
            xp_tick[(tile, nc_)] = (tv, sv)
        for hc in range(32):
            hb = pb[6 + hc % 2].ap()[:, 0:256]
            hbuf = hch[hc % 2]
            if (tile, hc - 2) in gelu_tick:
                p.wait("tensor", "sE", gelu_tick[(tile, hc - 2)])
            elif tile and hc < 2:
                p.wait("tensor", "sE", gelu_tick[(tile - 1, 30 + hc)])
            for dc in range(8):
                st_ = w1_sb[:, dc * 4096 + hc * 128: dc * 4096 + (hc + 1) * 128]
                tv = p.mm(hb, st_, xnb[:, dc * 256:(dc + 1) * 256],
                          start=(dc == 0), stop=(dc == 7), tick=(dc == 7))
            if hc == 31:
                xnb_done[tile] = tv
            p.wait("scalar", "tE", tv)
            if hc >= 2:
                p.wait("scalar", "tE", modmm_tick[(tile, hc - 2)])
            elif tile and hc < 2:
                p.wait("scalar", "tE", modmm_tick[(tile - 1, 30 + hc)])
            gv = p.tick("scalar", "sE", lambda s, o=hbuf, i=hb, b_=b1_sb[:, hc:hc + 1]:
                        s.activation(out=o, in_=i, func=AF.Gelu, bias=b_, scale=1.0))
            gelu_tick[(tile, hc)] = gv
            p.wait("tensor", "sE", gv)
            if hc == 0 and tile:
                p.wait("tensor", "vE", modu_evac)
            for dc in range(8):
                bank = pb[dc // 2].ap()[:, (dc % 2) * 256:(dc % 2) * 256 + 256]
                st_ = w2_sb[:, hc * 1024 + dc * 128: hc * 1024 + (dc + 1) * 128]
                tv = p.mm(bank, st_, hbuf, start=(hc == 0 and dc % 2 == 0),
                          stop=(hc == 31), tick=(dc == 7))
            modmm_tick[(tile, hc)] = tv
        p.wait("vector", "tE", modmm_tick[(tile, 31)])
        p.wait("vector", "sE", xp_tick[(tile, 7)][1])
        if tile:
            p.wait("vector", "qS", modu_q)
        for dc in range(8):
            bank = pb[dc // 2].ap()[:, (dc % 2) * 256:(dc % 2) * 256 + 256]
            vv = p.tick("vector", "vE", lambda v, o=modu_sb[:, dc * 256:(dc + 1) * 256],
                        i=bank, b_=b2_sb[:, dc:dc + 1], x2=xp_sb[:, dc * 256:(dc + 1) * 256]:
                        v.scalar_tensor_tensor(o, i, b_, x2, AOP.add, AOP.mult))
        modu_evac = vv
        p.wait("sync", "vE", modu_evac)
        b_of = tile // 8
        col0 = b_of * MTW + 64 + (tile % 8) * 256
        W2H = 2 * MTW
        dst = bass.AP(modT_hbm, col0, [[W2H, 128], [128 * W2H, 8], [1, 256]])
        modu_q = p.dma("sync", "qS", dst, modu_sb[:, :].rearrange("p (dc t) -> p dc t", dc=8))
        if tile % 8 == 7:
            hdst = bass.AP(modT_hbm, b_of * MTW, [[W2H, 128], [128 * W2H, 8], [1, 64]])
            modu_q = p.dma("sync", "qS", hdst,
                           modu_sb[:, :].rearrange("p (dc t) -> p dc t", dc=8)[:, :, 192:256])
        if tile + 2 < NT1:
            load_xnT(tile + 2)
    P1_MOD_Q = p.cnt["qS"]

    if "modT" in dbg:
        tap = dbgout("modT", [D, TPC], BF16)
        p.wait("scalar", "qS", P1_MOD_Q)
        for dc in range(8):
            for b_ in range(2):
                p.dma("scalar", "qA", tap.ap()[dc * 128:(dc + 1) * 128, b_ * Lq:(b_ + 1) * Lq],
                      modT_hbm.ap()[dc * 128:(dc + 1) * 128, b_ * MTW + 64: b_ * MTW + 64 + Lq])

    P1_T_END = p.cnt["tE"]

    MT_t = at("MT", 0, [128, 32 * 1024], BF16)
    ybuf_t = at("ybuf", 65536, [128, 32 * 512], BF16)
    toep_t = [at(f"toep{i}", 98304 + i * 8192, [128, 64 * 64], BF16) for i in range(2)]

    def load_toep(g):
        buf = g % 2
        return p.dma("gpsimd", "qG", toep_t[buf].ap()[:, :],
                     toep.ap()[:, (g % 16) * 4096:((g % 16) + 1) * 4096])

    CGRP = 16
    p.wait("gpsimd", "tE", P1_T_END)
    toep_q = {0: load_toep(0)}
    toep_q[1] = load_toep(1)

    y_store_q = 0
    mt_free_tick = 0
    ybuf_free = {}
    conv_mm_tick = {}
    for b_ in range(2):
        p.wait("sync", "qS", P1_MOD_Q)
        p.wait("sync", "tE", P1_T_END)
        if b_ == 1:
            p.wait("sync", "tE", mt_free_tick)
            p.wait("sync", "qS", y_store_q)
        for blk in range(32):
            p.dmaT("sync", "qS", MT_t.ap()[:, blk * 1024:(blk + 1) * 1024],
                   modT_hbm.ap()[:, b_ * MTW + blk * 64: b_ * MTW + blk * 64 + 128])
        mt_q = p.cnt["qS"]
        if b_ == 0 and "MT0" in dbg:
            tap = dbgout("MT0", [128, 32 * 1024], BF16)
            p.wait("scalar", "qS", mt_q)
            p.dma("scalar", "qA", tap.ap()[:, :], MT_t.ap()[:, :])
            p.wait("tensor", "qA", p.cnt["qA"])
        if b_ == 0 and "toep0" in dbg:
            tap2 = dbgout("toep0", [128, 4096], BF16)
            p.wait("scalar", "qG", toep_q[0])
            p.dma("scalar", "qA", tap2.ap()[:, :], bass.AP(toep_t[0], 0, [[4096, 128], [1, 4096]]))
            p.wait("tensor", "qA", p.cnt["qA"])
        p.wait("tensor", "qS", mt_q)
        for g in range(CGRP):
            gg = b_ * CGRP + g
            p.wait("tensor", "qG", toep_q[gg])
            for c_l in range(64):
                c = g * 64 + c_l
                bk = c_l // 16
                prevkey = (b_, g - 1, bk) if g else (b_ - 1, CGRP - 1, bk)
                if c_l % 16 == 0 and prevkey in ybuf_free:
                    p.wait("tensor", "vE", ybuf_free[prevkey])
                out = pb[bk].ap()[0:64, (c_l % 16) * 32:(c_l % 16) * 32 + 32]
                rhs = bass.AP(MT_t, c, [[32768, 128], [1024, 32]])
                tv = p.mm(out, toep_t[gg % 2].ap()[:, c_l * 64:(c_l + 1) * 64], rhs,
                          start=(c_l % 16 == 0), stop=True, tick=(c_l % 16 == 15))
                if c_l % 16 == 15:
                    conv_mm_tick[(b_, g, bk)] = tv
            if gg + 2 < 2 * CGRP:
                p.wait("gpsimd", "tE", conv_mm_tick[(b_, g, 3)])
                toep_q[gg + 2] = load_toep(gg + 2)
            for bk in range(4):
                cbase = g * 64 + bk * 16
                half = cbase // 512
                c2 = cbase % 512
                p.wait("vector", "tE", conv_mm_tick[(b_, g, bk)])
                src = bass.AP(pb[bk], 0, [[512, 64], [32, 16], [1, 32]])
                dstv = bass.AP(ybuf_t, half * 64 * 16384 + c2, [[16384, 64], [1, 16], [512, 32]])
                vv = p.tick("vector", "vE", lambda v, o=dstv, i=src: v.tensor_copy(o, i))
                ybuf_free[(b_, g, bk)] = vv
        mt_free_tick = conv_mm_tick[(b_, CGRP - 1, 3)]
        for g in range(CGRP):
            for bk in range(4):
                p.wait("sync", "vE", ybuf_free[(b_, g, bk)])
        for half in range(2):
            dst = bass.AP(y_hbm, b_ * Lq * D + half * 512, [[D, 64], [64 * D, 32], [1, 512]])
            src = bass.AP(ybuf_t, half * 64 * 16384, [[16384, 64], [512, 32], [1, 512]])
            y_store_q = p.dma("sync", "qS", dst, src)
    PCONV_Q = p.cnt["qS"]

    if "y" in dbg:
        tap = dbgout("y", [TPC, D], BF16)
        p.wait("scalar", "qS", PCONV_Q)
        p.dma("scalar", "qA", tap.ap()[:, :], y_hbm.ap()[:, :])

    yT_sb = at("yT_sb", 0, [128, 8 * TPC], BF16).ap()[:, :]
    outw_sb = at("outw_sb", 98304, [128, 8 * 1024], BF16).ap()[:, :]
    srow_sb = at("srow_sb", 81920, [128, 1024], BF16).ap()[:, :]
    p.wait("gpsimd", "qS", PCONV_Q)
    for cc in range(8):
        p.dma("gpsimd", "qG", outw_sb[:, cc * 1024:(cc + 1) * 1024], outw.ap()[cc * 128:(cc + 1) * 128, :])
    OUTW_Q = p.cnt["qG"]
    p.wait("sync", "tE", mt_free_tick)
    p.wait("sync", "qS", PCONV_Q)
    for cc in range(8):
        p.dmaT("sync", "qS", yT_sb[:, cc * TPC:(cc + 1) * TPC],
               y_hbm.ap()[:, cc * 128:(cc + 1) * 128])
    YT_Q = p.cnt["qS"]
    p.wait("tensor", "qS", YT_Q)
    p.wait("tensor", "qG", OUTW_Q)
    p.wait("tensor", "qG", CONST_G)
    s_evac_tick = {}
    s_q = {}
    for tt in range(NTILE):
        for hf in range(2):
            bank = pb[4 + hf]
            if (tt - 1, hf) in s_evac_tick:
                p.wait("tensor", "sE", s_evac_tick[(tt - 1, hf)])
            for cc in range(8):
                p.mm(bank.ap()[:, :],
                     yT_sb[:, cc * TPC + tt * 128: cc * TPC + (tt + 1) * 128],
                     outw_sb[:, cc * 1024 + hf * 512: cc * 1024 + (hf + 1) * 512],
                     start=(cc == 0), stop=False)
            tv = p.mm(bank.ap()[:, :], onesrow_sb.ap()[:, :],
                      outbrow_sb.ap()[:, hf * 512:(hf + 1) * 512],
                      start=False, stop=True, tick=True)
            p.wait("scalar", "tE", tv)
            if (tt - 1, hf) in s_q:
                p.wait("scalar", "qS", s_q[(tt - 1, hf)])
            sv = p.tick("scalar", "sE", lambda s, o=srow_sb[:, hf * 512:(hf + 1) * 512],
                        i=bank.ap()[:, :]: s.activation(out=o, in_=i, func=AF.Identity,
                                                        bias=0.0, scale=1.0))
            s_evac_tick[(tt, hf)] = sv
            p.wait("sync", "sE", sv)
            s_q[(tt, hf)] = p.dma("sync", "qS",
                                  s_shard.ap()[tt * 128:(tt + 1) * 128, hf * 512:(hf + 1) * 512],
                                  srow_sb[:, hf * 512:(hf + 1) * 512])
    POUT_Q = p.cnt["qS"]
    p.wait("gpsimd", "qS", POUT_Q)
    AG5 = p.inc("cc", 1)
    p.run("gpsimd", lambda g: g.collective_compute(
        "AllGather", AOP.bypass, replica_groups=RG,
        ins=[s_shard.ap().opt()], outs=[s_all.ap().opt()]).then_inc(p.sems["cc"], 1))

    if "s" in dbg:
        tap = dbgout("s", [TPC, D], BF16)
        p.wait("scalar", "qS", POUT_Q)
        p.dma("scalar", "qA", tap.ap()[:, :], s_shard.ap()[:, :])

    POUT_T = p.cnt["tE"]
    xnm = at("xnm", 0, [128, 8 * TPC], BF16)
    sT_sb = at("sT", 65536, [64, TPC], F32).ap()[:, :]
    sbT_sb = at("sbT", 81920, [64, TPC], F32).ap()[:, :]
    Itmf = at("Itmf", 161408, [128, NTILE * E], F32).ap()[:, :]
    Itmb = at("Itmb", 169600, [128, NTILE * E], BF16).ap()[:, :]
    Wtm = at("Wtm", 173696, [128, NTILE * E], F32).ap()[:, :]
    list6 = at("list6", 118784, [128, NTILE * K], I16)
    wq6 = at("wq6", 119808, [128, NTILE * K], I16)
    ls_idx = at("ls_idx", 124928, [128, NTILE * E], I16).ap()[:, :]
    ls_slot = at("ls_slot", 129024, [128, NTILE * E], I16).ap()[:, :]
    ls_wq = at("ls_wq", 133120, [128, NTILE * E], I16).ap()[:, :]
    fidx = at("fidx", 137216, [128, NTILE * K], I16).ap()[:, :]
    fval = at("fval", 138240, [128, NTILE * K * 2], I16).ap()[:, :]
    segpos_a = at("segpos", 140288, [128, NTILE * K], F32).ap()[:, :]
    sc2 = at("sc2", 143360, [128, 64 * 10], F32).ap()[:, :]
    o8_sb = at("o8", 145920, [128, 8], F32).ap()[:, :]
    ebb_sb = at("ebb", 146176, [128, 64], F32).ap()[:, :]
    e384b_sb = at("e384b", 146432, [128, 64], F32).ap()[:, :]
    cntv_sb = at("cntv", 146688, [64, 32], F32).ap()[:, :]
    goff_sb = at("goff", 146944, [128, 2 * 64], F32).ap()[:, :]
    myoffR = at("myoffR", 147456, [1, NTILE * E], F32).ap()[:, :]
    cumrow = at("cumrow", 155648, [1, 8], F32).ap()[:, :]
    rowtmp = at("rowtmp", 155712, [32, 64], F32).ap()[:, :]
    lidp1 = at("lidp1", 156160, [128, NTILE], I16).ap()[:, :]
    cntj_f = at("cntj_f", 156288, [1, 8], F32).ap()[:, :]
    cntj_ii = at("cntj_ii", 156352, [1, 8], I32).ap()[:, :]
    twc_sb = at("twc", 157696, [128, 2 * (TAB_ROWS // 128)], F32).ap()[:, :]
    myTW = at("myTW", 161408, [128, 24 * 64], F32).ap()[:, :]
    mytok_i = at("mytok_i", 159552, [128, 24], I16).ap()[:, :]
    xg_idx = at("xg_idx", 159616, [128, SPC // 16], I16).ap()[:, :]

    p.wait("gpsimd", "cc", AG1)
    p.wait("gpsimd", "tE", POUT_T)
    p.wait("gpsimd", "qS", POUT_Q)
    p.wait("vector", "tE", POUT_T)
    p.wait("vector", "qS", POUT_Q)
    p.wait("scalar", "qS", POUT_Q)
    XG1 = p.gatherT(xnm, 8 * TPC, xn_all.ap(), moegi_sb.ap(), TPC, elem=D, chunk=256)

    rT_sb = at("rT_sb", 159872, [128, 8 * 64], BF16).ap()[:, :]
    for dc in range(8):
        p.dma("gpsimd", "qG", rT_sb[:, dc * 64:(dc + 1) * 64], rT.ap()[dc * 128:(dc + 1) * 128, :])
    RT_Q = p.cnt["qG"]
    p.wait("tensor", "qG", RT_Q)
    p.wait("tensor", "qG", XG1)
    p.wait("tensor", "sE", p.cnt["sE"])
    for sl in range(16):
        for dc in range(8):
            tv = p.mm(pb[0].ap()[0:64, 0:256], rT_sb[:, dc * 64:(dc + 1) * 64],
                      bass.AP(xnm, sl * 2048 + dc * 256, [[8 * TPC, 128], [1, 256]]),
                      start=(dc == 0), stop=(dc == 7), tick=(dc == 7))
        p.wait("scalar", "tE", tv)
        sv = p.tick("scalar", "sE", lambda s, o=sT_sb[:, sl * 256:(sl + 1) * 256],
                    i=pb[0].ap()[0:64, 0:256]: s.activation(out=o, in_=i, func=AF.Sigmoid,
                                                            bias=0.0, scale=1.0))
        p.wait("tensor", "sE", sv)
    p.wait("vector", "sE", p.cnt["sE"])
    vv = p.tick("vector", "vE", lambda v: v.tensor_scalar_add(sbT_sb, sT_sb, ebcol_sb.ap()[:, :]))
    SBT_V = vv

    p.wait("tensor", "vE", SBT_V)
    tv = p.mm(pb[1].ap()[:, 0:64], onesrowf_sb.ap()[:, :], ebrow_sb.ap()[:, :],
              start=True, stop=True, tick=True)
    p.wait("vector", "tE", tv)
    p.tick("vector", "vE", lambda v: v.tensor_copy(ebb_sb, pb[1].ap()[:, 0:64]))
    tv = p.mm(pb[1].ap()[:, 64:128], onesrowf_sb.ap()[:, :], e384_sb.ap()[:, :],
              start=False, stop=True, tick=True)
    p.wait("vector", "tE", tv)
    vv = p.tick("vector", "vE", lambda v: v.tensor_copy(e384b_sb, pb[1].ap()[:, 64:128]))
    vv = p.tick("vector", "vE", lambda v: v.tensor_scalar(lidp1, lid_sb.ap()[:, :],
                                                          1.0, None, AOP.add))
    ONETIME_V = vv

    sc = lambda i: sc2[:, i * 64:(i + 1) * 64]
    cnt_mm = {}
    for tt in range(NTILE):
        p.wait("tensor", "vE", SBT_V)
        tv = p.tick("tensor", "tE", lambda t, i=sbT_sb[:, tt * 128:(tt + 1) * 128]:
                    t.transpose(pb[2].ap()[:, 0:64], i, identf_sb.ap()[0:64, 0:64]))
        p.wait("vector", "tE", tv)
        vv = p.tick("vector", "vE", lambda v: v.tensor_copy(sc(0), pb[2].ap()[:, 0:64]))
        p.wait("vector", "vE", vv)
        vv = p.tick("vector", "vE", lambda v: v.max(out=o8_sb, in_=sc(0)))
        p.wait("vector", "vE", vv)
        vv = p.tick("vector", "vE", lambda v: v.tensor_single_scalar(sc(1), sc(0), o8_sb[:, 5:6], AOP.is_ge))
        p.wait("vector", "vE", vv)
        p.tick("vector", "vE", lambda v, o=Itmf[:, tt * 64:(tt + 1) * 64]: v.tensor_copy(o, sc(1)))
        p.tick("vector", "vE", lambda v, o=Itmb[:, tt * 64:(tt + 1) * 64]: v.tensor_copy(o, sc(1)))
        p.wait("vector", "vE", ONETIME_V)
        p.tick("vector", "vE", lambda v: v.tensor_sub(sc(2), sc(0), ebb_sb))
        vv = p.tick("vector", "vE", lambda v: v.tensor_mul(sc(3), sc(1), sc(2)))
        p.wait("vector", "vE", vv)
        vv = p.tick("vector", "vE", lambda v: v.reduce_sum(sc(4)[:, 0:1], sc(3), axis=mybir.AxisListType.X))
        p.wait("vector", "vE", vv)
        vv = p.tick("vector", "vE", lambda v: v.tensor_scalar_add(sc(4)[:, 1:2], sc(4)[:, 0:1], 1e-20))
        p.wait("vector", "vE", vv)
        vv = p.tick("vector", "vE", lambda v: v.reciprocal(sc(4)[:, 2:3], sc(4)[:, 1:2]))
        p.wait("vector", "vE", vv)
        vv = p.tick("vector", "vE", lambda v, o=Wtm[:, tt * 64:(tt + 1) * 64]:
                    v.tensor_scalar_mul(o, sc(3), sc(4)[:, 2:3]))
        p.wait("tensor", "vE", p.cnt["vE"])
        tv = p.mm(pb[3].ap()[0:64, tt:tt + 1], Itmb[:, tt * 64:(tt + 1) * 64],
                  onescolb_sb.ap()[:, :], start=(tt == 0), stop=True, tick=True)
        cnt_mm[tt] = tv
    p.wait("vector", "tE", cnt_mm[NTILE - 1])
    vv = p.tick("vector", "vE", lambda v: v.tensor_copy(cntv_sb, pb[3].ap()[0:64, 0:32]))
    p.wait("sync", "vE", vv)
    CQ = p.dma("sync", "qS", bass.AP(cnt_shard, 0, [[1, 64], [64, 32]]), cntv_sb, allow_slow_non_contiguous=True)
    p.wait("gpsimd", "qS", CQ)
    AG2 = p.inc("cc", 1)
    p.run("gpsimd", lambda g: g.collective_compute(
        "AllGather", AOP.bypass, replica_groups=RG,
        ins=[cnt_shard.ap().opt()], outs=[cnt_all.ap().opt()]).then_inc(p.sems["cc"], 1))

    cntg = at("cntg", 160896, [128, 2 * 64], F32).ap()[:, :]
    p.wait("sync", "cc", AG2)
    for ch in range(2):
        p.dma("sync", "qS", cntg[:, ch * 64:(ch + 1) * 64],
              bass.AP(cnt_all, ch * 1024, [[64, 16], [2048, 8], [1, 64]]))
    GLD = p.cnt["qS"]
    if "cntg" in dbg:
        tapc = dbgout("cntg", [128, 128], F32)
        p.wait("scalar", "qS", GLD)
        p.dma("scalar", "qA", tapc.ap()[:, :], cntg)
    p.wait("tensor", "qS", GLD)
    tv = p.mm(pb[2].ap()[:, 0:64], lt128f_sb.ap()[:, :], cntg[:, 0:64],
              start=True, stop=True, tick=True)
    tv = p.mm(pb[2].ap()[:, 64:128], onesf_sb.ap()[:, :], cntg[:, 0:64],
              start=False, stop=True, tick=True)
    tv = p.mm(pb[2].ap()[:, 64:128], lt128f_sb.ap()[:, :], cntg[:, 64:128],
              start=False, stop=True, tick=True)
    p.wait("vector", "tE", tv)
    vv = p.tick("vector", "vE", lambda v: v.tensor_copy(goff_sb, pb[2].ap()[:, 0:128]))
    p.wait("tensor", "vE", vv)
    tv = p.mm(pb[3].ap()[0:64, 32:64], goff_sb[:, 0:64], selmat_sb.ap()[:, 0:NTILE],
              start=True, stop=False, tick=False)
    tv = p.mm(pb[3].ap()[0:64, 32:64], goff_sb[:, 64:128], selmat_sb.ap()[:, NTILE:2 * NTILE],
              start=False, stop=True, tick=True)
    p.wait("vector", "tE", tv)
    vv = p.tick("vector", "vE", lambda v: v.tensor_copy(sc(5)[0:64, 0:32], pb[3].ap()[0:64, 32:64]))
    p.wait("tensor", "vE", vv)
    tv = p.tick("tensor", "tE", lambda t: t.transpose(pb[2].ap()[0:32, 0:64],
                                                      sc(5)[0:64, 0:32], identf_sb.ap()[0:64, 0:64]))
    p.wait("vector", "tE", tv)
    vv = p.tick("vector", "vE", lambda v: v.tensor_copy(rowtmp, pb[2].ap()[0:32, 0:64]))
    p.wait("sync", "vE", vv)
    BQ = p.dma("sync", "qS", bass.AP(boun_f, 0, [[64, 32], [1, 64]]), rowtmp)
    p.wait("sync", "qS", BQ)
    BQ2 = p.dma("sync", "qS", myoffR, bass.AP(boun_f, 0, [[2048, 1], [1, 2048]]))

    lsv = 0
    rank_copy = 0
    for tt in range(NTILE):
        p.wait("tensor", "qS", BQ2)
        if rank_copy:
            p.wait("tensor", "vE", rank_copy)
        tv = p.mm(pb[2].ap()[:, 0:64], lt128f_sb.ap()[:, :],
                  Itmf[:, tt * 64:(tt + 1) * 64], start=True, stop=False)
        tv = p.mm(pb[2].ap()[:, 0:64], onesrowf_sb.ap()[:, :],
                  myoffR[:, tt * 64:(tt + 1) * 64], start=False, stop=True, tick=True)
        p.wait("vector", "tE", tv)
        vv = p.tick("vector", "vE", lambda v: v.tensor_copy(sc(6), pb[2].ap()[:, 0:64]))
        rank_copy = vv
        p.wait("vector", "vE", vv)
        vv = p.tick("vector", "vE", lambda v: v.tensor_single_scalar(sc(7), sc(6), float(CAP), AOP.is_lt))
        p.wait("vector", "vE", vv)
        vv = p.tick("vector", "vE", lambda v, i=Itmf[:, tt * 64:(tt + 1) * 64]:
                    v.tensor_mul(sc(7), sc(7), i))
        p.tick("vector", "vE", lambda v: v.tensor_add(sc(8), sc(6), e384b_sb))
        vv = p.tick("vector", "vE", lambda v, i=Itmf[:, tt * 64:(tt + 1) * 64]: v.tensor_tensor_scan(
            sc(9), i, zero_sb.ap()[:, 0:64], 0.0, AOP.add, AOP.add))
        p.wait("vector", "vE", vv)
        vv = p.tick("vector", "vE", lambda v, i=Itmf[:, tt * 64:(tt + 1) * 64]:
                    v.tensor_sub(sc(9), sc(9), i))
        p.wait("vector", "vE", vv)
        vv = p.tick("vector", "vE", lambda v, c_=float(6 * tt + 1): v.tensor_scalar(
            sc(9), sc(9), c_, None, AOP.add))
        p.wait("vector", "vE", vv)
        vv = p.tick("vector", "vE", lambda v: v.tensor_mul(sc(9), sc(9), sc(7)))
        p.wait("vector", "vE", vv)
        vv = p.tick("vector", "vE", lambda v, o=ls_idx[:, tt * 64:(tt + 1) * 64]:
                    v.tensor_scalar(o, sc(9), 1.0, None, AOP.subtract))
        p.tick("vector", "vE", lambda v, o=ls_slot[:, tt * 64:(tt + 1) * 64]:
               v.tensor_scalar(o, sc(8), 1.0, None, AOP.add))
        vv = p.tick("vector", "vE", lambda v, o=ls_wq[:, tt * 64:(tt + 1) * 64],
                    i=Wtm[:, tt * 64:(tt + 1) * 64]:
                    v.tensor_scalar(o, i, 32000.0, None, AOP.mult))
        lsv = vv
    p.wait("gpsimd", "vE", lsv)
    g1 = p.tick("gpsimd", "gE", lambda g: g.local_scatter(
        out_ap=list6.ap()[:, :], data_ap=ls_slot, idxs_ap=ls_idx,
        channels=128, num_elems=NTILE * K, num_idxs=NTILE * E))
    g2 = p.tick("gpsimd", "gE", lambda g: g.local_scatter(
        out_ap=wq6.ap()[:, :], data_ap=ls_wq, idxs_ap=ls_idx,
        channels=128, num_elems=NTILE * K, num_idxs=NTILE * E))

    if "route" in dbg:
        tapr = dbgout("route", [128, NTILE * K * 2], I16)
        p.wait("scalar", "gE", g2)
        p.dma("scalar", "qA", tapr.ap()[:, 0:NTILE * K], list6.ap()[:, :])
        p.dma("scalar", "qA", tapr.ap()[:, NTILE * K:], wq6.ap()[:, :])

    indf = at("indf", 65536, [128, NTILE * K], F32).ap()[:, :]
    l6f2 = at("l6f2", 66368, [128, NTILE * K], F32).ap()[:, :]
    wqf = at("wqf", 67200, [128, NTILE * K], F32).ap()[:, :]
    j4b = at("j4b", 68032, [128, 8], F32).ap()[:, :]
    tabval_t = at("tabval", 68096, [128, NTILE * K * 2], F32)
    tidx_str = at("tidx_str", 72832, [128, NSEL // 16], I16).ap()[:, :]
    zcomb_str = sbuf("zcomb_str", [128, NSEL // 16], I16).ap()[:, :]
    tabidx = at("tabidx", 75904, [128, NTILE * K], I16)
    zcf = at("zcf", 137216, [128, NTILE * K], F32).ap()[:, :]
    tbf = at("tbf", 138240, [128, NTILE * K], F32).ap()[:, :]
    mytf = at("mytf", 139264, [128, 24], F32).ap()[:, :]
    str16f_t = at("str16f", 147456, [16, NSEL // 16], F32)

    p.wait("vector", "gE", g2)
    vv = p.tick("vector", "vE", lambda v: v.tensor_copy(l6f2, list6.ap()[:, :]))
    p.wait("vector", "vE", vv)
    vv = p.tick("vector", "vE", lambda v: v.tensor_single_scalar(indf, l6f2, 0.5, AOP.is_gt))
    p.tick("vector", "vE", lambda v: v.tensor_copy(wqf, wq6.ap()[:, :]))
    vv = p.tick("vector", "vE", lambda v: v.tensor_scalar_mul(wqf, wqf, 1.0 / 32000.0))
    IND_V = vv

    p.wait("gpsimd", "vE", V_ZERO)
    off = 0
    while off < TAB_ROWS * 64:
        n = min(262144, TAB_ROWS * 64 - off)
        p.dma("gpsimd", "qG", bass.AP(tabTW, off, [[n // 128, 128], [1, n // 128]]), zero_sb.ap()[:, 0:n // 128])
        off += n
    p.dma("gpsimd", "qG", z_shard.ap()[SPC:ZROWS, :], zero_sb.ap()[:, 0:1024])
    PREZ_Q = p.cnt["qG"]

    p.wait("vector", "vE", IND_V)
    p.tick("vector", "vE", lambda v: v.tensor_scalar(sc2[:, 0:192], l6f2, 1.0, None, AOP.subtract))
    p.wait("vector", "vE", p.cnt["vE"])
    p.tick("vector", "vE", lambda v: v.tensor_sub(sc2[:, 0:192], sc2[:, 0:192], trashp_sb.ap()[:, :]))
    p.wait("vector", "vE", p.cnt["vE"])
    p.tick("vector", "vE", lambda v: v.tensor_mul(sc2[:, 0:192], sc2[:, 0:192], indf))
    p.wait("vector", "vE", p.cnt["vE"])
    p.tick("vector", "vE", lambda v: v.tensor_add(tbf, sc2[:, 0:192], trashp_sb.ap()[:, :]))
    for tt in range(NTILE):
        p.tick("vector", "vE", lambda v, o=bass.AP(tabval_t, tt * 12, [[NSEL * 2 // 128, 128], [2, 6]]),
               i=gid_sb.ap()[:, tt:tt + 1].to_broadcast([128, 6]): v.tensor_copy(o, i))
    tv2 = p.tick("vector", "vE", lambda v: v.tensor_copy(
        bass.AP(tabval_t, 1, [[NSEL * 2 // 128, 128], [2, NSEL // 128]]), wqf))
    p.wait("vector", "vE", p.cnt["vE"])
    p.tick("vector", "vE", lambda v: v.tensor_scalar(
        sc2[:, 0:192], l6f2, float(ZERO_ROW + 1), None, AOP.subtract))
    p.wait("vector", "vE", p.cnt["vE"])
    p.tick("vector", "vE", lambda v: v.tensor_mul(sc2[:, 0:192], sc2[:, 0:192], indf))
    p.wait("vector", "vE", p.cnt["vE"])
    p.tick("vector", "vE", lambda v: v.tensor_scalar(
        zcf, sc2[:, 0:192], float(ZERO_ROW), None, AOP.add))
    P3VEC = p.cnt["vE"]

    def stripe(natf_ap, ncols, dst_i16_ap, wait_v):
        W8 = ncols * 8
        last = wait_v
        for g16 in range(8):
            p.wait("tensor", "vE", last)
            tv = p.mm(pb[2].ap()[0:16, 0:ncols],
                      identf_sb.ap()[:, g16 * 16:(g16 + 1) * 16], natf_ap,
                      start=True, stop=True, tick=True)
            p.wait("vector", "tE", tv)
            last = p.tick("vector", "vE", lambda v, g16=g16, ncols=ncols: v.tensor_copy(
                bass.AP(str16f_t, g16, [[NSEL // 16, 16], [8, ncols]]), pb[2].ap()[0:16, 0:ncols]))
        out_v = last
        for k0 in range(0, W8, 512):
            kw = min(512, W8 - k0)
            p.wait("tensor", "vE", out_v)
            tv = p.mm(pb[3].ap()[:, 0:kw], rep16_sb.ap()[:, :],
                      bass.AP(str16f_t, k0, [[NSEL // 16, 16], [1, kw]]),
                      start=True, stop=True, tick=True)
            p.wait("vector", "tE", tv)
            out_v = p.tick("vector", "vE", lambda v, k0=k0, kw=kw, d=dst_i16_ap: v.tensor_copy(
                d[:, k0:k0 + kw], pb[3].ap()[:, 0:kw]))
        return out_v

    STP_TB = stripe(tbf, NTILE * K, tidx_str, P3VEC)
    STP_ZC = stripe(zcf, NTILE * K, zcomb_str, P3VEC)

    p.wait("gpsimd", "vE", STP_TB)
    p.wait("gpsimd", "qG", PREZ_Q)
    NCH = NSEL // 1536
    for k in range(NCH):
        p.inc("qG", 16)
        def c2(g, k=k):
            g.dma_scatter_add(
                out_ap=tabTW.ap()[:, 0:2],
                in_ap=bass.AP(tabval_t, k * 24, [[NSEL * 2 // 128, 128], [2, 12], [1, 2]]),
                idxs_ap=tidx_str[:, k * 96:(k + 1) * 96],
                num_idxs=1536, num_idxs_reg=p.swcnt(g, 1536), elem_size=2, elem_step=64,
            ).then_inc(p.sems["qG"], 16)
        p.on("gpsimd", c2)
    SCAT_Q = p.cnt["qG"]

    p.wait("gpsimd", "qG", SCAT_Q)
    tab2 = nc.dram_tensor("tab2", [TAB_ROWS, 64], F32, addr_space="Shared")
    AR3 = p.inc("cc", 1)
    p.run("gpsimd", lambda g: g.collective_compute(
        "AllReduce", AOP.add, replica_groups=RG,
        ins=[tabTW.ap().opt()], outs=[tab2.ap().opt()]).then_inc(p.sems["cc"], 1))
    p.wait("gpsimd", "cc", AR3)
    for k in range(3):
        p.inc("qG", 16)
        def c3(g, k=k):
            g.dma_gather(
                out_ap=bass.AP(myTW.tensor, k * 512, [[1536, 128], [64, 8], [1, 64]]),
                in_ap=tab2.ap(),
                idxs_ap=myslots_sb.ap()[:, k * 64:(k + 1) * 64],
                num_idxs=1024, num_idxs_reg=p.swcnt(g, 1024), elem_size=64,
            ).then_inc(p.sems["qG"], 16)
        p.on("gpsimd", c3)
    MYTW_Q = p.cnt["qG"]
    p.wait("vector", "qG", MYTW_Q)
    vv = p.tick("vector", "vE", lambda v: v.tensor_copy(
        mytf, bass.AP(myTW.tensor, 0, [[1536, 128], [64, 24]])))
    STP_MT = stripe(mytf, 24, xg_idx, vv)

    xsb_t = at("xsb", 65536, [128, 8 * SPC], BF16)
    p.wait("gpsimd", "vE", STP_MT)
    p.wait("gpsimd", "vE", P3VEC)
    p.wait("gpsimd", "qG", SCAT_Q)
    XSQ = p.gatherT(xsb_t, 8 * SPC, xn_all.ap(), xg_idx, SPC, elem=D, chunk=384)

    ew1_sb = [at(f"ew1_{i}", 114688 + i * 8192, [128, 8 * 512], BF16).ap()[:, :] for i in range(2)]
    ew2_sb = [at(f"ew2_{i}", 131072 + i * 8192, [128, 4 * 1024], BF16).ap()[:, :] for i in range(2)]
    hmid = at("hmid", 147456, [128, 4 * 384], BF16).ap()[:, :]
    zstg = [at(f"zstg{i}", 150592 + i * 2048, [128, 1024], BF16).ap()[:, :] for i in range(2)]

    def load_ew(e):
        buf = e % 2
        for dc in range(8):
            p.dma("gpsimd", "qG", ew1_sb[buf][:, dc * 512:(dc + 1) * 512],
                  ew1.ap()[e, dc * 128:(dc + 1) * 128, :])
        for fc in range(4):
            p.dma("gpsimd", "qG", ew2_sb[buf][:, fc * 1024:(fc + 1) * 1024],
                  ew2.ap()[e, fc * 128:(fc + 1) * 128, :])
        return p.cnt["qG"]

    ew_q = {0: load_ew(0)}
    ew_q[1] = load_ew(1)
    p.wait("tensor", "qG", XSQ)
    zq = {}
    ew_done = {}
    for e in range(EPC):
        buf = e % 2
        p.wait("tensor", "qG", ew_q[e])
        for fc in range(4):
            bank = pb[fc].ap()[:, 0:384]
            if e:
                p.wait("tensor", "sE", ew_done[(e - 1, fc)])
            for dc in range(8):
                tv = p.mm(bank, ew1_sb[buf][:, dc * 512 + fc * 128: dc * 512 + (fc + 1) * 128],
                          bass.AP(xsb_t, e * 3072 + dc * 384, [[8 * SPC, 128], [1, 384]]),
                          start=(dc == 0), stop=(dc == 7), tick=(dc == 7))
            p.wait("scalar", "tE", tv)
            sv = p.tick("scalar", "sE", lambda s, o=hmid[:, fc * 384:(fc + 1) * 384], i=bank:
                        s.activation(out=o, in_=i, func=AF.Gelu_apprx_tanh, bias=0.0, scale=1.0))
            ew_done[(e, fc)] = sv
        for sub in range(3):
            bank2 = [pb[4 + 2 * (sub % 2)], pb[5 + 2 * (sub % 2)]]
            p.wait("tensor", "sE", ew_done[(e, 3)])
            if (e, sub - 2) in zq:
                p.wait("tensor", "qS", zq[(e, sub - 2)])
            elif e and sub < 2:
                p.wait("tensor", "qS", zq.get((e - 1, sub + 1), 0))
            for hf in range(2):
                for fc in range(4):
                    tv = p.mm(bank2[hf].ap()[:, :],
                              hmid[:, fc * 384 + sub * 128: fc * 384 + sub * 128 + 128],
                              ew2_sb[buf][:, fc * 1024 + hf * 512: fc * 1024 + (hf + 1) * 512],
                              start=(fc == 0), stop=(fc == 3), tick=(fc == 3))
                p.wait("scalar", "tE", tv)
                wcol = bass.AP(myTW.tensor, 64 * (e * 3 + sub) + 1, [[1536, 128], [1, 1]])
                sv = p.tick("scalar", "sE", lambda s, o=zstg[sub % 2][:, hf * 512:(hf + 1) * 512],
                            i=bank2[hf].ap()[:, :], w=wcol:
                            s.activation(out=o, in_=i, func=AF.Identity, bias=0.0, scale=w))
            p.wait("sync", "sE", sv)
            zq[(e, sub)] = p.dma("sync", "qS",
                                 z_shard.ap()[e * 384 + sub * 128: e * 384 + (sub + 1) * 128, :],
                                 zstg[sub % 2][:, :])
        if e + 2 < EPC:
            p.wait("gpsimd", "tE", p.cnt["tE"])
            ew_q[e + 2] = load_ew(e + 2)
    PEXP_Q = p.cnt["qS"]
    p.wait("gpsimd", "qS", PEXP_Q)
    p.wait("gpsimd", "qG", PREZ_Q)
    AG4 = p.inc("cc", 1)
    p.run("gpsimd", lambda g: g.collective_compute(
        "AllGather", AOP.bypass, replica_groups=RG,
        ins=[z_shard.ap().opt()], outs=[z_all.ap().opt()]).then_inc(p.sems["cc"], 1))

    if "ztab" in dbg:
        tapz = dbgout("ztab", [ZROWS, D], BF16)
        p.wait("scalar", "qS", PEXP_Q)
        p.dma("scalar", "qA", tapz.ap()[:, :], z_shard.ap()[:, :])
        tapt = dbgout("tab", [TAB_ROWS, 2], F32)
        p.wait("scalar", "cc", AR3)
        p.dma("scalar", "qA", tapt.ap()[:, :], tab2.ap()[:, 0:2])

    sw1_sb = at("sw1_sb", 114688, [128, 8 * SF], BF16).ap()[:, :]
    sw2_sb = at("sw2_sb", 147456, [128, 16 * 1024], BF16).ap()[:, :]
    mid_sb = at("mid_sb", 65536, [128, 16 * 256], BF16).ap()[:, :]
    zrows = [at(f"zrows{i}", 73728 + i * 12288, [128, 6 * 1024], BF16) for i in range(2)]
    srows4_t = at("srows4", 98304, [128, 4 * 1024], BF16)
    xrow2 = at("xrow2", 106496, [128, 1024], F32).ap()[:, :]
    orow = at("orow", 110592, [128, 1024], F32).ap()[:, :]

    p.wait("gpsimd", "tE", p.cnt["tE"])
    for dc in range(8):
        p.dma("gpsimd", "qG", sw1_sb[:, dc * SF:(dc + 1) * SF], sw1T.ap()[dc * 128:(dc + 1) * 128, :])
    for hc in range(16):
        p.dma("gpsimd", "qG", sw2_sb[:, hc * 1024:(hc + 1) * 1024], sw2T.ap()[hc * 128:(hc + 1) * 128, :])
    SWQ = p.cnt["qG"]

    CC_ALL = p.cnt["cc"]
    p.wait("gpsimd", "cc", CC_ALL)
    p.wait("gpsimd", "vE", STP_ZC)
    p.wait("gpsimd", "qG", SWQ)

    ZG = {}
    def em_zg(tt):
        if tt >= 2:
            p.wait("gpsimd", "qG", ZG[tt - 1])
        ZG[tt] = p.inc("qG", 16)
        def call(g, tt=tt):
            g.dma_gather(
                out_ap=bass.AP(zrows[tt % 2], 0, [[6 * 1024, 128], [1024, 6], [1, 1024]]),
                in_ap=z_all.ap(),
                idxs_ap=zcomb_str[:, tt * 48:(tt + 1) * 48],
                num_idxs=768, num_idxs_reg=p.swcnt(g, 768), elem_size=D,
            ).then_inc(p.sems["qG"], 16)
        p.on("gpsimd", call)

    SGq = {}
    def em_sg(gg):
        if gg >= 1:
            p.wait("gpsimd", "qG", SGq[gg - 1])
        SGq[gg] = p.inc("qG", 16)
        def call(g, gg=gg):
            g.dma_gather(
                out_ap=bass.AP(srows4_t, 0, [[4 * 1024, 128], [1024, 4], [1, 1024]]),
                in_ap=s_all.ap(),
                idxs_ap=moegi_sb.ap()[:, gg * 32:(gg + 1) * 32],
                num_idxs=512, num_idxs_reg=p.swcnt(g, 512), elem_size=D,
            ).then_inc(p.sems["qG"], 16)
        p.on("gpsimd", call)

    em_zg(0)
    em_zg(1)
    em_sg(0)
    XR = {0: p.dma("sync", "qS", xrow2, x_moe.ap()[0:128, :])}

    p.wait("tensor", "qG", SWQ)
    gq = {}
    ztick = {}
    vops = {}
    oq = {}
    for tl in range(16):
        for hc in range(16):
            bank = pb[hc % 2].ap()[:, 0:256]
            for dc in range(8):
                tv = p.mm(bank, sw1_sb[:, dc * SF + hc * 128: dc * SF + (hc + 1) * 128],
                          bass.AP(xnm, tl * 2048 + dc * 256, [[8 * TPC, 128], [1, 256]]),
                          start=(dc == 0), stop=(dc == 7), tick=(dc == 7))
            p.wait("scalar", "tE", tv)
            sv = p.tick("scalar", "sE", lambda s, o=mid_sb[:, hc * 256:(hc + 1) * 256], i=bank:
                        s.activation(out=o, in_=i, func=AF.Gelu_apprx_tanh, bias=0.0, scale=1.0))
            gq[("mid", tl, hc)] = sv
            p.wait("tensor", "sE", sv)
        p.wait("tensor", "sE", gq[("mid", tl, 15)])
        for sub in range(2):
            tt = tl * 2 + sub
            b2 = [pb[4 + 2 * sub], pb[5 + 2 * sub]]
            if tt >= 2:
                p.wait("tensor", "sE", gq[("sil", tt - 2)])
            for hf in range(2):
                for hc in range(16):
                    p.mm(b2[hf].ap()[:, :],
                         mid_sb[:, hc * 256 + sub * 128: hc * 256 + sub * 128 + 128],
                         sw2_sb[:, hc * 1024 + hf * 512: hc * 1024 + (hf + 1) * 512],
                         start=(hc == 0), stop=False)
            p.wait("tensor", "qG", ZG[tt])
            for hf in range(2):
                for j in range(6):
                    tv = p.mm(b2[hf].ap()[:, :], ident_sb.ap()[:, :],
                              bass.AP(zrows[tt % 2], j * 1024 + hf * 512, [[6 * 1024, 128], [1, 512]]),
                              start=False, stop=(j == 5), tick=(j == 5))
            ztick[tt] = tv
            if tt + 2 < NTILE:
                p.wait("gpsimd", "tE", ztick[tt])
                em_zg(tt + 2)
            if "gdbg" in dbg:
                tapg = dbgout("gdbg", [TPC, D], F32) if tt == 0 else dbg_t["gdbg"]
                p.wait("vector", "tE", ztick[tt])
                if tt >= 1:
                    p.wait("vector", "qA", p.cnt["qA"])
                    p.wait("vector", "qS", oq[tt - 1])
                for hf_ in range(2):
                    gv = p.tick("vector", "vE", lambda v, o=orow[:, hf_ * 512:(hf_ + 1) * 512],
                                i=b2[hf_].ap()[:, :]: v.tensor_copy(o, i))
                p.wait("scalar", "vE", gv)
                p.dma("scalar", "qA", tapg.ap()[tt * 128:(tt + 1) * 128, :], orow)
            p.wait("scalar", "tE", ztick[tt])
            if tt >= 1:
                p.wait("scalar", "vE", vops[tt - 1])
            for hf in range(2):
                sv = p.tick("scalar", "sE", lambda s, i=b2[hf].ap()[:, :], o=pb[2 + hf].ap()[:, :]:
                            s.activation(out=o, in_=i, func=AF.Silu, bias=0.0, scale=1.0))
            gq[("sil", tt)] = sv
            gg, r = tt // 4, tt % 4
            if "gdbg" in dbg:
                p.wait("vector", "qA", p.cnt["qA"])
            p.wait("vector", "sE", sv)
            p.wait("vector", "qG", SGq[gg])
            p.wait("vector", "qS", XR[tt])
            if tt >= 1:
                p.wait("vector", "qS", oq[tt - 1])
            for hf in range(2):
                vv = p.tick("vector", "vE", lambda v, hf=hf, r=r:
                            v.tensor_mul(orow[:, hf * 512:(hf + 1) * 512],
                                         pb[2 + hf].ap()[:, :],
                                         bass.AP(srows4_t, r * 1024 + hf * 512, [[4 * 1024, 128], [1, 512]])))
            vops[tt] = vv
            p.wait("vector", "vE", vv)
            vv = p.tick("vector", "vE", lambda v: v.tensor_add(orow, orow, xrow2))
            p.wait("sync", "vE", vv)
            oq[tt] = p.dma("sync", "qS", out_t.ap()[tt * 128:(tt + 1) * 128, :], orow)
            if tt + 1 < NTILE:
                XR[tt + 1] = p.dma("sync", "qS", xrow2, x_moe.ap()[(tt + 1) * 128:(tt + 2) * 128, :])
            if r == 3 and gg + 1 < 8:
                p.wait("gpsimd", "vE", vops[tt])
                em_sg(gg + 1)
    OUT_Q = p.cnt["qS"]

    if "zc" in dbg:
        tapzc = dbgout("zc", [128, NSEL // 16], I16)
        p.wait("scalar", "qS", OUT_Q)
        p.dma("scalar", "qA", tapzc.ap()[:, :], zcomb_str)
    if "zall" in dbg:
        tapza = dbgout("zall", [NCORES * ZROWS, D], BF16)
        p.wait("scalar", "qS", OUT_Q)
        p.dma("scalar", "qA", tapza.ap()[:, :], z_all.ap()[:, :])
    if "sall" in dbg:
        tapsa = dbgout("sall", [T, D], BF16)
        p.wait("scalar", "qS", OUT_Q)
        p.dma("scalar", "qA", tapsa.ap()[:, :], s_all.ap()[:, :])
    if dbg_t:
        for e_ in ("scalar",):
            p.wait(e_, "qA", p.cnt["qA"])
    p.wait("sync", "qS", OUT_Q)
    p.wait("sync", "qG", p.cnt["qG"])

    with nc.Block() as block:
        p.emit(block)
    _finalize_passes(nc)
    return nc


_NC_CACHE = {}


def kernel(**inputs):
    in_maps = _host_prepare(inputs)
    key = ()
    if key not in _NC_CACHE:
        _NC_CACHE[key] = build()
    nc = _NC_CACHE[key]
    res = run_bass_kernel_spmd(nc, in_maps, core_ids=list(range(NCORES)))
    return _assemble(res.results)
